# version 16
# baseline (speedup 1.0000x reference)
from contextlib import ExitStack

import numpy as np
import ml_dtypes

import concourse.bass as bass
import concourse.mybir as mybir
from concourse.bass_utils import run_bass_kernel_spmd

F32 = mybir.dt.float32
BF16 = mybir.dt.bfloat16
AF = mybir.ActivationFunctionType
ALU = mybir.AluOpType
AX = mybir.AxisListType

NCORE = 8
N = 2048
D = 1024
NH = 16
DK = 64
HF = 4096
TPC = 2 * N
C_RND = 12582912.0
EPS = 1e-5
INV127 = 1.0 / 127.0

ENGN = ("sync", "tensor", "scalar", "vector", "gpsimd")

SEMNAMES = [
    "te", "sc", "ve", "ge", "cc",
    "sc0",
    "gx0", "gx1",
    "gw",
    "gs",
    "ssw0", "ssw1",
    "sw0", "sw1",
    "sxr0", "sxr1",
    "ssp0", "ssp1",
    "sdn",
    "sbs",
    "sdl", "sdi",
    "srb", "srb2",
    "sh0", "sh1", "sh2", "sh3",
    "shq0", "shq1",
    "sy0", "sy1",
    "sdbg",
]


class Sched:

    def __init__(self, nc, sems):
        self.nc = nc
        self.sems = sems
        self.cnt = {k: 0 for k in sems}
        self.steps = []

    def op(self, engine, emit, incs=(), waits=()):
        for s, v in waits:
            assert v <= self.cnt[s], f"wait {s}>={v} but only {self.cnt[s]}"
        self.steps.append((engine, emit, tuple(incs), tuple(waits)))
        for s, k in incs:
            self.cnt[s] += k

    def dma(self, engine, out, in_, stream, waits=()):
        def em(nc_, eng, out=out, in_=in_):
            return eng.dma_start(out=out, in_=in_)
        self.op(engine, em, incs=[(stream, 16)], waits=waits)
        return self.cnt[stream]

    def mark(self, sem):
        return self.cnt[sem]

    def marks(self):
        return dict(self.cnt)

    def play(self, block, entry=None):
        nc, steps, sems = self.nc, self.steps, self.sems
        for eng_name in ENGN:
            def body(eng, _en=eng_name):
                waited = {}
                if entry:
                    for s, v in entry.items():
                        if v > 0:
                            eng.wait_ge(sems[s], v)
                            waited[s] = v
                for engine, emit, incs, waits in steps:
                    if engine != _en:
                        continue
                    for s, v in waits:
                        if waited.get(s, -1) < v:
                            eng.wait_ge(sems[s], v)
                            waited[s] = v
                    instr = emit(nc, eng)
                    for s, k in incs:
                        instr.then_inc(sems[s], k)
            getattr(block, eng_name)(body)
        self.steps = []


def _rope_perm():
    idx = np.zeros(D, dtype=np.int64)
    for h in range(NH):
        b = h * DK
        idx[b:b + 32] = b + np.arange(0, DK, 2)
        idx[b + 32:b + 64] = b + np.arange(1, DK, 2)
    return idx


def _rope_tables():
    powers = np.arange(0, DK, 2, dtype=np.float64) / DK
    freqs = 1.0 / (10000.0 ** powers)
    t = np.arange(N, dtype=np.float64)
    ang = np.outer(freqs, t)
    cos = np.cos(ang)
    sin = np.sin(ang)
    c2 = np.concatenate([cos, cos, cos, cos], 0).astype(np.float32)
    s2 = np.concatenate([-sin, sin, -sin, sin], 0).astype(np.float32)
    return (c2.astype(ml_dtypes.bfloat16), s2.astype(ml_dtypes.bfloat16))


def _host_prep(inputs):
    perm = _rope_perm()
    wq = np.asarray(inputs["L0_attn_wq"])
    wk = np.asarray(inputs["L0_attn_wk"])
    wv = np.asarray(inputs["L0_attn_wv"])
    wo = np.asarray(inputs["L0_attn_wo"])
    w1 = np.asarray(inputs["L0_ffn_w1"])
    w2 = np.asarray(inputs["L0_ffn_w2"])
    b1 = np.asarray(inputs["L0_ffn_b1"])
    b2 = np.asarray(inputs["L0_ffn_b2"])
    c2, s2 = _rope_tables()
    common = {
        "wq_t": np.ascontiguousarray(wq[perm].T),
        "wk_t": np.ascontiguousarray(wk[perm].T),
        "wv_t": np.ascontiguousarray(wv.T),
        "wo_t": np.ascontiguousarray(wo.T),
        "w1_t": np.ascontiguousarray(w1.T),
        "w2_t": np.ascontiguousarray(w2.T),
        "b1_col": np.ascontiguousarray(b1.reshape(32, 128).T),
        "b2_row": np.ascontiguousarray(b2.reshape(1, D)),
        "tab_c2": c2,
        "tab_s2": s2,
    }
    x = np.asarray(inputs["L0_attn_x"])
    return [dict(common, x=np.ascontiguousarray(
        x[2 * c:2 * c + 2].reshape(TPC, D))) for c in range(NCORE)]


def build_program(debug=False, race=True):
    nc = bass.Bass("TRN2", target_bir_lowering=False,
                   detect_race_conditions=race)

    x_d = nc.dram_tensor("x", [TPC, D], F32, kind="ExternalInput").ap()
    wq_d = nc.dram_tensor("wq_t", [D, D], F32, kind="ExternalInput").ap()
    wk_d = nc.dram_tensor("wk_t", [D, D], F32, kind="ExternalInput").ap()
    wv_d = nc.dram_tensor("wv_t", [D, D], F32, kind="ExternalInput").ap()
    wo_d = nc.dram_tensor("wo_t", [D, D], F32, kind="ExternalInput").ap()
    w1_d = nc.dram_tensor("w1_t", [D, HF], F32, kind="ExternalInput").ap()
    w2_d = nc.dram_tensor("w2_t", [HF, D], F32, kind="ExternalInput").ap()
    b1_d = nc.dram_tensor("b1_col", [128, 32], F32, kind="ExternalInput").ap()
    b2_d = nc.dram_tensor("b2_row", [1, D], F32, kind="ExternalInput").ap()
    c2_d = nc.dram_tensor("tab_c2", [128, N], BF16, kind="ExternalInput").ap()
    s2_d = nc.dram_tensor("tab_s2", [128, N], BF16, kind="ExternalInput").ap()
    y_d = nc.dram_tensor("y", [TPC, D], F32, kind="ExternalOutput").ap()

    scratch_kind = "ExternalOutput" if debug else "Internal"
    x1_dram = nc.dram_tensor("x1_scr", [TPC, D], F32, kind=scratch_kind).ap()
    h_dram = nc.dram_tensor("h_scr", [32, 128, 4096], BF16,
                            kind=scratch_kind).ap()
    den_dram = nc.dram_tensor("den_scr", [2, 8, 4096], F32).ap()
    deni_dram = nc.dram_tensor("deni_scr", [2, 8, 4096], BF16).ap()
    sc_dram = nc.dram_tensor("sc_scr", [64], F32).ap()
    c0_dram = nc.dram_tensor("c0_scr", [64], F32).ap()
    relay_dram = nc.dram_tensor("relay_scr", [1, 128], F32).ap()
    cc1i = nc.dram_tensor("cc1i", [1, 128], F32).ap()
    cc1o = nc.dram_tensor("cc1o", [1, 128], F32).ap()
    cc2i = nc.dram_tensor("cc2i", [1, 128], F32).ap()
    cc2o = nc.dram_tensor("cc2o", [1, 128], F32).ap()

    dbgmap = {}
    if debug:
        for nm, shp, dt in [("qt", [2, 128, 8 * N], BF16),
                            ("kt", [2, 128, 8 * N], BF16),
                            ("v2", [2, 128, 16 * 16 * 65], BF16),
                            ("avt", [2, 128, 8 * N], BF16),
                            ("sc16", [1, 16], F32),
                            ("xmaxd", [128, 1], F32),
                            ("relayd", [1, 128], F32),
                            ("srowd", [1, 32], F32),
                            ("nmxd", [1, 32], F32),
                            ("qt0", [128, 2048], BF16),
                            ("res0", [2, 128, 1024], F32),
                            ("lnd", [2, 128, 8], F32),
                            ("sqd", [2, 128, 1024], F32),
                            ("lnv0", [2, 128, 1], F32)]:
            dbgmap[nm] = nc.dram_tensor("dbg_" + nm, shp, dt,
                                        kind="ExternalOutput").ap()

    es = ExitStack()
    sb = lambda nm, shp, dt: es.enter_context(nc.sbuf_tensor(nm, shp, dt))
    sem = lambda nm: es.enter_context(nc.semaphore(nm))

    C2 = sb("C2", [128, N], BF16)
    S2t = sb("S2t", [128, N], BF16)
    ident = sb("ident", [128, 128], BF16)
    trimask = sb("trimask", [128, 128], BF16)
    maskA = sb("maskA", [128, 128], BF16)
    onescol = sb("onescol", [128, 1], BF16)
    ones2 = sb("ones2", [128, 2], BF16)
    c0t = sb("c0t", [1, 64], F32)
    b1c = sb("b1c", [128, 32], F32)
    b1s = sb("b1s", [128, 32], F32)
    scal = sb("scal", [128, 16], F32)
    srow = sb("srow", [128, 32], F32)
    xmax = sb("xmax", [128, 1], F32)
    hmax = sb("hmax", [128, 1], F32)
    wmax = sb("wmax", [128, 1], F32)
    w1max = sb("w1max", [128, 1], F32)
    w2max = sb("w2max", [128, 1], F32)
    relay = sb("relay", [1, 128], F32)
    lnm = sb("lnm", [128, 1], F32)
    lnv = sb("lnv", [128, 1], F32)
    lnr = sb("lnr", [128, 1], F32)
    lns = sb("lns", [128, 1], F32)
    lnq = sb("lnq", [128, 1], F32)
    rmx = sb("rmx", [128, 1], F32)

    CNC0, CNC1, CRX, CS1, CRH, CS2, CRW1, CRW2 = range(8)

    sems = {k: sem(k) for k in SEMNAMES}
    S = Sched(nc, sems)

    def col(i):
        return scal[:, i:i + 1]

    with nc.Block() as blk:
        S.dma("sync", C2[:, :], c2_d[:, :], "sc0")
        S.dma("sync", S2t[:, :], s2_d[:, :], "sc0")
        S.dma("sync", b1c[:, :], b1_d[:, :], "sc0")

        def em(nc_, eng):
            eng.memset(ident[:, :], 1.0)
            eng.memset(maskA[:, :], 0.0)
            return eng.memset(trimask[:, :], 1.0)
        S.op("vector", em, incs=[("ve", 1)])
        msk_ve = S.mark("ve")

        def em(nc_, eng):
            eng.affine_select(out=ident[:, :], in_=ident[:, :],
                              compare_op=ALU.is_equal, fill=0.0, base=0,
                              pattern=[[-1, 128]], channel_multiplier=1)
            eng.affine_select(out=trimask[:, :], in_=trimask[:, :],
                              compare_op=ALU.is_ge, fill=0.0, base=0,
                              pattern=[[1, 128]], channel_multiplier=-1)
            return eng.affine_select(out=maskA[:, :], in_=maskA[:, :],
                                     compare_op=ALU.is_ge, fill=-1e30,
                                     base=0, pattern=[[-1, 128]],
                                     channel_multiplier=1)
        S.op("gpsimd", em, incs=[("ge", 1)], waits=[("ve", msk_ve)])

        def em(nc_, eng):
            eng.memset(onescol[:, :], 1.0)
            eng.memset(ones2[:, :], 0.0)
            eng.memset(ones2[0:64, 0:1], 1.0)
            eng.memset(ones2[64:128, 1:2], 1.0)
            eng.memset(xmax[:, :], 0.0)
            eng.memset(hmax[:, :], 0.0)
            eng.memset(w1max[:, :], 0.0)
            eng.memset(w2max[:, :], 0.0)
            eng.memset(srow[:, :], 0.0)
            return eng.memset(scal[:, :], 0.0)
        S.op("vector", em, incs=[("ve", 1)])
        S.dma("gpsimd", sc_dram[0:32], srow[0:1, 0:32], "gs",
              waits=[("ve", S.mark("ve"))])
        S.play(blk)

    with ExitStack() as aes:
        asb = lambda nm, shp, dt: aes.enter_context(nc.sbuf_tensor(nm, shp, dt))
        QT = asb("QT", [128, 8 * N], BF16)
        KT = asb("KT", [128, 8 * N], BF16)
        V2 = asb("V2", [128, 16 * 16 * 65], BF16)
        slotA = asb("slotA", [128, 8 * N], BF16)
        Wld = asb("Wld", [128, 8 * 1024], BF16)
        xstage = asb("xstage", [128, 2 * 1024], BF16)
        rsc1 = asb("rsc1", [128, N], BF16)
        rsc2 = asb("rsc2", [128, N], BF16)
        ATst = asb("ATst", [128, 2 * 1024], BF16)
        Bstage = asb("Bstage", [64, N], BF16)
        rb = asb("rb", [128, 2 * N], BF16)
        deni_b = asb("deni_b", [128, 32], BF16)
        nmx = asb("nmx", [2, 32], F32)
        xf32 = asb("xf32", [128, 2 * 1024], F32)
        x1t = asb("x1t", [128, 2 * 1024], F32)
        ATq = asb("ATq", [128, 16 * 128], BF16)

        QTv = QT.ap().rearrange("p (c n) -> p c n", c=8)
        KTv = KT.ap().rearrange("p (c n) -> p c n", c=8)
        V2v = V2.ap().rearrange("p (k s e) -> p k s e", k=16, s=16)
        slotAv = slotA.ap().rearrange("p (c n) -> p c n", c=8)
        Wldv = Wld.ap().rearrange("p (c n) -> p c n", c=8)
        xstv = xstage.ap().rearrange("p (r n) -> p r n", r=2)
        ATv = ATst.ap().rearrange("p (r n) -> p r n", r=2)
        xf32v = xf32.ap().rearrange("p (r n) -> p r n", r=2)
        x1tv = x1t.ap().rearrange("p (r n) -> p r n", r=2)
        wstag = [rsc1.ap().bitcast(F32), rsc2.ap().bitcast(F32)]

        vbar = S.marks()
        with nc.Block() as blk:
            def em(nc_, eng):
                return eng.memset(V2v[:, :, :, 64], 1.0)
            S.op("vector", em, incs=[("ve", 1)])
            S.play(blk, entry=vbar)

        for s_idx in range(2):
            tok0 = s_idx * N
            bar = S.marks()

            with ExitStack() as pes:
                pps = lambda nm, shp, dt: pes.enter_context(
                    nc.psum_tensor(nm + f"_{s_idx}", shp, dt))
                PT = [pps(f"PT{i}", [128, 1024], BF16) for i in range(2)]
                PS = [pps(f"PS{i}", [128, 512], F32) for i in range(2)]
                PN = pps("PN", [2, 2048], F32)
                PTv = [p.ap().rearrange("p (c n) -> p c n", c=8) for p in PT]
                PSh = [p.ap().rearrange("p (h e) -> p h e", h=8) for p in PS]

                with nc.Block() as blk:
                    pt_free = [None, None]
                    xst_free = [None, None]
                    for tt in range(16):
                        xp = tt % 2
                        gx = "gx%d" % xp
                        w = [] if xst_free[xp] is None else [("te", xst_free[xp])]
                        ldm = S.dma("gpsimd", xstv[:, xp, :],
                                    x_d[tok0 + 128 * tt: tok0 + 128 * (tt + 1), :],
                                    gx, w)
                        pp = tt % 2
                        wt = [(gx, ldm)]
                        if pt_free[pp] is not None:
                            wt.append(("sc", pt_free[pp]))
                        def em(nc_, eng, xp=xp, pp=pp):
                            r = None
                            for j in range(8):
                                r = eng.transpose(
                                    PT[pp][:, 128 * j:128 * (j + 1)],
                                    xstv[:, xp, 128 * j:128 * (j + 1)],
                                    ident[:, :])
                            return r
                        S.op("tensor", em, incs=[("te", 1)], waits=wt)
                        xst_free[xp] = S.mark("te")
                        def em(nc_, eng, tt=tt, pp=pp):
                            return eng.activation(
                                slotAv[:, 0:8, 128 * tt:128 * (tt + 1)],
                                PTv[pp], AF.Copy)
                        S.op("scalar", em, incs=[("sc", 1)],
                             waits=[("te", S.mark("te"))])
                        pt_free[pp] = S.mark("sc")
                    XT_sc = S.mark("sc")

                    ps_free = [None] * 2
                    psn = [0]
                    rsc_guard = {"t1": [None, None], "t2": [None, None],
                                 "nm": [None, None]}
                    swn = [0]

                    QFs = [rsc1.ap().bitcast(F32), x1tv[:, 0, :]]
                    QsFs = [rsc2.ap().bitcast(F32), x1tv[:, 1, :]]

                    def proj_fm(wd, store_v, qk_idx, ldw_waits):
                        ldw = S.dma("gpsimd", Wldv[:, :, :],
                                    wd.rearrange("(c p) n -> p c n", p=128),
                                    "gw", ldw_waits)
                        add_marks = []
                        for oc in range(8):
                            for h2 in range(2):
                                sl2 = (2 * oc + h2) % 2
                                QF = QFs[sl2]
                                QsF = QsFs[sl2]
                                for q4s in range(2):
                                    q4 = 2 * h2 + q4s
                                    sl = psn[0] % 2
                                    psn[0] += 1
                                    wt = [("gw", ldw), ("sc", XT_sc)]
                                    if ps_free[sl] is not None:
                                        wt.append(("sc", ps_free[sl]))
                                    def em(nc_, eng, oc=oc, q4=q4, sl=sl):
                                        r = None
                                        for dc in range(8):
                                            r = eng.matmul(
                                                PS[sl][:, :],
                                                Wldv[:, dc,
                                                     128 * oc:128 * (oc + 1)],
                                                slotAv[:, dc,
                                                       512 * q4:512 * (q4 + 1)],
                                                start=(dc == 0),
                                                stop=(dc == 7))
                                        return r
                                    S.op("tensor", em, incs=[("te", 1)],
                                         waits=wt)
                                    tm = S.mark("te")
                                    evw = [("te", tm)]
                                    if rsc_guard["t1"][sl2] is not None:
                                        evw.append(
                                            ("ve", rsc_guard["t1"][sl2]))
                                    def em(nc_, eng, q4s=q4s, sl=sl, QF=QF):
                                        return eng.activation(
                                            QF[:, 512 * q4s:512 * (q4s + 1)],
                                            PS[sl][:, :], AF.Copy)
                                    S.op("scalar", em, incs=[("sc", 1)],
                                         waits=evw)
                                    ps_free[sl] = S.mark("sc")
                                evm = S.mark("sc")
                                ssw = "ssw%d" % sl2
                                swn[0] += 1
                                sw_wt = [("sc", evm)]
                                if rsc_guard["t2"][sl2] is not None:
                                    sw_wt.append(("ve", rsc_guard["t2"][sl2]))
                                swm = None
                                for (od, os_) in ((0, 32), (32, 0), (64, 96),
                                                  (96, 64)):
                                    swm = S.dma("sync", QsF[od:od + 32, :],
                                                QF[os_:os_ + 32, :],
                                                ssw, sw_wt)
                                    sw_wt = []
                                tok0h = 1024 * h2
                                def em(nc_, eng, tok0h=tok0h, QF=QF):
                                    return eng.tensor_mul(
                                        QF[:, :], QF[:, :],
                                        C2[:, tok0h:tok0h + 1024])
                                S.op("vector", em, incs=[("ve", 1)],
                                     waits=[(ssw, swm)])
                                def em(nc_, eng, tok0h=tok0h, QsF=QsF):
                                    return eng.tensor_mul(
                                        QsF[:, :], QsF[:, :],
                                        S2t[:, tok0h:tok0h + 1024])
                                S.op("vector", em, incs=[("ve", 1)])
                                rsc_guard["t2"][sl2] = S.mark("ve")
                                def em(nc_, eng, oc=oc, h2=h2,
                                       store_v=store_v, QF=QF, QsF=QsF):
                                    return eng.tensor_add(
                                        store_v[:, oc,
                                                1024 * h2:1024 * (h2 + 1)],
                                        QF[:, :], QsF[:, :])
                                S.op("vector", em, incs=[("ve", 1)],
                                     waits=[("ve", rsc_guard["t2"][sl2])])
                                rsc_guard["t1"][sl2] = S.mark("ve")
                            add_marks.append(S.mark("ve"))
                        proj_te = S.mark("te")
                        sq_bufs = [ATst.ap(), ATq.ap()]
                        for oc in range(8):
                            sqb = sq_bufs[oc % 2]
                            sq_wt = [("ve", add_marks[oc])]
                            if rsc_guard["nm"][oc % 2] is not None:
                                sq_wt.append(("te", rsc_guard["nm"][oc % 2]))
                            def em(nc_, eng, oc=oc, store_v=store_v,
                                   sqb=sqb):
                                return eng.tensor_mul(sqb[:, :],
                                                      store_v[:, oc, :],
                                                      store_v[:, oc, :])
                            S.op("vector", em, incs=[("ve", 1)], waits=sq_wt)
                            sqm = ("ve", S.mark("ve"))
                            def em(nc_, eng, sqb=sqb):
                                r = None
                                for q4 in range(4):
                                    r = eng.matmul(
                                        PN[:, 512 * q4:512 * (q4 + 1)],
                                        ones2[:, :],
                                        sqb[:, 512 * q4:512 * (q4 + 1)],
                                        start=True, stop=True)
                                return r
                            S.op("tensor", em, incs=[("te", 1)],
                                 waits=[sqm])
                            nm_ = S.mark("te")
                            rsc_guard["nm"][oc % 2] = nm_
                            cc_ = 8 * qk_idx + oc
                            def em(nc_, eng, cc_=cc_):
                                return eng.reduce_max(nmx[0:2, cc_:cc_ + 1],
                                                      PN[0:2, :], axis=AX.X)
                            S.op("vector", em, incs=[("ve", 1)],
                                 waits=[("te", nm_)])
                        return proj_te

                    qte = proj_fm(wq_d, QTv, 0, [])
                    kte = proj_fm(wk_d, KTv, 1, [("te", qte)])
                    ROPE_ve = S.mark("ve")

                    ldw = S.dma("gpsimd", Wldv[:, :, :],
                                wv_d.rearrange("(c p) n -> p c n", p=128),
                                "gw", [("te", kte)])
                    for tt in range(16):
                        for hh in range(2):
                            sl = psn[0] % 2
                            psn[0] += 1
                            wt = [("gw", ldw), ("sc", XT_sc)]
                            if ps_free[sl] is not None:
                                wt.append(("sc", ps_free[sl]))
                            def em(nc_, eng, tt=tt, hh=hh, sl=sl):
                                r = None
                                for dc in range(8):
                                    r = eng.matmul(
                                        PS[sl][:, :],
                                        slotAv[:, dc, 128 * tt:128 * (tt + 1)],
                                        Wldv[:, dc, 512 * hh:512 * (hh + 1)],
                                        start=(dc == 0), stop=(dc == 7))
                                return r
                            S.op("tensor", em, incs=[("te", 1)], waits=wt)
                            tm = S.mark("te")
                            def em(nc_, eng, tt=tt, hh=hh, sl=sl):
                                return eng.activation(
                                    V2v[:, tt, 8 * hh:8 * (hh + 1), 0:64],
                                    PSh[sl], AF.Copy)
                            S.op("scalar", em, incs=[("sc", 1)],
                                 waits=[("te", tm)])
                            ps_free[sl] = S.mark("sc")

                    r1 = S.dma("gpsimd", c0_dram[0:32], nmx[0:2, 0:16],
                               "gs", [("ve", ROPE_ve)])
                    r2 = S.dma("gpsimd", c0t[0:1, 0:32],
                               c0_dram[0:32].unsqueeze(0), "gs",
                               [("gs", r1)])
                    def em(nc_, eng):
                        eng.tensor_mul(c0t[0:1, 32:40], c0t[0:1, 0:8],
                                       c0t[0:1, 8:16])
                        return eng.tensor_mul(c0t[0:1, 40:48],
                                              c0t[0:1, 16:24],
                                              c0t[0:1, 24:32])
                    S.op("vector", em, incs=[("ve", 1)], waits=[("gs", r2)])
                    rm = S.mark("ve")
                    def em(nc_, eng):
                        eng.tensor_scalar(c0t[0:1, 52:60], c0t[0:1, 32:40],
                                          1.0, None, ALU.mult)
                        return eng.tensor_scalar(c0t[0:1, 52:60],
                                                 c0t[0:1, 40:48], 1.0, None,
                                                 ALU.max)
                    S.op("vector", em, incs=[("ve", 1)], waits=[("ve", rm)])
                    rm = S.mark("ve")
                    def em(nc_, eng):
                        return eng.reduce_max(c0t[0:1, 48:49],
                                              c0t[0:1, 32:48], axis=AX.X)
                    S.op("vector", em, incs=[("ve", 1)], waits=[("ve", rm)])
                    def em(nc_, eng):
                        return eng.activation(c0t[0:1, 49:50],
                                              c0t[0:1, 48:49], AF.Sqrt)
                    S.op("scalar", em, incs=[("sc", 1)],
                         waits=[("ve", S.mark("ve"))])
                    def em(nc_, eng):
                        return eng.tensor_scalar(c0t[0:1, 50:51],
                                                 c0t[0:1, 49:50],
                                                 -0.125, None, ALU.mult)
                    S.op("vector", em, incs=[("ve", 1)],
                         waits=[("sc", S.mark("sc"))])
                    wm = S.dma("gpsimd", sc_dram[s_idx:s_idx + 1],
                               c0t[0:1, 50:51], "gs",
                               [("ve", S.mark("ve"))])
                    S.dma("gpsimd", col(s_idx),
                          sc_dram[s_idx:s_idx + 1].unsqueeze(0)
                          .partition_broadcast(128),
                          "gs", [("gs", wm)])

                    sp_ps = None
                    sp_pt = None
                    for hp in range(8):
                        wt = [("ve", ROPE_ve)]
                        if sp_ps is not None:
                            wt.append(("sc", sp_ps))
                        else:
                            for f in ps_free:
                                if f is not None:
                                    wt.append(("sc", f))
                        def em(nc_, eng, hp=hp):
                            eng.matmul(PS[0][:, 0:128],
                                       QTv[0:64, hp, 0:128],
                                       KTv[0:64, hp, 0:128],
                                       start=True, stop=True,
                                       tile_position=(0, 0))
                            return eng.matmul(PS[1][:, 0:128],
                                             QTv[64:128, hp, 0:128],
                                             KTv[64:128, hp, 0:128],
                                             start=True, stop=True,
                                             tile_position=(64, 0))
                        S.op("tensor", em, incs=[("te", 1)], waits=wt)
                        spm_te = S.mark("te")
                        def em(nc_, eng):
                            eng.tensor_add(PS[0][:, 0:128], PS[0][:, 0:128],
                                           maskA[:, :])
                            return eng.tensor_add(PS[1][:, 0:128],
                                                  PS[1][:, 0:128],
                                                  maskA[:, :])
                        S.op("vector", em, incs=[("ve", 1)],
                             waits=[("te", spm_te)])
                        mka = S.mark("ve")
                        def em(nc_, eng):
                            eng.reduce_max(lns[:, :], PS[0][:, 0:128],
                                           axis=AX.X, negate=True)
                            return eng.reduce_max(lnv[:, :], PS[1][:, 0:128],
                                                  axis=AX.X, negate=True)
                        S.op("vector", em, incs=[("ve", 1)],
                             waits=[("ve", mka)])
                        rdm = S.mark("ve")
                        def em(nc_, eng):
                            eng.tensor_scalar(lnm[:, :], lns[:, :], 0.125,
                                              None, ALU.mult)
                            return eng.tensor_scalar(lnr[:, :], lnv[:, :],
                                                     0.125, None, ALU.mult)
                        S.op("vector", em, incs=[("ve", 1)],
                             waits=[("ve", rdm)])
                        bsm = S.mark("ve")
                        def em(nc_, eng):
                            eng.activation(rsc1[:, 0:128], PS[0][:, 0:128],
                                           AF.Exp, bias=lnm[:, :],
                                           scale=0.125)
                            return eng.activation(rsc1[:, 128:256],
                                                  PS[1][:, 0:128],
                                                  AF.Exp, bias=lnr[:, :],
                                                  scale=0.125)
                        S.op("scalar", em, incs=[("sc", 1)],
                             waits=[("ve", bsm)])
                        sp_ps = S.mark("sc")
                        wtt = [("sc", sp_ps)]
                        if sp_pt is not None:
                            wtt.append(("sc", sp_pt))
                        else:
                            for f in pt_free:
                                if f is not None:
                                    wtt.append(("sc", f))
                        def em(nc_, eng):
                            eng.transpose(PT[0][:, 0:128], rsc1[:, 0:128],
                                          ident[:, :])
                            return eng.transpose(PT[1][:, 0:128],
                                                 rsc1[:, 128:256],
                                                 ident[:, :])
                        S.op("tensor", em, incs=[("te", 1)], waits=wtt)
                        trm = S.mark("te")
                        def em(nc_, eng, hp=hp):
                            eng.activation(
                                ATq[:, 256 * hp:256 * hp + 128],
                                PT[0][:, 0:128], AF.Copy)
                            return eng.activation(
                                ATq[:, 256 * hp + 128:256 * hp + 256],
                                PT[1][:, 0:128], AF.Copy)
                        S.op("scalar", em, incs=[("sc", 1)],
                             waits=[("te", trm)])
                        sp_pt = S.mark("sc")
                    S.play(blk, entry=bar)
            C0_gs = S.mark("gs")

            if debug:
                dbar = S.marks()
                with nc.Block() as blk:
                    S.dma("sync", dbgmap["qt"][s_idx], QT[:, :], "sdbg")
                    S.dma("sync", dbgmap["kt"][s_idx], KT[:, :], "sdbg")
                    S.dma("sync", dbgmap["v2"][s_idx], V2[:, :], "sdbg")
                    if s_idx == 0:
                        S.dma("sync", dbgmap["nmxd"][0:1, :],
                              nmx[0:2, 0:16], "sdbg")
                    S.play(blk, entry=dbar)

            bar = S.marks()
            with ExitStack() as pes:
                pps = lambda nm, shp, dt: pes.enter_context(
                    nc.psum_tensor(nm + f"_{s_idx}", shp, dt))
                PSC = [pps(f"PSC{i}", [128, 1024], F32) for i in range(2)]
                PAV = [pps(f"PAV{i}", [65, 512], F32) for i in range(2)]
                PO = pps("PO", [128, 1024], F32)
                PSCh = [p.ap().rearrange("p (h e) -> p h e", h=2) for p in PSC]

                with nc.Block() as blk:
                    ldwo = S.dma("gpsimd", Wldv[:, :, :],
                                 wo_d.rearrange("(c p) n -> p c n", p=128),
                                 "gw")

                    wd_stream = w1_d if s_idx == 0 else w2_d
                    wtgt = w1max if s_idx == 0 else w2max
                    wflat = wd_stream.rearrange("a b -> (a b)")
                    gfree = [None, None]
                    for t in range(32):
                        sl = t % 2
                        sw = "sw%d" % sl
                        wt = [] if gfree[sl] is None else [("ve", gfree[sl])]
                        lm = S.dma("sync", wstag[sl][:, :],
                                   wflat[131072 * t:131072 * (t + 1)]
                                   .rearrange("(p n) -> p n", p=128), sw, wt)
                        def em(nc_, eng, sl=sl):
                            return eng.reduce_max(wmax[:, 0:1],
                                                  wstag[sl][:, :],
                                                  axis=AX.X,
                                                  apply_absolute_value=True)
                        S.op("vector", em, incs=[("ve", 1)],
                             waits=[(sw, lm)])
                        rm = S.mark("ve")
                        def em(nc_, eng, wtgt=wtgt):
                            return eng.tensor_max(wtgt[:, :], wtgt[:, :],
                                                  wmax[:, 0:1])
                        S.op("vector", em, incs=[("ve", 1)],
                             waits=[("ve", rm)])
                        gfree[sl] = S.mark("ve")

                    psc_free = [None, None]
                    at_free = [None, None]
                    av_free = None
                    den_freeA = None
                    den_freeB = None
                    bst_free = None
                    den_recm = {}
                    den_std = {}
                    nrm_marks = {}
                    rbv = rb.ap().rearrange("p (r n) -> p r n", r=2)
                    atn = [0]
                    pend_den = [None]
                    pend_norm = []
                    norm_delay = [0]

                    def emit_den(hp, bstm):
                        wld = [("sdn", S.mark("sdn"))]
                        if hp >= 1:
                            wld.append(("ve", den_recm[hp - 1]))
                        ldd = S.dma("sync",
                                    xf32v[:, 0, 32 * hp:32 * (hp + 1)],
                                    den_dram[s_idx, hp].rearrange(
                                        "(p n) -> p n", p=128),
                                    "sdl", wld)
                        def em(nc_, eng, hp=hp):
                            return eng.reciprocal(
                                xf32v[:, 1, 32 * hp:32 * (hp + 1)],
                                xf32v[:, 0, 32 * hp:32 * (hp + 1)])
                        S.op("vector", em, incs=[("ve", 1)],
                             waits=[("sdl", ldd)])
                        rcp = S.mark("ve")
                        wcv = [("ve", rcp)]
                        if hp >= 1:
                            wcv.append(("sdi", den_std[hp - 1]))
                        def em(nc_, eng, hp=hp):
                            return eng.tensor_scalar(
                                deni_b[:, :],
                                xf32v[:, 1, 32 * hp:32 * (hp + 1)],
                                1.0, None, ALU.mult)
                        S.op("vector", em, incs=[("ve", 1)], waits=wcv)
                        den_recm[hp] = S.mark("ve")
                        std = S.dma("sync",
                                    deni_dram[s_idx, hp].rearrange(
                                        "(p n) -> p n", p=128),
                                    deni_b[:, :],
                                    "sdi", [("ve", den_recm[hp])])
                        den_std[hp] = std
                        sl = hp % 2
                        srbs = "srb" if sl == 0 else "srb2"
                        wtb = [("sdi", std), ("sbs", bstm)]
                        if hp >= 2:
                            wtb.append(("ve", nrm_marks[hp - 2]))
                        S.dma("sync", rbv[0:64, sl, :],
                              deni_dram[s_idx, hp, 0:2048].unsqueeze(0)
                              .partition_broadcast(64),
                              srbs, wtb)
                        bB = S.dma("sync", rbv[64:128, sl, :],
                                   deni_dram[s_idx, hp, 2048:4096]
                                   .unsqueeze(0).partition_broadcast(64),
                                   srbs)
                        for pc in range(4):
                            pend_norm.append((hp, sl, pc, srbs, bB))
                        norm_delay[0] = 6

                    def emit_norm_piece():
                        hp, sl, pc, srbs, bB = pend_norm.pop(0)
                        def em(nc_, eng, hp=hp, sl=sl, pc=pc):
                            return eng.tensor_mul(
                                slotAv[:, hp, 512 * pc:512 * (pc + 1)],
                                slotAv[:, hp, 512 * pc:512 * (pc + 1)],
                                rbv[:, sl, 512 * pc:512 * (pc + 1)])
                        S.op("vector", em, incs=[("ve", 1)],
                             waits=[(srbs, bB)])
                        if pc == 3:
                            nrm_marks[hp] = S.mark("ve")

                    for hp in range(8):
                        for qc in range(4):
                            first = True
                            pend_av = None
                            if qc == 0:
                                wavs = []
                                if av_free is not None:
                                    wavs.append(("ve", av_free))
                                def em(nc_, eng, hp=hp):
                                    r = None
                                    for hh in (0, 1):
                                        r = eng.matmul(
                                            PAV[hh][:, 0:128],
                                            V2v[:, 0, 2 * hp + hh, :],
                                            ATq[:, 256 * hp + 128 * hh:
                                                256 * hp + 128 * hh + 128],
                                            start=True, stop=False)
                                    return r
                                S.op("tensor", em, incs=[("te", 1)],
                                     waits=wavs)
                                first = False
                            for kt in range(4 * qc + 4):
                                a = max(512 * qc, 128 * kt, 128)
                                w = 512 * (qc + 1) - a
                                pf = atn[0] % 2
                                ap_ = atn[0] % 2
                                atn[0] += 1
                                wt = [("gs", C0_gs)]
                                if psc_free[pf] is not None:
                                    wt.append(("sc", psc_free[pf]))
                                def em(nc_, eng, hp=hp, kt=kt, a=a, w=w,
                                       pf=pf):
                                    eng.matmul(
                                        PSC[pf][:, 0:w],
                                        KTv[0:64, hp, 128 * kt:128 * (kt + 1)],
                                        QTv[0:64, hp, a:a + w],
                                        start=True, stop=True,
                                        tile_position=(0, 0))
                                    return eng.matmul(
                                        PSC[pf][:, 512:512 + w],
                                        KTv[64:128, hp,
                                            128 * kt:128 * (kt + 1)],
                                        QTv[64:128, hp, a:a + w],
                                        start=True, stop=True,
                                        tile_position=(64, 0))
                                S.op("tensor", em, incs=[("te", 1)], waits=wt)
                                qkm = S.mark("te")
                                wte = [("te", qkm)]
                                if at_free[ap_] is not None:
                                    wte.append(("te", at_free[ap_]))
                                def em(nc_, eng, w=w, pf=pf, ap_=ap_,
                                       s_idx=s_idx):
                                    return eng.activation(
                                        ATv[:, ap_, :].rearrange(
                                            "p (h e) -> p h e", h=2)[:, :, 0:w],
                                        PSCh[pf][:, :, 0:w],
                                        AF.Exp, bias=col(s_idx), scale=0.125)
                                S.op("scalar", em, incs=[("sc", 1)], waits=wte)
                                psc_free[pf] = S.mark("sc")
                                expm = S.mark("sc")
                                mkm = None
                                if a == 128 * kt:
                                    def em(nc_, eng, ap_=ap_):
                                        eng.tensor_mul(ATv[:, ap_, 0:128],
                                                       ATv[:, ap_, 0:128],
                                                       trimask[:, :])
                                        return eng.tensor_mul(
                                            ATv[:, ap_, 512:640],
                                            ATv[:, ap_, 512:640],
                                            trimask[:, :])
                                    S.op("vector", em, incs=[("ve", 1)],
                                         waits=[("sc", expm)])
                                    mkm = S.mark("ve")
                                wav = [("sc", expm)]
                                if mkm is not None:
                                    wav.append(("ve", mkm))
                                if first and av_free is not None:
                                    wav.append(("ve", av_free))
                                def em(nc_, eng, hp=hp, kt=kt, a=a, w=w,
                                       qc=qc, ap_=ap_, first=first):
                                    off = a - 512 * qc
                                    last = (kt == 4 * qc + 3)
                                    r = None
                                    for hh in (0, 1):
                                        r = eng.matmul(
                                            PAV[hh][:, off:off + w],
                                            V2v[:, kt, 2 * hp + hh, :],
                                            ATv[:, ap_,
                                                512 * hh:512 * hh + w],
                                            start=first, stop=last)
                                    return r
                                if pend_av is not None:
                                    pw, pap = pend_av
                                    S.op("tensor", pw, incs=[("te", 1)],
                                         waits=pap)
                                    at_free[1 - ap_] = S.mark("te")
                                pend_av = (em, wav)
                                first = False
                                norm_delay[0] -= 1
                                if pend_norm and norm_delay[0] <= 0:
                                    emit_norm_piece()
                            pw, pap = pend_av
                            S.op("tensor", pw, incs=[("te", 1)], waits=pap)
                            at_free[(atn[0] - 1) % 2] = S.mark("te")
                            avm = S.mark("te")
                            wtA = [("te", avm)]
                            if den_freeA is not None:
                                wtA.append(("sdn", den_freeA))
                            def em(nc_, eng, hp=hp, qc=qc):
                                eng.tensor_copy(xf32.ap()[64:65, 256:768],
                                                PAV[0][64:65, :])
                                return eng.tensor_copy(
                                    slotAv[0:64, hp, 512 * qc:512 * (qc + 1)],
                                    PAV[0][0:64, :])
                            S.op("vector", em, incs=[("ve", 1)], waits=wtA)
                            dmA = S.dma("sync",
                                        den_dram[s_idx, hp,
                                                 512 * qc:512 * (qc + 1)],
                                        xf32.ap()[64:65, 256:768], "sdn",
                                        [("ve", S.mark("ve"))])
                            den_freeA = dmA
                            wtB = []
                            if den_freeB is not None:
                                wtB.append(("sdn", den_freeB))
                            if bst_free is not None and qc == 0:
                                wtB.append(("sbs", bst_free))
                            def em(nc_, eng, qc=qc):
                                eng.tensor_copy(xf32.ap()[64:65, 1280:1792],
                                                PAV[1][64:65, :])
                                return eng.tensor_copy(
                                    Bstage[0:64, 512 * qc:512 * (qc + 1)],
                                    PAV[1][0:64, :])
                            S.op("vector", em, incs=[("ve", 1)], waits=wtB)
                            dmB = S.dma("sync",
                                        den_dram[s_idx, hp,
                                                 2048 + 512 * qc:
                                                 2048 + 512 * (qc + 1)],
                                        xf32.ap()[64:65, 1280:1792], "sdn",
                                        [("ve", S.mark("ve"))])
                            den_freeB = dmB
                            av_free = S.mark("ve")
                            if qc == 0 and pend_den[0] is not None:
                                emit_den(*pend_den[0])
                                pend_den[0] = None
                        bst_free = S.dma("sync", slotAv[64:128, hp, :],
                                         Bstage[0:64, :], "sbs",
                                         [("ve", S.mark("ve"))])
                        pend_den[0] = (hp, bst_free)
                    emit_den(*pend_den[0])
                    pend_den[0] = None
                    while pend_norm:
                        emit_norm_piece()
                    NRM_ve = S.mark("ve")
                    recm = den_recm[7]
                    std = den_std[7]

                    if debug:
                        S.dma("sync", dbgmap["avt"][s_idx], slotA[:, :],
                              "sdbg", [("ve", NRM_ve)])

                    po_free = None
                    sdn_fin = S.mark("sdn")
                    x_free = [[("ve", recm), ("sdn", sdn_fin)],
                              [("ve", recm), ("sdi", std), ("sdn", sdn_fin)]]
                    for tt in range(16):
                        xp = tt % 2
                        sxr = "sxr%d" % xp
                        ssp = "ssp%d" % xp
                        lx = S.dma("sync", xf32v[:, xp, :],
                                   x_d[tok0 + 128 * tt:
                                       tok0 + 128 * (tt + 1), :],
                                   sxr, x_free[xp])
                        wt = [("ve", NRM_ve), ("gw", ldwo)]
                        if po_free is not None:
                            wt.append(("ve", po_free))
                        def em(nc_, eng, tt=tt):
                            r = None
                            for hh in range(2):
                                for dc in range(8):
                                    r = eng.matmul(
                                        PO[:, 512 * hh:512 * (hh + 1)],
                                        slotAv[:, dc, 128 * tt:128 * (tt + 1)],
                                        Wldv[:, dc, 512 * hh:512 * (hh + 1)],
                                        start=(dc == 0), stop=(dc == 7))
                            return r
                        S.op("tensor", em, incs=[("te", 1)], waits=wt)
                        opm = S.mark("te")
                        def em(nc_, eng, xp=xp):
                            return eng.tensor_add(x1tv[:, xp, :], PO[:, :],
                                                  xf32v[:, xp, :])
                        S.op("vector", em, incs=[("ve", 1)],
                             waits=[("te", opm), (sxr, lx)])
                        po_free = S.mark("ve")
                        dbg_gate = [("ve", po_free)]
                        if debug and tt == 0:
                            dm_ = S.dma("sync", dbgmap["res0"][s_idx],
                                        x1tv[:, xp, :], "sdbg",
                                        [("ve", po_free)])
                            dbg_gate = [("sdbg", dm_)]
                        def em(nc_, eng, xp=xp):
                            return eng.activation(xf32v[:, xp, :],
                                                  x1tv[:, xp, :], AF.Square)
                        S.op("scalar", em, incs=[("sc", 1)], waits=dbg_gate)
                        sqsc = S.mark("sc")
                        def em(nc_, eng, xp=xp):
                            return eng.reduce_sum(lns[:, :], x1tv[:, xp, :],
                                                  axis=AX.X)
                        S.op("vector", em, incs=[("ve", 1)], waits=dbg_gate)
                        sqv = S.mark("ve")
                        def em(nc_, eng, xp=xp):
                            return eng.reduce_sum(lnv[:, :], xf32v[:, xp, :],
                                                  axis=AX.X)
                        S.op("vector", em, incs=[("ve", 1)],
                             waits=[("ve", sqv), ("sc", sqsc)])
                        summ = S.mark("ve")
                        sqm = summ
                        if debug and tt == 0:
                            dm2_ = S.dma("sync", dbgmap["sqd"][s_idx],
                                         xf32v[:, xp, :], "sdbg",
                                         [("ve", summ)])
                            def em(nc_, eng):
                                return eng.tensor_scalar(srow[:, 31:32],
                                                         lnv[:, :], 1.0,
                                                         None, ALU.mult)
                            S.op("vector", em, incs=[("ve", 1)],
                                 waits=[("ve", summ)])
                            dm3_ = S.dma("sync", dbgmap["lnv0"][s_idx],
                                         srow[:, 31:32], "sdbg",
                                         [("ve", S.mark("ve"))])
                            summ = S.mark("ve")
                        def em(nc_, eng):
                            eng.tensor_mul(lnq[:, :], lns[:, :], lns[:, :])
                            eng.tensor_scalar(lnm[:, :], lns[:, :], 1.0 / D,
                                              None, ALU.mult)
                            return eng.tensor_scalar(
                                lnq[:, :], lnq[:, :], 1.0 / (D * D), EPS,
                                ALU.mult, ALU.subtract)
                        S.op("vector", em, incs=[("ve", 1)],
                             waits=[("ve", summ)])
                        st1a = S.mark("ve")
                        def em(nc_, eng):
                            return eng.scalar_tensor_tensor(
                                lnv[:, :], lnv[:, :], 1.0 / D, lnq[:, :],
                                ALU.mult, ALU.subtract)
                        S.op("vector", em, incs=[("ve", 1)],
                             waits=[("ve", st1a)])
                        st1m_ = S.mark("ve")
                        sq_gate = [("ve", st1m_)]
                        if debug and tt == 0:
                            def em(nc_, eng, s_idx=s_idx):
                                eng.tensor_scalar(srow[:, 28:29], lns[:, :],
                                                  1.0, None, ALU.mult)
                                eng.tensor_scalar(srow[:, 29:30], lnm[:, :],
                                                  1.0, None, ALU.mult)
                                return eng.tensor_scalar(srow[:, 30:31],
                                                         lnv[:, :], 1.0,
                                                         None, ALU.mult)
                            S.op("vector", em, incs=[("ve", 1)],
                                 waits=[("ve", st1m_)])
                            dm_ = S.dma("sync", dbgmap["lnd"][s_idx][:, 0:3],
                                        srow[:, 28:31], "sdbg",
                                        [("ve", S.mark("ve"))])
                            sq_gate = [("sdbg", dm_)]
                        def em(nc_, eng):
                            return eng.activation(lnr[:, :], lnv[:, :],
                                                  AF.Sqrt)
                        S.op("scalar", em, incs=[("sc", 1)],
                             waits=sq_gate)
                        def em(nc_, eng):
                            return eng.reciprocal(lnr[:, :], lnr[:, :])
                        S.op("vector", em, incs=[("ve", 1)],
                             waits=[("sc", S.mark("sc"))])
                        vrc = S.mark("ve")
                        def em(nc_, eng, xp=xp):
                            return eng.tensor_scalar(
                                x1tv[:, xp, :], x1tv[:, xp, :], lnm[:, :],
                                lnr[:, :], ALU.subtract, ALU.mult)
                        S.op("vector", em, incs=[("ve", 1)],
                             waits=[("ve", vrc)])
                        def em(nc_, eng, xp=xp):
                            return eng.reduce_max(rmx[:, :], x1tv[:, xp, :],
                                                  axis=AX.X,
                                                  apply_absolute_value=True)
                        S.op("vector", em, incs=[("ve", 1)])
                        rm = S.mark("ve")
                        def em(nc_, eng):
                            return eng.tensor_max(xmax[:, :], xmax[:, :],
                                                  rmx[:, :])
                        S.op("vector", em, incs=[("ve", 1)],
                             waits=[("ve", rm)])
                        vem = S.mark("ve")
                        spm = S.dma("sync",
                                    x1_dram[tok0 + 128 * tt:
                                            tok0 + 128 * (tt + 1), :],
                                    x1tv[:, xp, :], ssp, [("ve", vem)])
                        x_free[xp] = [("ve", vem), (ssp, spm)]
                    S.play(blk, entry=bar)

    bar = S.marks()
    with ExitStack() as fes:
        fsb = lambda nm, shp, dt: fes.enter_context(
            nc.sbuf_tensor(nm, shp, dt))
        w1qt = fsb("w1qt", [128, 8 * HF], BF16)
        wqst = fsb("wqst", [128, 2 * HF], F32)
        x1qs = fsb("x1qs", [128, 2 * 8 * 512], BF16)
        xld = fsb("xldC", [128, 2 * 1024], F32)
        xqb = fsb("xqbC", [128, 2 * 1024], BF16)
        hst = fsb("hstC", [128, 4 * 512], BF16)
        w1qv = w1qt.ap().rearrange("p (c n) -> p c n", c=8)
        wqsv = wqst.ap().rearrange("p (r n) -> p r n", r=2)
        x1qv = x1qs.ap().rearrange("p (g c n) -> p g c n", g=2, c=8)
        xldv = xld.ap().rearrange("p (r n) -> p r n", r=2)
        xqbv = xqb.ap().rearrange("p (r n) -> p r n", r=2)
        hstv = hst.ap().rearrange("p (r n) -> p r n", r=4)

        with ExitStack() as pes:
            pps = lambda nm, shp, dt: pes.enter_context(
                nc.psum_tensor(nm, shp, dt))
            PT = [pps(f"PTC{i}", [128, 1024], BF16) for i in range(2)]
            PH = [pps(f"PH{i}", [128, 512], F32) for i in range(4)]
            PTv = [p.ap().rearrange("p (c n) -> p c n", c=8) for p in PT]

            with nc.Block() as blk:
                m1 = S.dma("gpsimd", cc1i[0:1, :], xmax[:, 0], "gs")
                def em(nc_, eng):
                    return eng.collective_compute(
                        "AllReduce", ALU.max, replica_groups=[list(range(NCORE))],
                        ins=[cc1i], outs=[cc1o])
                S.op("gpsimd", em, incs=[("cc", 1)], waits=[("gs", m1)])
                m3 = S.dma("gpsimd", relay_dram[0, :], w1max[:, 0], "gs")
                m4 = S.dma("gpsimd", relay[0:1, :], relay_dram[0:1, :], "gs",
                           [("gs", m3)])
                def em(nc_, eng):
                    return eng.reduce_max(srow[0:1, 9:10], relay[0:1, :], axis=AX.X)
                S.op("vector", em, incs=[("ve", 1)], waits=[("gs", m4)])
                vw1 = S.mark("ve")
                m5 = S.dma("gpsimd", relay_dram[0, :], w2max[:, 0], "gs",
                           [("ve", vw1)])
                m6 = S.dma("gpsimd", relay[0:1, :], relay_dram[0:1, :], "gs",
                           [("gs", m5)])
                def em(nc_, eng):
                    return eng.reduce_max(srow[0:1, 10:11], relay[0:1, :], axis=AX.X)
                S.op("vector", em, incs=[("ve", 1)], waits=[("gs", m6)])
                vw2 = S.mark("ve")
                m2 = S.dma("gpsimd", relay[0:1, :], cc1o[0:1, :], "gs",
                           [("cc", S.mark("cc")), ("ve", vw2)])
                def em(nc_, eng):
                    return eng.reduce_max(srow[0:1, 8:9], relay[0:1, :], axis=AX.X)
                S.op("vector", em, incs=[("ve", 1)], waits=[("gs", m2)])
                rm = S.mark("ve")
                def em(nc_, eng):
                    eng.tensor_scalar(srow[0:1, 11:12], srow[0:1, 8:9], 1e-6, None,
                                      ALU.max)
                    eng.tensor_scalar(srow[0:1, 12:13], srow[0:1, 9:10], 1e-6, None,
                                      ALU.max)
                    return eng.tensor_scalar(srow[0:1, 13:14], srow[0:1, 10:11],
                                             1e-6, None, ALU.max)
                S.op("vector", em, incs=[("ve", 1)], waits=[("ve", rm)])
                rm = S.mark("ve")
                def em(nc_, eng):
                    eng.reciprocal(srow[0:1, 14:15], srow[0:1, 11:12])
                    eng.reciprocal(srow[0:1, 15:16], srow[0:1, 12:13])
                    return eng.reciprocal(srow[0:1, 16:17], srow[0:1, 13:14])
                S.op("vector", em, incs=[("ve", 1)], waits=[("ve", rm)])
                rm = S.mark("ve")
                def em(nc_, eng):
                    eng.tensor_scalar(srow[0:1, 17:20], srow[0:1, 14:17], 127.0,
                                      None, ALU.mult)
                    return eng.tensor_scalar(srow[0:1, 20:23], srow[0:1, 11:14],
                                             INV127, None, ALU.mult)
                S.op("vector", em, incs=[("ve", 1)], waits=[("ve", rm)])
                rm = S.mark("ve")
                def em(nc_, eng):
                    return eng.tensor_mul(srow[0:1, 24:25], srow[0:1, 20:21],
                                          srow[0:1, 21:22])
                S.op("vector", em, incs=[("ve", 1)], waits=[("ve", rm)])
                vm = S.mark("ve")
                ma = S.dma("gpsimd", sc_dram[CRX:CRX + 1], srow[0:1, 17:18], "gs",
                           [("ve", vm)])
                ma = S.dma("gpsimd", sc_dram[CS1:CS1 + 1], srow[0:1, 24:25], "gs",
                           [("gs", ma)])
                ma = S.dma("gpsimd", sc_dram[CRW1:CRW1 + 1], srow[0:1, 18:19], "gs",
                           [("gs", ma)])
                ma = S.dma("gpsimd", sc_dram[CRW2:CRW2 + 1], srow[0:1, 19:20], "gs",
                           [("gs", ma)])
                me = S.dma("gpsimd", scal[:, 0:8],
                           sc_dram[0:8].unsqueeze(0).partition_broadcast(128),
                           "gs", [("gs", ma)])
                if debug:
                    S.dma("sync", dbgmap["sc16"][0:1, 0:16], scal[0:1, 0:16], "sdbg",
                          [("gs", me)])
                    S.dma("sync", dbgmap["xmaxd"][:, :], xmax[:, :], "sdbg",
                          [("gs", me)])
                    S.dma("sync", dbgmap["relayd"][:, :], relay[:, :], "sdbg",
                          [("gs", me)])
                    S.dma("sync", dbgmap["srowd"][0:1, :], srow[0:1, 0:32], "sdbg",
                          [("gs", me)])
                bc1m = S.mark("gs")
                def em(nc_, eng):
                    return eng.tensor_scalar(b1s[:, :], b1c[:, :],
                                             col(CS1), None, ALU.mult)
                S.op("vector", em, incs=[("ve", 1)], waits=[("gs", bc1m)])
                B1S_ve = S.mark("ve")

                wq_free = [None, None]
                for dc in range(8):
                    wp = dc % 2
                    sw = "sw%d" % wp
                    wt = [] if wq_free[wp] is None else [("ve", wq_free[wp])]
                    lm = S.dma("sync", wqsv[:, wp, :],
                               w1_d[128 * dc:128 * (dc + 1), :], sw, wt)
                    def em(nc_, eng, wp=wp):
                        return eng.tensor_scalar(wqsv[:, wp, :],
                                                 wqsv[:, wp, :],
                                                 col(CRW1), C_RND, ALU.mult,
                                                 ALU.add)
                    S.op("vector", em, incs=[("ve", 1)],
                         waits=[(sw, lm), ("gs", bc1m)])
                    qsp = S.mark("ve")
                    def em(nc_, eng, wp=wp, dc=dc):
                        return eng.tensor_scalar(w1qv[:, dc, :],
                                                 wqsv[:, wp, :], -C_RND,
                                                 None, ALU.add)
                    S.op("vector", em, incs=[("ve", 1)], waits=[("ve", qsp)])
                    wq_free[wp] = S.mark("ve")
                W1Q_ve = S.mark("ve")

                pt_free = [None, None]
                x1q_free = [None, None]
                xld_free = [None, None]
                xqb_free = [None, None]
                ph_free = [None] * 4
                hs_free = [None] * 4
                phn = [0]
                hsn = [0]
                for tg in range(8):
                    pg = tg % 2
                    wt_stage = [] if x1q_free[pg] is None else \
                        [("te", x1q_free[pg])]
                    for t4 in range(4):
                        tt = 4 * tg + t4
                        xp = tt % 2
                        sxr = "sxr%d" % xp
                        wtl = [] if xld_free[xp] is None else \
                            [("ve", xld_free[xp])]
                        lm = S.dma("sync", xldv[:, xp, :],
                                   x1_dram[128 * tt:128 * (tt + 1), :],
                                   sxr, wtl)
                        wq_ = [(sxr, lm), ("gs", bc1m)]
                        if xqb_free[xp] is not None:
                            wq_.append(("te", xqb_free[xp]))
                        def em(nc_, eng, xp=xp):
                            return eng.tensor_scalar(
                                xldv[:, xp, :], xldv[:, xp, :],
                                col(CRX), C_RND, ALU.mult, ALU.add)
                        S.op("vector", em, incs=[("ve", 1)], waits=wq_)
                        qsp = S.mark("ve")
                        def em(nc_, eng, xp=xp):
                            return eng.tensor_scalar(xqbv[:, xp, :],
                                                     xldv[:, xp, :], -C_RND,
                                                     None, ALU.add)
                        S.op("vector", em, incs=[("ve", 1)],
                             waits=[("ve", qsp)])
                        xld_free[xp] = S.mark("ve")
                        qm = S.mark("ve")
                        pp = tt % 2
                        wt2 = [("ve", qm)] + wt_stage
                        if pt_free[pp] is not None:
                            wt2.append(("sc", pt_free[pp]))
                        def em(nc_, eng, xp=xp, pp=pp):
                            r = None
                            for j in range(8):
                                r = eng.transpose(
                                    PT[pp][:, 128 * j:128 * (j + 1)],
                                    xqbv[:, xp, 128 * j:128 * (j + 1)],
                                    ident[:, :])
                            return r
                        S.op("tensor", em, incs=[("te", 1)], waits=wt2)
                        xqb_free[xp] = S.mark("te")
                        def em(nc_, eng, pg=pg, t4=t4, pp=pp):
                            return eng.activation(
                                x1qv[:, pg, 0:8, 128 * t4:128 * (t4 + 1)],
                                PTv[pp], AF.Copy)
                        S.op("scalar", em, incs=[("sc", 1)],
                             waits=[("te", S.mark("te"))])
                        pt_free[pp] = S.mark("sc")
                    stage_sc = S.mark("sc")

                    for hc in range(32):
                        sl = phn[0] % 4
                        phn[0] += 1
                        wt = [("sc", stage_sc), ("ve", W1Q_ve)]
                        if ph_free[sl] is not None:
                            wt.append(("sc", ph_free[sl]))
                        def em(nc_, eng, hc=hc, pg=pg, sl=sl):
                            r = None
                            for dc in range(8):
                                r = eng.matmul(
                                    PH[sl][:, :],
                                    w1qv[:, dc, 128 * hc:128 * (hc + 1)],
                                    x1qv[:, pg, dc, :],
                                    start=(dc == 0), stop=(dc == 7))
                            return r
                        S.op("tensor", em, incs=[("te", 1)], waits=wt)
                        tm = S.mark("te")
                        hsl = hsn[0] % 4
                        hsn[0] += 1
                        sh = "sh%d" % hsl
                        wg = [("te", tm), ("ve", B1S_ve)]
                        if hs_free[hsl] is not None:
                            wg.append((sh, hs_free[hsl][0]))
                            wg.append(("ve", hs_free[hsl][1]))
                        def em(nc_, eng, hc=hc, sl=sl, hsl=hsl):
                            return eng.activation(
                                hstv[:, hsl, :], PH[sl][:, :], AF.Gelu,
                                bias=b1s[:, hc:hc + 1], scale=col(CS1))
                        S.op("scalar", em, incs=[("sc", 1)], waits=wg)
                        ph_free[sl] = S.mark("sc")
                        gm = S.mark("sc")
                        def em(nc_, eng, hsl=hsl):
                            return eng.reduce_max(rmx[:, :], hstv[:, hsl, :],
                                                  axis=AX.X,
                                                  apply_absolute_value=True)
                        S.op("vector", em, incs=[("ve", 1)],
                             waits=[("sc", gm)])
                        rm = S.mark("ve")
                        def em(nc_, eng):
                            return eng.tensor_max(hmax[:, :], hmax[:, :],
                                                  rmx[:, :])
                        S.op("vector", em, incs=[("ve", 1)],
                             waits=[("ve", rm)])
                        hm = S.mark("ve")
                        sp = S.dma("sync", h_dram[hc, :, 512 * tg:512 * (tg + 1)],
                                   hstv[:, hsl, :], sh, [("sc", gm)])
                        hs_free[hsl] = (sp, hm)
                    x1q_free[pg] = S.mark("te")
                S.play(blk, entry=bar)

    bar = S.marks()
    with ExitStack() as fes:
        fsb = lambda nm, shp, dt: fes.enter_context(
            nc.sbuf_tensor(nm, shp, dt))
        w2qt = fsb("w2qt", [128, 32 * 1024], BF16)
        w2st = fsb("w2st", [128, 2 * 1024], F32)
        hrl = fsb("hrl", [128, 3 * 32 * 512], BF16)
        hsc = fsb("hscr", [128, 4 * 512], F32)
        x1ld = fsb("x1ldD", [128, 2 * 1024], F32)
        ot = fsb("otD", [128, 2 * 1024], F32)
        B2S = fsb("B2S", [128, 1024], F32)
        w2qv = w2qt.ap().rearrange("p (c n) -> p c n", c=32)
        w2sv = w2st.ap().rearrange("p (r n) -> p r n", r=2)
        hrlv = hrl.ap().rearrange("p (r c n) -> p r c n", r=3, c=32)
        hscv = hsc.ap().rearrange("p (c n) -> p c n", c=4)
        x1lv = x1ld.ap().rearrange("p (r n) -> p r n", r=2)
        otv = ot.ap().rearrange("p (r n) -> p r n", r=2)

        with ExitStack() as pes:
            PO2 = [pes.enter_context(
                nc.psum_tensor(f"PO2{i}", [128, 1024], F32))
                for i in range(4)]

            with nc.Block() as blk:
                m1 = S.dma("gpsimd", cc2i[0:1, :], hmax[:, 0], "gs")
                def em(nc_, eng):
                    return eng.collective_compute(
                        "AllReduce", ALU.max,
                        replica_groups=[list(range(NCORE))],
                        ins=[cc2i], outs=[cc2o])
                S.op("gpsimd", em, incs=[("cc", 1)], waits=[("gs", m1)])
                m2 = S.dma("gpsimd", relay[0:1, :], cc2o[0:1, :], "gs",
                           [("cc", S.mark("cc"))])
                def em(nc_, eng):
                    return eng.reduce_max(srow[0:1, 8:9], relay[0:1, :],
                                          axis=AX.X)
                S.op("vector", em, incs=[("ve", 1)], waits=[("gs", m2)])
                rm = S.mark("ve")
                def em(nc_, eng):
                    return eng.tensor_scalar(srow[0:1, 9:10], srow[0:1, 8:9],
                                             1e-6, None, ALU.max)
                S.op("vector", em, incs=[("ve", 1)], waits=[("ve", rm)])
                rm = S.mark("ve")
                def em(nc_, eng):
                    return eng.reciprocal(srow[0:1, 10:11], srow[0:1, 9:10])
                S.op("vector", em, incs=[("ve", 1)], waits=[("ve", rm)])
                rm = S.mark("ve")
                def em(nc_, eng):
                    eng.tensor_scalar(srow[0:1, 11:12], srow[0:1, 10:11],
                                      127.0, None, ALU.mult)
                    return eng.tensor_scalar(srow[0:1, 12:13], srow[0:1, 9:10],
                                             INV127, None, ALU.mult)
                S.op("vector", em, incs=[("ve", 1)], waits=[("ve", rm)])
                rm = S.mark("ve")
                def em(nc_, eng):
                    return eng.tensor_mul(srow[0:1, 13:14], srow[0:1, 12:13],
                                          srow[0:1, 22:23])
                S.op("vector", em, incs=[("ve", 1)], waits=[("ve", rm)])
                vm = S.mark("ve")
                ma = S.dma("gpsimd", sc_dram[CRH:CRH + 1], srow[0:1, 11:12],
                           "gs", [("ve", vm)])
                ma = S.dma("gpsimd", sc_dram[CS2:CS2 + 1], srow[0:1, 13:14],
                           "gs", [("gs", ma)])
                bc = S.dma("gpsimd", scal[:, 0:8],
                           sc_dram[0:8].unsqueeze(0).partition_broadcast(128),
                           "gs", [("gs", ma)])

                bm = S.dma("sync", B2S[:, :],
                           b2_d[0:1, :].partition_broadcast(128), "sc0")
                def em(nc_, eng):
                    return eng.tensor_scalar(B2S[:, :], B2S[:, :], col(CS2),
                                             None, ALU.mult)
                S.op("vector", em, incs=[("ve", 1)],
                     waits=[("sc0", bm), ("gs", bc)])
                B2S_ve = S.mark("ve")

                W2STREAMS = ["sw0", "sw1", "ssw0", "ssw1"]
                w2q_mark = [None] * 32
                w2_free = [None] * 2
                def emit_w2(lo, hi):
                    for hc in range(lo, hi):
                        wp = hc % 2
                        sw = W2STREAMS[wp]
                        wt = [] if w2_free[wp] is None else \
                            [("ve", w2_free[wp])]
                        lm = S.dma("sync", w2sv[:, wp, :],
                                   w2_d[128 * hc:128 * (hc + 1), :], sw, wt)
                        def em(nc_, eng, wp=wp):
                            return eng.tensor_scalar(
                                w2sv[:, wp, :], w2sv[:, wp, :],
                                col(CRW2), C_RND, ALU.mult, ALU.add)
                        S.op("vector", em, incs=[("ve", 1)],
                             waits=[(sw, lm)])
                        qsp = S.mark("ve")
                        def em(nc_, eng, wp=wp, hc=hc):
                            return eng.tensor_scalar(
                                w2qv[:, hc, :], w2sv[:, wp, :], -C_RND,
                                None, ALU.add)
                        S.op("vector", em, incs=[("ve", 1)],
                             waits=[("ve", qsp)])
                        w2_free[wp] = S.mark("ve")
                        w2q_mark[hc] = S.mark("ve")

                mm_done = [None] * 8
                qd_done = [None] * 8
                po_free = [None] * 4
                xl_free = [None, None]

                def emit_reload_quant(bb):
                    hp = bb % 3
                    shq = "shq%d" % (bb % 2)
                    wl = [("te", mm_done[bb - 3])] if bb >= 3 else []
                    lmg = []
                    for g in range(4):
                        lm = S.dma("sync", hrlv[:, hp, 8 * g:8 * (g + 1), :],
                                   h_dram[8 * g:8 * (g + 1), :,
                                          512 * bb:512 * (bb + 1)].rearrange(
                                       "h p n -> p h n"),
                                   shq, wl)
                        lmg.append(lm)
                    for g in range(8):
                        wq_ = [(shq, lmg[3]), ("gs", bc)]
                        def em(nc_, eng, hp=hp, g=g):
                            return eng.tensor_scalar(
                                hscv[:, :, :],
                                hrlv[:, hp, 4 * g:4 * (g + 1), :],
                                col(CRH), C_RND, ALU.mult, ALU.add)
                        S.op("vector", em, incs=[("ve", 1)], waits=wq_)
                        v1 = S.mark("ve")
                        def em(nc_, eng, hp=hp, g=g):
                            return eng.tensor_scalar(
                                hrlv[:, hp, 4 * g:4 * (g + 1), :],
                                hscv[:, :, :], -C_RND, None, ALU.add)
                        S.op("vector", em, incs=[("ve", 1)],
                             waits=[("ve", v1)])
                    qd_done[bb] = S.mark("ve")

                def emit_tiles(bb):
                    hp = bb % 3
                    for j in range(4):
                        tt = 4 * bb + j
                        pop = tt % 4
                        for gs_ in range(4):
                            wt = [("ve", w2q_mark[8 * gs_ + 7])]
                            if gs_ == 0:
                                wt.append(("ve", qd_done[bb]))
                                if po_free[pop] is not None:
                                    wt.append(("ve", po_free[pop]))
                            def em(nc_, eng, hp=hp, j=j, pop=pop, gs_=gs_):
                                r = None
                                for hc in range(8 * gs_, 8 * gs_ + 8):
                                    for hh in range(2):
                                        r = eng.matmul(
                                            PO2[pop][:,
                                                     512 * hh:512 * (hh + 1)],
                                            hrlv[:, hp, hc,
                                                 128 * j:128 * (j + 1)],
                                            w2qv[:, hc,
                                                 512 * hh:512 * (hh + 1)],
                                            start=(hc == 0),
                                            stop=(hc == 31))
                                return r
                            S.op("tensor", em, incs=[("te", 1)], waits=wt)
                        mm = S.mark("te")
                        if j == 3:
                            mm_done[bb] = mm

                        ep = tt % 2
                        sxr = "sxr%d" % ep
                        sy = "sy%d" % ep
                        wx = [] if xl_free[ep] is None else xl_free[ep]
                        lx = S.dma("sync", x1lv[:, ep, :],
                                   x1_dram[128 * tt:128 * (tt + 1), :],
                                   sxr, wx)
                        wstt = [("te", mm), ("ve", B2S_ve)]
                        if xl_free[ep]:
                            wstt += [w for w in xl_free[ep] if w[0] == sy]
                        def em(nc_, eng, ep=ep, pop=pop):
                            return eng.scalar_tensor_tensor(
                                otv[:, ep, :], PO2[pop][:, :], col(CS2),
                                B2S[:, :], ALU.mult, ALU.add)
                        S.op("vector", em, incs=[("ve", 1)], waits=wstt)
                        sttm = S.mark("ve")
                        def em(nc_, eng, ep=ep):
                            return eng.tensor_add(otv[:, ep, :],
                                                  otv[:, ep, :],
                                                  x1lv[:, ep, :])
                        S.op("vector", em, incs=[("ve", 1)],
                             waits=[(sxr, lx), ("ve", sttm)])
                        po_free[pop] = S.mark("ve")
                        def em(nc_, eng, ep=ep):
                            eng.tensor_mul(x1lv[:, ep, :], otv[:, ep, :],
                                           otv[:, ep, :])
                            return eng.reduce_sum(lns[:, :], otv[:, ep, :],
                                                  axis=AX.X)
                        S.op("vector", em, incs=[("ve", 1)],
                             waits=[("ve", po_free[pop])])
                        sqv = S.mark("ve")
                        def em(nc_, eng, ep=ep):
                            return eng.reduce_sum(lnv[:, :], x1lv[:, ep, :],
                                                  axis=AX.X)
                        S.op("vector", em, incs=[("ve", 1)],
                             waits=[("ve", sqv)])
                        summ = S.mark("ve")
                        def em(nc_, eng):
                            eng.tensor_mul(lnq[:, :], lns[:, :], lns[:, :])
                            eng.tensor_scalar(lnm[:, :], lns[:, :], 1.0 / D,
                                              None, ALU.mult)
                            return eng.tensor_scalar(
                                lnq[:, :], lnq[:, :], 1.0 / (D * D), EPS,
                                ALU.mult, ALU.subtract)
                        S.op("vector", em, incs=[("ve", 1)],
                             waits=[("ve", summ)])
                        st1a = S.mark("ve")
                        def em(nc_, eng):
                            return eng.scalar_tensor_tensor(
                                lnv[:, :], lnv[:, :], 1.0 / D, lnq[:, :],
                                ALU.mult, ALU.subtract)
                        S.op("vector", em, incs=[("ve", 1)],
                             waits=[("ve", st1a)])
                        def em(nc_, eng):
                            return eng.activation(lnr[:, :], lnv[:, :],
                                                  AF.Sqrt)
                        S.op("scalar", em, incs=[("sc", 1)],
                             waits=[("ve", S.mark("ve"))])
                        def em(nc_, eng):
                            return eng.reciprocal(lnr[:, :], lnr[:, :])
                        S.op("vector", em, incs=[("ve", 1)],
                             waits=[("sc", S.mark("sc"))])
                        vrc = S.mark("ve")
                        def em(nc_, eng, ep=ep):
                            return eng.tensor_scalar(
                                otv[:, ep, :], otv[:, ep, :], lnm[:, :],
                                lnr[:, :], ALU.subtract, ALU.mult)
                        S.op("vector", em, incs=[("ve", 1)],
                             waits=[("ve", vrc)])
                        vem = S.mark("ve")
                        sym = S.dma("sync", y_d[128 * tt:128 * (tt + 1), :],
                                    otv[:, ep, :], sy, [("ve", vem)])
                        xl_free[ep] = [("ve", vem), (sy, sym)]

                emit_w2(0, 8)
                emit_reload_quant(0)
                emit_w2(8, 16)
                emit_reload_quant(1)
                emit_w2(16, 32)
                emit_reload_quant(2)
                for bb in range(8):
                    emit_tiles(bb)
                    if bb + 3 < 8:
                        emit_reload_quant(bb + 3)
                S.play(blk, entry=bar)

    fin = S.marks()
    with nc.Block() as blk:
        def em(nc_, eng):
            return eng.engine_nop()
        S.op("vector", em, incs=[("ve", 1)])
        S.play(blk, entry=fin)

    es.close()
    return nc, dbgmap


def kernel(**inputs):
    per_core = _host_prep(inputs)
    nc, _ = build_program()
    res = run_bass_kernel_spmd(nc, per_core, core_ids=list(range(NCORE)))
    return np.concatenate(
        [np.asarray(r["y"]).reshape(2, N, D) for r in res.results], axis=0)


# version 17
# speedup vs baseline: 1.0007x; 1.0007x over previous
from contextlib import ExitStack

import numpy as np
import ml_dtypes

import concourse.bass as bass
import concourse.mybir as mybir
from concourse.bass_utils import run_bass_kernel_spmd

F32 = mybir.dt.float32
BF16 = mybir.dt.bfloat16
AF = mybir.ActivationFunctionType
ALU = mybir.AluOpType
AX = mybir.AxisListType

NCORE = 8
N = 2048
D = 1024
NH = 16
DK = 64
HF = 4096
TPC = 2 * N
C_RND = 12582912.0
EPS = 1e-5
INV127 = 1.0 / 127.0

ENGN = ("sync", "tensor", "scalar", "vector", "gpsimd")

SEMNAMES = [
    "te", "sc", "ve", "ge", "cc",
    "sc0",
    "gx0", "gx1",
    "gw",
    "gs",
    "ssw0", "ssw1",
    "sw0", "sw1",
    "sxr0", "sxr1",
    "ssp0", "ssp1",
    "sdn",
    "sbs",
    "sdl", "sdi",
    "srb", "srb2",
    "hs",
    "sh0", "sh1", "sh2", "sh3",
    "shq0", "shq1",
    "sy0", "sy1",
    "sdbg",
]


class Sched:

    def __init__(self, nc, sems):
        self.nc = nc
        self.sems = sems
        self.cnt = {k: 0 for k in sems}
        self.steps = []

    def op(self, engine, emit, incs=(), waits=()):
        for s, v in waits:
            assert v <= self.cnt[s], f"wait {s}>={v} but only {self.cnt[s]}"
        self.steps.append((engine, emit, tuple(incs), tuple(waits)))
        for s, k in incs:
            self.cnt[s] += k

    def dma(self, engine, out, in_, stream, waits=()):
        def em(nc_, eng, out=out, in_=in_):
            return eng.dma_start(out=out, in_=in_)
        self.op(engine, em, incs=[(stream, 16)], waits=waits)
        return self.cnt[stream]

    def mark(self, sem):
        return self.cnt[sem]

    def marks(self):
        return dict(self.cnt)

    def play(self, block, entry=None):
        nc, steps, sems = self.nc, self.steps, self.sems
        for eng_name in ENGN:
            def body(eng, _en=eng_name):
                waited = {}
                if entry:
                    for s, v in entry.items():
                        if v > 0:
                            eng.wait_ge(sems[s], v)
                            waited[s] = v
                for engine, emit, incs, waits in steps:
                    if engine != _en:
                        continue
                    for s, v in waits:
                        if waited.get(s, -1) < v:
                            eng.wait_ge(sems[s], v)
                            waited[s] = v
                    instr = emit(nc, eng)
                    for s, k in incs:
                        instr.then_inc(sems[s], k)
            getattr(block, eng_name)(body)
        self.steps = []


def _rope_perm():
    idx = np.zeros(D, dtype=np.int64)
    for h in range(NH):
        b = h * DK
        idx[b:b + 32] = b + np.arange(0, DK, 2)
        idx[b + 32:b + 64] = b + np.arange(1, DK, 2)
    return idx


def _rope_tables():
    powers = np.arange(0, DK, 2, dtype=np.float64) / DK
    freqs = 1.0 / (10000.0 ** powers)
    t = np.arange(N, dtype=np.float64)
    ang = np.outer(freqs, t)
    cos = np.cos(ang)
    sin = np.sin(ang)
    c2 = np.concatenate([cos, cos, cos, cos], 0).astype(np.float32)
    s2 = np.concatenate([-sin, sin, -sin, sin], 0).astype(np.float32)
    return (c2.astype(ml_dtypes.bfloat16), s2.astype(ml_dtypes.bfloat16))


def _host_prep(inputs):
    perm = _rope_perm()
    wq = np.asarray(inputs["L0_attn_wq"])
    wk = np.asarray(inputs["L0_attn_wk"])
    wv = np.asarray(inputs["L0_attn_wv"])
    wo = np.asarray(inputs["L0_attn_wo"])
    w1 = np.asarray(inputs["L0_ffn_w1"])
    w2 = np.asarray(inputs["L0_ffn_w2"])
    b1 = np.asarray(inputs["L0_ffn_b1"])
    b2 = np.asarray(inputs["L0_ffn_b2"])
    c2, s2 = _rope_tables()
    common = {
        "wq_t": np.ascontiguousarray(wq[perm].T),
        "wk_t": np.ascontiguousarray(wk[perm].T),
        "wv_t": np.ascontiguousarray(wv.T),
        "wo_t": np.ascontiguousarray(wo.T),
        "w1_t": np.ascontiguousarray(w1.T),
        "w2_t": np.ascontiguousarray(w2.T),
        "b1_col": np.ascontiguousarray(b1.reshape(32, 128).T),
        "b2_row": np.ascontiguousarray(b2.reshape(1, D)),
        "tab_c2": c2,
        "tab_s2": s2,
    }
    x = np.asarray(inputs["L0_attn_x"])
    return [dict(common, x=np.ascontiguousarray(
        x[2 * c:2 * c + 2].reshape(TPC, D))) for c in range(NCORE)]


def build_program(debug=False, race=True):
    nc = bass.Bass("TRN2", target_bir_lowering=False,
                   detect_race_conditions=race)

    x_d = nc.dram_tensor("x", [TPC, D], F32, kind="ExternalInput").ap()
    wq_d = nc.dram_tensor("wq_t", [D, D], F32, kind="ExternalInput").ap()
    wk_d = nc.dram_tensor("wk_t", [D, D], F32, kind="ExternalInput").ap()
    wv_d = nc.dram_tensor("wv_t", [D, D], F32, kind="ExternalInput").ap()
    wo_d = nc.dram_tensor("wo_t", [D, D], F32, kind="ExternalInput").ap()
    w1_d = nc.dram_tensor("w1_t", [D, HF], F32, kind="ExternalInput").ap()
    w2_d = nc.dram_tensor("w2_t", [HF, D], F32, kind="ExternalInput").ap()
    b1_d = nc.dram_tensor("b1_col", [128, 32], F32, kind="ExternalInput").ap()
    b2_d = nc.dram_tensor("b2_row", [1, D], F32, kind="ExternalInput").ap()
    c2_d = nc.dram_tensor("tab_c2", [128, N], BF16, kind="ExternalInput").ap()
    s2_d = nc.dram_tensor("tab_s2", [128, N], BF16, kind="ExternalInput").ap()
    y_d = nc.dram_tensor("y", [TPC, D], F32, kind="ExternalOutput").ap()

    scratch_kind = "ExternalOutput" if debug else "Internal"
    x1_dram = nc.dram_tensor("x1_scr", [TPC, D], F32, kind=scratch_kind).ap()
    h_dram = nc.dram_tensor("h_scr", [32, 128, 4096], BF16,
                            kind=scratch_kind).ap()
    den_dram = nc.dram_tensor("den_scr", [2, 8, 4096], F32).ap()
    deni_dram = nc.dram_tensor("deni_scr", [2, 8, 4096], BF16).ap()
    sc_dram = nc.dram_tensor("sc_scr", [64], F32).ap()
    c0_dram = nc.dram_tensor("c0_scr", [64], F32).ap()
    relay_dram = nc.dram_tensor("relay_scr", [1, 128], F32).ap()
    cc1i = nc.dram_tensor("cc1i", [1, 128], F32).ap()
    cc1o = nc.dram_tensor("cc1o", [1, 128], F32).ap()
    cc2i = nc.dram_tensor("cc2i", [1, 128], F32).ap()
    cc2o = nc.dram_tensor("cc2o", [1, 128], F32).ap()

    dbgmap = {}
    if debug:
        for nm, shp, dt in [("qt", [2, 128, 8 * N], BF16),
                            ("kt", [2, 128, 8 * N], BF16),
                            ("v2", [2, 128, 16 * 16 * 65], BF16),
                            ("avt", [2, 128, 8 * N], BF16),
                            ("sc16", [1, 16], F32),
                            ("xmaxd", [128, 1], F32),
                            ("relayd", [1, 128], F32),
                            ("srowd", [1, 32], F32),
                            ("nmxd", [1, 32], F32),
                            ("qt0", [128, 2048], BF16),
                            ("res0", [2, 128, 1024], F32),
                            ("lnd", [2, 128, 8], F32),
                            ("sqd", [2, 128, 1024], F32),
                            ("lnv0", [2, 128, 1], F32)]:
            dbgmap[nm] = nc.dram_tensor("dbg_" + nm, shp, dt,
                                        kind="ExternalOutput").ap()

    es = ExitStack()
    sb = lambda nm, shp, dt: es.enter_context(nc.sbuf_tensor(nm, shp, dt))
    sem = lambda nm: es.enter_context(nc.semaphore(nm))

    C2 = sb("C2", [128, N], BF16)
    S2t = sb("S2t", [128, N], BF16)
    ident = sb("ident", [128, 128], BF16)
    trimask = sb("trimask", [128, 128], BF16)
    maskA = sb("maskA", [128, 128], BF16)
    onescol = sb("onescol", [128, 1], BF16)
    ones2 = sb("ones2", [128, 2], BF16)
    c0t = sb("c0t", [1, 64], F32)
    b1c = sb("b1c", [128, 32], F32)
    b1s = sb("b1s", [128, 32], F32)
    scal = sb("scal", [128, 16], F32)
    srow = sb("srow", [128, 32], F32)
    xmax = sb("xmax", [128, 1], F32)
    hmax = sb("hmax", [128, 1], F32)
    wmax = sb("wmax", [128, 1], F32)
    w1max = sb("w1max", [128, 1], F32)
    w2max = sb("w2max", [128, 1], F32)
    relay = sb("relay", [1, 128], F32)
    lnm = sb("lnm", [128, 1], F32)
    lnv = sb("lnv", [128, 1], F32)
    lnr = sb("lnr", [128, 1], F32)
    lns = sb("lns", [128, 1], F32)
    lnq = sb("lnq", [128, 1], F32)
    rmx = sb("rmx", [128, 1], F32)

    CNC0, CNC1, CRX, CS1, CRH, CS2, CRW1, CRW2 = range(8)

    sems = {k: sem(k) for k in SEMNAMES}
    S = Sched(nc, sems)

    def col(i):
        return scal[:, i:i + 1]

    with nc.Block() as blk:
        S.dma("sync", C2[:, :], c2_d[:, :], "sc0")
        S.dma("sync", S2t[:, :], s2_d[:, :], "sc0")
        S.dma("sync", b1c[:, :], b1_d[:, :], "sc0")

        def em(nc_, eng):
            eng.memset(ident[:, :], 1.0)
            eng.memset(maskA[:, :], 0.0)
            return eng.memset(trimask[:, :], 1.0)
        S.op("vector", em, incs=[("ve", 1)])
        msk_ve = S.mark("ve")

        def em(nc_, eng):
            eng.affine_select(out=ident[:, :], in_=ident[:, :],
                              compare_op=ALU.is_equal, fill=0.0, base=0,
                              pattern=[[-1, 128]], channel_multiplier=1)
            eng.affine_select(out=trimask[:, :], in_=trimask[:, :],
                              compare_op=ALU.is_ge, fill=0.0, base=0,
                              pattern=[[1, 128]], channel_multiplier=-1)
            return eng.affine_select(out=maskA[:, :], in_=maskA[:, :],
                                     compare_op=ALU.is_ge, fill=-1e30,
                                     base=0, pattern=[[-1, 128]],
                                     channel_multiplier=1)
        S.op("gpsimd", em, incs=[("ge", 1)], waits=[("ve", msk_ve)])

        def em(nc_, eng):
            eng.memset(onescol[:, :], 1.0)
            eng.memset(ones2[:, :], 0.0)
            eng.memset(ones2[0:64, 0:1], 1.0)
            eng.memset(ones2[64:128, 1:2], 1.0)
            eng.memset(xmax[:, :], 0.0)
            eng.memset(hmax[:, :], 0.0)
            eng.memset(w1max[:, :], 0.0)
            eng.memset(w2max[:, :], 0.0)
            eng.memset(srow[:, :], 0.0)
            return eng.memset(scal[:, :], 0.0)
        S.op("vector", em, incs=[("ve", 1)])
        S.dma("gpsimd", sc_dram[0:32], srow[0:1, 0:32], "gs",
              waits=[("ve", S.mark("ve"))])
        S.play(blk)

    with ExitStack() as aes:
        asb = lambda nm, shp, dt: aes.enter_context(nc.sbuf_tensor(nm, shp, dt))
        QT = asb("QT", [128, 8 * N], BF16)
        KT = asb("KT", [128, 8 * N], BF16)
        V2 = asb("V2", [128, 16 * 16 * 65], BF16)
        slotA = asb("slotA", [128, 8 * N], BF16)
        Wld = asb("Wld", [128, 8 * 1024], BF16)
        xstage = asb("xstage", [128, 2 * 1024], BF16)
        rsc1 = asb("rsc1", [128, N], BF16)
        rsc2 = asb("rsc2", [128, N], BF16)
        ATst = asb("ATst", [128, 2 * 1024], BF16)
        Bstage = asb("Bstage", [64, N], BF16)
        rb = asb("rb", [128, 2 * N], BF16)
        deni_b = asb("deni_b", [128, 32], BF16)
        nmx = asb("nmx", [2, 32], F32)
        xf32 = asb("xf32", [128, 2 * 1024], F32)
        x1t = asb("x1t", [128, 2 * 1024], F32)
        ATq = asb("ATq", [128, 16 * 128], BF16)

        QTv = QT.ap().rearrange("p (c n) -> p c n", c=8)
        KTv = KT.ap().rearrange("p (c n) -> p c n", c=8)
        V2v = V2.ap().rearrange("p (k s e) -> p k s e", k=16, s=16)
        slotAv = slotA.ap().rearrange("p (c n) -> p c n", c=8)
        Wldv = Wld.ap().rearrange("p (c n) -> p c n", c=8)
        xstv = xstage.ap().rearrange("p (r n) -> p r n", r=2)
        ATv = ATst.ap().rearrange("p (r n) -> p r n", r=2)
        xf32v = xf32.ap().rearrange("p (r n) -> p r n", r=2)
        x1tv = x1t.ap().rearrange("p (r n) -> p r n", r=2)
        wstag = [rsc1.ap().bitcast(F32), rsc2.ap().bitcast(F32)]

        vbar = S.marks()
        with nc.Block() as blk:
            def em(nc_, eng):
                return eng.memset(V2v[:, :, :, 64], 1.0)
            S.op("vector", em, incs=[("ve", 1)])
            S.play(blk, entry=vbar)

        for s_idx in range(2):
            tok0 = s_idx * N
            bar = S.marks()

            with ExitStack() as pes:
                pps = lambda nm, shp, dt: pes.enter_context(
                    nc.psum_tensor(nm + f"_{s_idx}", shp, dt))
                PT = [pps(f"PT{i}", [128, 1024], BF16) for i in range(2)]
                PS = [pps(f"PS{i}", [128, 512], F32) for i in range(2)]
                PN = pps("PN", [2, 2048], F32)
                PTv = [p.ap().rearrange("p (c n) -> p c n", c=8) for p in PT]
                PSh = [p.ap().rearrange("p (h e) -> p h e", h=8) for p in PS]

                with nc.Block() as blk:
                    pt_free = [None, None]
                    xst_free = [None, None]
                    for tt in range(16):
                        xp = tt % 2
                        gx = "gx%d" % xp
                        w = [] if xst_free[xp] is None else [("te", xst_free[xp])]
                        ldm = S.dma("gpsimd", xstv[:, xp, :],
                                    x_d[tok0 + 128 * tt: tok0 + 128 * (tt + 1), :],
                                    gx, w)
                        pp = tt % 2
                        wt = [(gx, ldm)]
                        if pt_free[pp] is not None:
                            wt.append(("sc", pt_free[pp]))
                        def em(nc_, eng, xp=xp, pp=pp):
                            r = None
                            for j in range(8):
                                r = eng.transpose(
                                    PT[pp][:, 128 * j:128 * (j + 1)],
                                    xstv[:, xp, 128 * j:128 * (j + 1)],
                                    ident[:, :])
                            return r
                        S.op("tensor", em, incs=[("te", 1)], waits=wt)
                        xst_free[xp] = S.mark("te")
                        def em(nc_, eng, tt=tt, pp=pp):
                            return eng.activation(
                                slotAv[:, 0:8, 128 * tt:128 * (tt + 1)],
                                PTv[pp], AF.Copy)
                        S.op("scalar", em, incs=[("sc", 1)],
                             waits=[("te", S.mark("te"))])
                        pt_free[pp] = S.mark("sc")
                    XT_sc = S.mark("sc")

                    ps_free = [None] * 2
                    psn = [0]
                    rsc_guard = {"t1": [None, None], "t2": [None, None],
                                 "nm": [None, None]}
                    swn = [0]

                    QFs = [rsc1.ap().bitcast(F32), x1tv[:, 0, :]]
                    QsFs = [rsc2.ap().bitcast(F32), x1tv[:, 1, :]]

                    def proj_fm(wd, store_v, qk_idx, ldw_waits):
                        ldw = S.dma("gpsimd", Wldv[:, :, :],
                                    wd.rearrange("(c p) n -> p c n", p=128),
                                    "gw", ldw_waits)
                        add_marks = []
                        for oc in range(8):
                            for h2 in range(2):
                                sl2 = (2 * oc + h2) % 2
                                QF = QFs[sl2]
                                QsF = QsFs[sl2]
                                for q4s in range(2):
                                    q4 = 2 * h2 + q4s
                                    sl = psn[0] % 2
                                    psn[0] += 1
                                    wt = [("gw", ldw), ("sc", XT_sc)]
                                    if ps_free[sl] is not None:
                                        wt.append(("sc", ps_free[sl]))
                                    def em(nc_, eng, oc=oc, q4=q4, sl=sl):
                                        r = None
                                        for dc in range(8):
                                            r = eng.matmul(
                                                PS[sl][:, :],
                                                Wldv[:, dc,
                                                     128 * oc:128 * (oc + 1)],
                                                slotAv[:, dc,
                                                       512 * q4:512 * (q4 + 1)],
                                                start=(dc == 0),
                                                stop=(dc == 7))
                                        return r
                                    S.op("tensor", em, incs=[("te", 1)],
                                         waits=wt)
                                    tm = S.mark("te")
                                    evw = [("te", tm)]
                                    if rsc_guard["t1"][sl2] is not None:
                                        evw.append(
                                            ("ve", rsc_guard["t1"][sl2]))
                                    def em(nc_, eng, q4s=q4s, sl=sl, QF=QF):
                                        return eng.activation(
                                            QF[:, 512 * q4s:512 * (q4s + 1)],
                                            PS[sl][:, :], AF.Copy)
                                    S.op("scalar", em, incs=[("sc", 1)],
                                         waits=evw)
                                    ps_free[sl] = S.mark("sc")
                                evm = S.mark("sc")
                                ssw = "ssw%d" % sl2
                                swn[0] += 1
                                sw_wt = [("sc", evm)]
                                if rsc_guard["t2"][sl2] is not None:
                                    sw_wt.append(("ve", rsc_guard["t2"][sl2]))
                                swm = None
                                for (od, os_) in ((0, 32), (32, 0), (64, 96),
                                                  (96, 64)):
                                    swm = S.dma("sync", QsF[od:od + 32, :],
                                                QF[os_:os_ + 32, :],
                                                ssw, sw_wt)
                                    sw_wt = []
                                tok0h = 1024 * h2
                                def em(nc_, eng, tok0h=tok0h, QF=QF):
                                    return eng.tensor_mul(
                                        QF[:, :], QF[:, :],
                                        C2[:, tok0h:tok0h + 1024])
                                S.op("vector", em, incs=[("ve", 1)],
                                     waits=[(ssw, swm)])
                                def em(nc_, eng, tok0h=tok0h, QsF=QsF):
                                    return eng.tensor_mul(
                                        QsF[:, :], QsF[:, :],
                                        S2t[:, tok0h:tok0h + 1024])
                                S.op("vector", em, incs=[("ve", 1)])
                                rsc_guard["t2"][sl2] = S.mark("ve")
                                def em(nc_, eng, oc=oc, h2=h2,
                                       store_v=store_v, QF=QF, QsF=QsF):
                                    return eng.tensor_add(
                                        store_v[:, oc,
                                                1024 * h2:1024 * (h2 + 1)],
                                        QF[:, :], QsF[:, :])
                                S.op("vector", em, incs=[("ve", 1)],
                                     waits=[("ve", rsc_guard["t2"][sl2])])
                                rsc_guard["t1"][sl2] = S.mark("ve")
                            add_marks.append(S.mark("ve"))
                        proj_te = S.mark("te")
                        sq_bufs = [ATst.ap(), ATq.ap()]
                        for oc in range(8):
                            sqb = sq_bufs[oc % 2]
                            sq_wt = [("ve", add_marks[oc])]
                            if rsc_guard["nm"][oc % 2] is not None:
                                sq_wt.append(("te", rsc_guard["nm"][oc % 2]))
                            def em(nc_, eng, oc=oc, store_v=store_v,
                                   sqb=sqb):
                                return eng.tensor_mul(sqb[:, :],
                                                      store_v[:, oc, :],
                                                      store_v[:, oc, :])
                            S.op("vector", em, incs=[("ve", 1)], waits=sq_wt)
                            sqm = ("ve", S.mark("ve"))
                            def em(nc_, eng, sqb=sqb):
                                r = None
                                for q4 in range(4):
                                    r = eng.matmul(
                                        PN[:, 512 * q4:512 * (q4 + 1)],
                                        ones2[:, :],
                                        sqb[:, 512 * q4:512 * (q4 + 1)],
                                        start=True, stop=True)
                                return r
                            S.op("tensor", em, incs=[("te", 1)],
                                 waits=[sqm])
                            nm_ = S.mark("te")
                            rsc_guard["nm"][oc % 2] = nm_
                            cc_ = 8 * qk_idx + oc
                            def em(nc_, eng, cc_=cc_):
                                return eng.reduce_max(nmx[0:2, cc_:cc_ + 1],
                                                      PN[0:2, :], axis=AX.X)
                            S.op("vector", em, incs=[("ve", 1)],
                                 waits=[("te", nm_)])
                        return proj_te

                    qte = proj_fm(wq_d, QTv, 0, [])
                    kte = proj_fm(wk_d, KTv, 1, [("te", qte)])
                    ROPE_ve = S.mark("ve")

                    ldw = S.dma("gpsimd", Wldv[:, :, :],
                                wv_d.rearrange("(c p) n -> p c n", p=128),
                                "gw", [("te", kte)])
                    for tt in range(16):
                        for hh in range(2):
                            sl = psn[0] % 2
                            psn[0] += 1
                            wt = [("gw", ldw), ("sc", XT_sc)]
                            if ps_free[sl] is not None:
                                wt.append(("sc", ps_free[sl]))
                            def em(nc_, eng, tt=tt, hh=hh, sl=sl):
                                r = None
                                for dc in range(8):
                                    r = eng.matmul(
                                        PS[sl][:, :],
                                        slotAv[:, dc, 128 * tt:128 * (tt + 1)],
                                        Wldv[:, dc, 512 * hh:512 * (hh + 1)],
                                        start=(dc == 0), stop=(dc == 7))
                                return r
                            S.op("tensor", em, incs=[("te", 1)], waits=wt)
                            tm = S.mark("te")
                            def em(nc_, eng, tt=tt, hh=hh, sl=sl):
                                return eng.activation(
                                    V2v[:, tt, 8 * hh:8 * (hh + 1), 0:64],
                                    PSh[sl], AF.Copy)
                            S.op("scalar", em, incs=[("sc", 1)],
                                 waits=[("te", tm)])
                            ps_free[sl] = S.mark("sc")

                    r1 = S.dma("gpsimd", c0_dram[0:32], nmx[0:2, 0:16],
                               "gs", [("ve", ROPE_ve)])
                    r2 = S.dma("gpsimd", c0t[0:1, 0:32],
                               c0_dram[0:32].unsqueeze(0), "gs",
                               [("gs", r1)])
                    def em(nc_, eng):
                        eng.tensor_mul(c0t[0:1, 32:40], c0t[0:1, 0:8],
                                       c0t[0:1, 8:16])
                        return eng.tensor_mul(c0t[0:1, 40:48],
                                              c0t[0:1, 16:24],
                                              c0t[0:1, 24:32])
                    S.op("vector", em, incs=[("ve", 1)], waits=[("gs", r2)])
                    rm = S.mark("ve")
                    def em(nc_, eng):
                        eng.tensor_scalar(c0t[0:1, 52:60], c0t[0:1, 32:40],
                                          1.0, None, ALU.mult)
                        return eng.tensor_scalar(c0t[0:1, 52:60],
                                                 c0t[0:1, 40:48], 1.0, None,
                                                 ALU.max)
                    S.op("vector", em, incs=[("ve", 1)], waits=[("ve", rm)])
                    rm = S.mark("ve")
                    def em(nc_, eng):
                        return eng.reduce_max(c0t[0:1, 48:49],
                                              c0t[0:1, 32:48], axis=AX.X)
                    S.op("vector", em, incs=[("ve", 1)], waits=[("ve", rm)])
                    def em(nc_, eng):
                        return eng.activation(c0t[0:1, 49:50],
                                              c0t[0:1, 48:49], AF.Sqrt)
                    S.op("scalar", em, incs=[("sc", 1)],
                         waits=[("ve", S.mark("ve"))])
                    def em(nc_, eng):
                        return eng.tensor_scalar(c0t[0:1, 50:51],
                                                 c0t[0:1, 49:50],
                                                 -0.125, None, ALU.mult)
                    S.op("vector", em, incs=[("ve", 1)],
                         waits=[("sc", S.mark("sc"))])
                    wm = S.dma("gpsimd", sc_dram[s_idx:s_idx + 1],
                               c0t[0:1, 50:51], "gs",
                               [("ve", S.mark("ve"))])
                    S.dma("gpsimd", col(s_idx),
                          sc_dram[s_idx:s_idx + 1].unsqueeze(0)
                          .partition_broadcast(128),
                          "gs", [("gs", wm)])

                    sp_ps = None
                    sp_pt = None
                    for hp in range(8):
                        wt = [("ve", ROPE_ve)]
                        if sp_ps is not None:
                            wt.append(("sc", sp_ps))
                        else:
                            for f in ps_free:
                                if f is not None:
                                    wt.append(("sc", f))
                        def em(nc_, eng, hp=hp):
                            eng.matmul(PS[0][:, 0:128],
                                       QTv[0:64, hp, 0:128],
                                       KTv[0:64, hp, 0:128],
                                       start=True, stop=True,
                                       tile_position=(0, 0))
                            return eng.matmul(PS[1][:, 0:128],
                                             QTv[64:128, hp, 0:128],
                                             KTv[64:128, hp, 0:128],
                                             start=True, stop=True,
                                             tile_position=(64, 0))
                        S.op("tensor", em, incs=[("te", 1)], waits=wt)
                        spm_te = S.mark("te")
                        def em(nc_, eng):
                            eng.tensor_add(PS[0][:, 0:128], PS[0][:, 0:128],
                                           maskA[:, :])
                            return eng.tensor_add(PS[1][:, 0:128],
                                                  PS[1][:, 0:128],
                                                  maskA[:, :])
                        S.op("vector", em, incs=[("ve", 1)],
                             waits=[("te", spm_te)])
                        mka = S.mark("ve")
                        def em(nc_, eng):
                            eng.reduce_max(lns[:, :], PS[0][:, 0:128],
                                           axis=AX.X, negate=True)
                            return eng.reduce_max(lnv[:, :], PS[1][:, 0:128],
                                                  axis=AX.X, negate=True)
                        S.op("vector", em, incs=[("ve", 1)],
                             waits=[("ve", mka)])
                        rdm = S.mark("ve")
                        def em(nc_, eng):
                            eng.tensor_scalar(lnm[:, :], lns[:, :], 0.125,
                                              None, ALU.mult)
                            return eng.tensor_scalar(lnr[:, :], lnv[:, :],
                                                     0.125, None, ALU.mult)
                        S.op("vector", em, incs=[("ve", 1)],
                             waits=[("ve", rdm)])
                        bsm = S.mark("ve")
                        def em(nc_, eng):
                            eng.activation(rsc1[:, 0:128], PS[0][:, 0:128],
                                           AF.Exp, bias=lnm[:, :],
                                           scale=0.125)
                            return eng.activation(rsc1[:, 128:256],
                                                  PS[1][:, 0:128],
                                                  AF.Exp, bias=lnr[:, :],
                                                  scale=0.125)
                        S.op("scalar", em, incs=[("sc", 1)],
                             waits=[("ve", bsm)])
                        sp_ps = S.mark("sc")
                        wtt = [("sc", sp_ps)]
                        if sp_pt is not None:
                            wtt.append(("sc", sp_pt))
                        else:
                            for f in pt_free:
                                if f is not None:
                                    wtt.append(("sc", f))
                        def em(nc_, eng):
                            eng.transpose(PT[0][:, 0:128], rsc1[:, 0:128],
                                          ident[:, :])
                            return eng.transpose(PT[1][:, 0:128],
                                                 rsc1[:, 128:256],
                                                 ident[:, :])
                        S.op("tensor", em, incs=[("te", 1)], waits=wtt)
                        trm = S.mark("te")
                        def em(nc_, eng, hp=hp):
                            eng.activation(
                                ATq[:, 256 * hp:256 * hp + 128],
                                PT[0][:, 0:128], AF.Copy)
                            return eng.activation(
                                ATq[:, 256 * hp + 128:256 * hp + 256],
                                PT[1][:, 0:128], AF.Copy)
                        S.op("scalar", em, incs=[("sc", 1)],
                             waits=[("te", trm)])
                        sp_pt = S.mark("sc")
                    S.play(blk, entry=bar)
            C0_gs = S.mark("gs")

            if debug:
                dbar = S.marks()
                with nc.Block() as blk:
                    S.dma("sync", dbgmap["qt"][s_idx], QT[:, :], "sdbg")
                    S.dma("sync", dbgmap["kt"][s_idx], KT[:, :], "sdbg")
                    S.dma("sync", dbgmap["v2"][s_idx], V2[:, :], "sdbg")
                    if s_idx == 0:
                        S.dma("sync", dbgmap["nmxd"][0:1, :],
                              nmx[0:2, 0:16], "sdbg")
                    S.play(blk, entry=dbar)

            bar = S.marks()
            with ExitStack() as pes:
                pps = lambda nm, shp, dt: pes.enter_context(
                    nc.psum_tensor(nm + f"_{s_idx}", shp, dt))
                PSC = [pps(f"PSC{i}", [128, 1024], F32) for i in range(2)]
                PAV = [pps(f"PAV{i}", [65, 512], F32) for i in range(2)]
                PO = pps("PO", [128, 1024], F32)
                PSCh = [p.ap().rearrange("p (h e) -> p h e", h=2) for p in PSC]

                with nc.Block() as blk:
                    ldwo = S.dma("gpsimd", Wldv[:, :, :],
                                 wo_d.rearrange("(c p) n -> p c n", p=128),
                                 "gw")

                    wd_stream = w1_d if s_idx == 0 else w2_d
                    wtgt = w1max if s_idx == 0 else w2max
                    wflat = wd_stream.rearrange("a b -> (a b)")
                    gfree = [None, None]
                    for t in range(32):
                        sl = t % 2
                        sw = "sw%d" % sl
                        wt = [] if gfree[sl] is None else [("ve", gfree[sl])]
                        lm = S.dma("sync", wstag[sl][:, :],
                                   wflat[131072 * t:131072 * (t + 1)]
                                   .rearrange("(p n) -> p n", p=128), sw, wt)
                        def em(nc_, eng, sl=sl):
                            return eng.reduce_max(wmax[:, 0:1],
                                                  wstag[sl][:, :],
                                                  axis=AX.X,
                                                  apply_absolute_value=True)
                        S.op("vector", em, incs=[("ve", 1)],
                             waits=[(sw, lm)])
                        rm = S.mark("ve")
                        def em(nc_, eng, wtgt=wtgt):
                            return eng.tensor_max(wtgt[:, :], wtgt[:, :],
                                                  wmax[:, 0:1])
                        S.op("vector", em, incs=[("ve", 1)],
                             waits=[("ve", rm)])
                        gfree[sl] = S.mark("ve")

                    psc_free = [None, None]
                    at_free = [None, None]
                    av_free = None
                    den_freeA = None
                    den_freeB = None
                    bst_free = None
                    den_recm = {}
                    den_std = {}
                    nrm_marks = {}
                    rbv = rb.ap().rearrange("p (r n) -> p r n", r=2)
                    atn = [0]
                    pend_den = [None]
                    pend_norm = []
                    norm_delay = [0]

                    def emit_den(hp, bstm):
                        wld = [("sdn", S.mark("sdn"))]
                        if hp >= 1:
                            wld.append(("ve", den_recm[hp - 1]))
                        ldd = S.dma("sync",
                                    xf32v[:, 0, 32 * hp:32 * (hp + 1)],
                                    den_dram[s_idx, hp].rearrange(
                                        "(p n) -> p n", p=128),
                                    "sdl", wld)
                        def em(nc_, eng, hp=hp):
                            return eng.reciprocal(
                                xf32v[:, 1, 32 * hp:32 * (hp + 1)],
                                xf32v[:, 0, 32 * hp:32 * (hp + 1)])
                        S.op("vector", em, incs=[("ve", 1)],
                             waits=[("sdl", ldd)])
                        rcp = S.mark("ve")
                        wcv = [("ve", rcp)]
                        if hp >= 1:
                            wcv.append(("sdi", den_std[hp - 1]))
                        def em(nc_, eng, hp=hp):
                            return eng.tensor_scalar(
                                deni_b[:, :],
                                xf32v[:, 1, 32 * hp:32 * (hp + 1)],
                                1.0, None, ALU.mult)
                        S.op("vector", em, incs=[("ve", 1)], waits=wcv)
                        den_recm[hp] = S.mark("ve")
                        std = S.dma("sync",
                                    deni_dram[s_idx, hp].rearrange(
                                        "(p n) -> p n", p=128),
                                    deni_b[:, :],
                                    "sdi", [("ve", den_recm[hp])])
                        den_std[hp] = std
                        sl = hp % 2
                        srbs = "srb" if sl == 0 else "srb2"
                        wtb = [("sdi", std), ("sbs", bstm)]
                        if hp >= 2:
                            wtb.append(("ve", nrm_marks[hp - 2]))
                        S.dma("sync", rbv[0:64, sl, :],
                              deni_dram[s_idx, hp, 0:2048].unsqueeze(0)
                              .partition_broadcast(64),
                              srbs, wtb)
                        bB = S.dma("sync", rbv[64:128, sl, :],
                                   deni_dram[s_idx, hp, 2048:4096]
                                   .unsqueeze(0).partition_broadcast(64),
                                   srbs)
                        for pc in range(4):
                            pend_norm.append((hp, sl, pc, srbs, bB))
                        norm_delay[0] = 6

                    def emit_norm_piece():
                        hp, sl, pc, srbs, bB = pend_norm.pop(0)
                        def em(nc_, eng, hp=hp, sl=sl, pc=pc):
                            return eng.tensor_mul(
                                slotAv[:, hp, 512 * pc:512 * (pc + 1)],
                                slotAv[:, hp, 512 * pc:512 * (pc + 1)],
                                rbv[:, sl, 512 * pc:512 * (pc + 1)])
                        S.op("vector", em, incs=[("ve", 1)],
                             waits=[(srbs, bB)])
                        if pc == 3:
                            nrm_marks[hp] = S.mark("ve")

                    for hp in range(8):
                        for qc in range(4):
                            first = True
                            pend_av = None
                            if qc == 0:
                                wavs = []
                                if av_free is not None:
                                    wavs.append(("ve", av_free))
                                def em(nc_, eng, hp=hp):
                                    r = None
                                    for hh in (0, 1):
                                        r = eng.matmul(
                                            PAV[hh][:, 0:128],
                                            V2v[:, 0, 2 * hp + hh, :],
                                            ATq[:, 256 * hp + 128 * hh:
                                                256 * hp + 128 * hh + 128],
                                            start=True, stop=False)
                                    return r
                                S.op("tensor", em, incs=[("te", 1)],
                                     waits=wavs)
                                first = False
                            for kt in range(4 * qc + 4):
                                a = max(512 * qc, 128 * kt, 128)
                                w = 512 * (qc + 1) - a
                                pf = atn[0] % 2
                                ap_ = atn[0] % 2
                                atn[0] += 1
                                wt = [("gs", C0_gs)]
                                if psc_free[pf] is not None:
                                    wt.append(("sc", psc_free[pf]))
                                def em(nc_, eng, hp=hp, kt=kt, a=a, w=w,
                                       pf=pf):
                                    eng.matmul(
                                        PSC[pf][:, 0:w],
                                        KTv[0:64, hp, 128 * kt:128 * (kt + 1)],
                                        QTv[0:64, hp, a:a + w],
                                        start=True, stop=True,
                                        tile_position=(0, 0))
                                    return eng.matmul(
                                        PSC[pf][:, 512:512 + w],
                                        KTv[64:128, hp,
                                            128 * kt:128 * (kt + 1)],
                                        QTv[64:128, hp, a:a + w],
                                        start=True, stop=True,
                                        tile_position=(64, 0))
                                S.op("tensor", em, incs=[("te", 1)], waits=wt)
                                qkm = S.mark("te")
                                wte = [("te", qkm)]
                                if at_free[ap_] is not None:
                                    wte.append(("te", at_free[ap_]))
                                def em(nc_, eng, w=w, pf=pf, ap_=ap_,
                                       s_idx=s_idx):
                                    return eng.activation(
                                        ATv[:, ap_, :].rearrange(
                                            "p (h e) -> p h e", h=2)[:, :, 0:w],
                                        PSCh[pf][:, :, 0:w],
                                        AF.Exp, bias=col(s_idx), scale=0.125)
                                S.op("scalar", em, incs=[("sc", 1)], waits=wte)
                                psc_free[pf] = S.mark("sc")
                                expm = S.mark("sc")
                                mkm = None
                                if a == 128 * kt:
                                    def em(nc_, eng, ap_=ap_):
                                        eng.tensor_mul(ATv[:, ap_, 0:128],
                                                       ATv[:, ap_, 0:128],
                                                       trimask[:, :])
                                        return eng.tensor_mul(
                                            ATv[:, ap_, 512:640],
                                            ATv[:, ap_, 512:640],
                                            trimask[:, :])
                                    S.op("vector", em, incs=[("ve", 1)],
                                         waits=[("sc", expm)])
                                    mkm = S.mark("ve")
                                wav = [("sc", expm)]
                                if mkm is not None:
                                    wav.append(("ve", mkm))
                                if first and av_free is not None:
                                    wav.append(("ve", av_free))
                                def em(nc_, eng, hp=hp, kt=kt, a=a, w=w,
                                       qc=qc, ap_=ap_, first=first):
                                    off = a - 512 * qc
                                    last = (kt == 4 * qc + 3)
                                    r = None
                                    for hh in (0, 1):
                                        r = eng.matmul(
                                            PAV[hh][:, off:off + w],
                                            V2v[:, kt, 2 * hp + hh, :],
                                            ATv[:, ap_,
                                                512 * hh:512 * hh + w],
                                            start=first, stop=last)
                                    return r
                                if pend_av is not None:
                                    pw, pap = pend_av
                                    S.op("tensor", pw, incs=[("te", 1)],
                                         waits=pap)
                                    at_free[1 - ap_] = S.mark("te")
                                pend_av = (em, wav)
                                first = False
                                norm_delay[0] -= 1
                                if pend_norm and norm_delay[0] <= 0:
                                    emit_norm_piece()
                            pw, pap = pend_av
                            S.op("tensor", pw, incs=[("te", 1)], waits=pap)
                            at_free[(atn[0] - 1) % 2] = S.mark("te")
                            avm = S.mark("te")
                            wtA = [("te", avm)]
                            if den_freeA is not None:
                                wtA.append(("sdn", den_freeA))
                            def em(nc_, eng, hp=hp, qc=qc):
                                eng.tensor_copy(xf32.ap()[64:65, 256:768],
                                                PAV[0][64:65, :])
                                return eng.tensor_copy(
                                    slotAv[0:64, hp, 512 * qc:512 * (qc + 1)],
                                    PAV[0][0:64, :])
                            S.op("vector", em, incs=[("ve", 1)], waits=wtA)
                            dmA = S.dma("sync",
                                        den_dram[s_idx, hp,
                                                 512 * qc:512 * (qc + 1)],
                                        xf32.ap()[64:65, 256:768], "sdn",
                                        [("ve", S.mark("ve"))])
                            den_freeA = dmA
                            wtB = []
                            if den_freeB is not None:
                                wtB.append(("sdn", den_freeB))
                            if bst_free is not None and qc == 0:
                                wtB.append(("sbs", bst_free))
                            def em(nc_, eng, qc=qc):
                                eng.tensor_copy(xf32.ap()[64:65, 1280:1792],
                                                PAV[1][64:65, :])
                                return eng.tensor_copy(
                                    Bstage[0:64, 512 * qc:512 * (qc + 1)],
                                    PAV[1][0:64, :])
                            S.op("vector", em, incs=[("ve", 1)], waits=wtB)
                            dmB = S.dma("sync",
                                        den_dram[s_idx, hp,
                                                 2048 + 512 * qc:
                                                 2048 + 512 * (qc + 1)],
                                        xf32.ap()[64:65, 1280:1792], "sdn",
                                        [("ve", S.mark("ve"))])
                            den_freeB = dmB
                            av_free = S.mark("ve")
                            if qc == 0 and pend_den[0] is not None:
                                emit_den(*pend_den[0])
                                pend_den[0] = None
                        bst_free = S.dma("sync", slotAv[64:128, hp, :],
                                         Bstage[0:64, :], "sbs",
                                         [("ve", S.mark("ve"))])
                        pend_den[0] = (hp, bst_free)
                    emit_den(*pend_den[0])
                    pend_den[0] = None
                    while pend_norm:
                        emit_norm_piece()
                    NRM_ve = S.mark("ve")
                    recm = den_recm[7]
                    std = den_std[7]

                    if debug:
                        S.dma("sync", dbgmap["avt"][s_idx], slotA[:, :],
                              "sdbg", [("ve", NRM_ve)])

                    po_free = None
                    sdn_fin = S.mark("sdn")
                    x_free = [[("ve", recm), ("sdn", sdn_fin)],
                              [("ve", recm), ("sdi", std), ("sdn", sdn_fin)]]
                    for tt in range(16):
                        xp = tt % 2
                        sxr = "sxr%d" % xp
                        ssp = "ssp%d" % xp
                        lx = S.dma("sync", xf32v[:, xp, :],
                                   x_d[tok0 + 128 * tt:
                                       tok0 + 128 * (tt + 1), :],
                                   sxr, x_free[xp])
                        wt = [("ve", NRM_ve), ("gw", ldwo)]
                        if po_free is not None:
                            wt.append(("ve", po_free))
                        def em(nc_, eng, tt=tt):
                            r = None
                            for hh in range(2):
                                for dc in range(8):
                                    r = eng.matmul(
                                        PO[:, 512 * hh:512 * (hh + 1)],
                                        slotAv[:, dc, 128 * tt:128 * (tt + 1)],
                                        Wldv[:, dc, 512 * hh:512 * (hh + 1)],
                                        start=(dc == 0), stop=(dc == 7))
                            return r
                        S.op("tensor", em, incs=[("te", 1)], waits=wt)
                        opm = S.mark("te")
                        def em(nc_, eng, xp=xp):
                            return eng.tensor_add(x1tv[:, xp, :], PO[:, :],
                                                  xf32v[:, xp, :])
                        S.op("vector", em, incs=[("ve", 1)],
                             waits=[("te", opm), (sxr, lx)])
                        po_free = S.mark("ve")
                        dbg_gate = [("ve", po_free)]
                        if debug and tt == 0:
                            dm_ = S.dma("sync", dbgmap["res0"][s_idx],
                                        x1tv[:, xp, :], "sdbg",
                                        [("ve", po_free)])
                            dbg_gate = [("sdbg", dm_)]
                        def em(nc_, eng, xp=xp):
                            return eng.activation(xf32v[:, xp, :],
                                                  x1tv[:, xp, :], AF.Square)
                        S.op("scalar", em, incs=[("sc", 1)], waits=dbg_gate)
                        sqsc = S.mark("sc")
                        def em(nc_, eng, xp=xp):
                            return eng.reduce_sum(lns[:, :], x1tv[:, xp, :],
                                                  axis=AX.X)
                        S.op("vector", em, incs=[("ve", 1)], waits=dbg_gate)
                        sqv = S.mark("ve")
                        def em(nc_, eng, xp=xp):
                            return eng.reduce_sum(lnv[:, :], xf32v[:, xp, :],
                                                  axis=AX.X)
                        S.op("vector", em, incs=[("ve", 1)],
                             waits=[("ve", sqv), ("sc", sqsc)])
                        summ = S.mark("ve")
                        sqm = summ
                        if debug and tt == 0:
                            dm2_ = S.dma("sync", dbgmap["sqd"][s_idx],
                                         xf32v[:, xp, :], "sdbg",
                                         [("ve", summ)])
                            def em(nc_, eng):
                                return eng.tensor_scalar(srow[:, 31:32],
                                                         lnv[:, :], 1.0,
                                                         None, ALU.mult)
                            S.op("vector", em, incs=[("ve", 1)],
                                 waits=[("ve", summ)])
                            dm3_ = S.dma("sync", dbgmap["lnv0"][s_idx],
                                         srow[:, 31:32], "sdbg",
                                         [("ve", S.mark("ve"))])
                            summ = S.mark("ve")
                        def em(nc_, eng):
                            eng.tensor_mul(lnq[:, :], lns[:, :], lns[:, :])
                            eng.tensor_scalar(lnm[:, :], lns[:, :], 1.0 / D,
                                              None, ALU.mult)
                            return eng.tensor_scalar(
                                lnq[:, :], lnq[:, :], 1.0 / (D * D), EPS,
                                ALU.mult, ALU.subtract)
                        S.op("vector", em, incs=[("ve", 1)],
                             waits=[("ve", summ)])
                        st1a = S.mark("ve")
                        def em(nc_, eng):
                            return eng.scalar_tensor_tensor(
                                lnv[:, :], lnv[:, :], 1.0 / D, lnq[:, :],
                                ALU.mult, ALU.subtract)
                        S.op("vector", em, incs=[("ve", 1)],
                             waits=[("ve", st1a)])
                        st1m_ = S.mark("ve")
                        sq_gate = [("ve", st1m_)]
                        if debug and tt == 0:
                            def em(nc_, eng, s_idx=s_idx):
                                eng.tensor_scalar(srow[:, 28:29], lns[:, :],
                                                  1.0, None, ALU.mult)
                                eng.tensor_scalar(srow[:, 29:30], lnm[:, :],
                                                  1.0, None, ALU.mult)
                                return eng.tensor_scalar(srow[:, 30:31],
                                                         lnv[:, :], 1.0,
                                                         None, ALU.mult)
                            S.op("vector", em, incs=[("ve", 1)],
                                 waits=[("ve", st1m_)])
                            dm_ = S.dma("sync", dbgmap["lnd"][s_idx][:, 0:3],
                                        srow[:, 28:31], "sdbg",
                                        [("ve", S.mark("ve"))])
                            sq_gate = [("sdbg", dm_)]
                        def em(nc_, eng):
                            return eng.activation(lnr[:, :], lnv[:, :],
                                                  AF.Sqrt)
                        S.op("scalar", em, incs=[("sc", 1)],
                             waits=sq_gate)
                        def em(nc_, eng):
                            return eng.reciprocal(lnr[:, :], lnr[:, :])
                        S.op("vector", em, incs=[("ve", 1)],
                             waits=[("sc", S.mark("sc"))])
                        vrc = S.mark("ve")
                        def em(nc_, eng, xp=xp):
                            return eng.tensor_scalar(
                                x1tv[:, xp, :], x1tv[:, xp, :], lnm[:, :],
                                lnr[:, :], ALU.subtract, ALU.mult)
                        S.op("vector", em, incs=[("ve", 1)],
                             waits=[("ve", vrc)])
                        def em(nc_, eng, xp=xp):
                            return eng.reduce_max(rmx[:, :], x1tv[:, xp, :],
                                                  axis=AX.X,
                                                  apply_absolute_value=True)
                        S.op("vector", em, incs=[("ve", 1)])
                        rm = S.mark("ve")
                        def em(nc_, eng):
                            return eng.tensor_max(xmax[:, :], xmax[:, :],
                                                  rmx[:, :])
                        S.op("vector", em, incs=[("ve", 1)],
                             waits=[("ve", rm)])
                        vem = S.mark("ve")
                        spm = S.dma("sync",
                                    x1_dram[tok0 + 128 * tt:
                                            tok0 + 128 * (tt + 1), :],
                                    x1tv[:, xp, :], ssp, [("ve", vem)])
                        x_free[xp] = [("ve", vem), (ssp, spm)]
                    S.play(blk, entry=bar)

    bar = S.marks()
    with ExitStack() as fes:
        fsb = lambda nm, shp, dt: fes.enter_context(
            nc.sbuf_tensor(nm, shp, dt))
        w1qt = fsb("w1qt", [128, 8 * HF], BF16)
        wqst = fsb("wqst", [128, 2 * HF], F32)
        x1qs = fsb("x1qs", [128, 2 * 8 * 512], BF16)
        xld = fsb("xldC", [128, 2 * 1024], F32)
        xqb = fsb("xqbC", [128, 2 * 1024], BF16)
        hst = fsb("hstC", [128, 4 * 512], BF16)
        w1qv = w1qt.ap().rearrange("p (c n) -> p c n", c=8)
        wqsv = wqst.ap().rearrange("p (r n) -> p r n", r=2)
        x1qv = x1qs.ap().rearrange("p (g c n) -> p g c n", g=2, c=8)
        xldv = xld.ap().rearrange("p (r n) -> p r n", r=2)
        xqbv = xqb.ap().rearrange("p (r n) -> p r n", r=2)
        hstv = hst.ap().rearrange("p (r n) -> p r n", r=4)

        with ExitStack() as pes:
            pps = lambda nm, shp, dt: pes.enter_context(
                nc.psum_tensor(nm, shp, dt))
            PT = [pps(f"PTC{i}", [128, 1024], BF16) for i in range(2)]
            PH = [pps(f"PH{i}", [128, 512], F32) for i in range(4)]
            PTv = [p.ap().rearrange("p (c n) -> p c n", c=8) for p in PT]

            with nc.Block() as blk:
                m1 = S.dma("scalar", cc1i[0:1, :], xmax[:, 0], "hs")
                def em(nc_, eng):
                    return eng.collective_compute(
                        "AllReduce", ALU.max, replica_groups=[list(range(NCORE))],
                        ins=[cc1i], outs=[cc1o])
                S.op("gpsimd", em, incs=[("cc", 1)], waits=[("hs", m1)])
                m3 = S.dma("scalar", relay_dram[0, :], w1max[:, 0], "hs")
                m4 = S.dma("scalar", relay[0:1, :], relay_dram[0:1, :], "hs",
                           [("hs", m3)])
                def em(nc_, eng):
                    return eng.reduce_max(srow[0:1, 9:10], relay[0:1, :], axis=AX.X)
                S.op("vector", em, incs=[("ve", 1)], waits=[("hs", m4)])
                vw1 = S.mark("ve")
                m5 = S.dma("scalar", relay_dram[0, :], w2max[:, 0], "hs",
                           [("ve", vw1)])
                m6 = S.dma("scalar", relay[0:1, :], relay_dram[0:1, :], "hs",
                           [("hs", m5)])
                def em(nc_, eng):
                    return eng.reduce_max(srow[0:1, 10:11], relay[0:1, :], axis=AX.X)
                S.op("vector", em, incs=[("ve", 1)], waits=[("hs", m6)])
                vw2 = S.mark("ve")
                m2 = S.dma("scalar", relay[0:1, :], cc1o[0:1, :], "hs",
                           [("cc", S.mark("cc")), ("ve", vw2)])
                def em(nc_, eng):
                    return eng.reduce_max(srow[0:1, 8:9], relay[0:1, :], axis=AX.X)
                S.op("vector", em, incs=[("ve", 1)], waits=[("hs", m2)])
                rm = S.mark("ve")
                def em(nc_, eng):
                    eng.tensor_scalar(srow[0:1, 11:12], srow[0:1, 8:9], 1e-6, None,
                                      ALU.max)
                    eng.tensor_scalar(srow[0:1, 12:13], srow[0:1, 9:10], 1e-6, None,
                                      ALU.max)
                    return eng.tensor_scalar(srow[0:1, 13:14], srow[0:1, 10:11],
                                             1e-6, None, ALU.max)
                S.op("vector", em, incs=[("ve", 1)], waits=[("ve", rm)])
                rm = S.mark("ve")
                def em(nc_, eng):
                    eng.reciprocal(srow[0:1, 14:15], srow[0:1, 11:12])
                    eng.reciprocal(srow[0:1, 15:16], srow[0:1, 12:13])
                    return eng.reciprocal(srow[0:1, 16:17], srow[0:1, 13:14])
                S.op("vector", em, incs=[("ve", 1)], waits=[("ve", rm)])
                rm = S.mark("ve")
                def em(nc_, eng):
                    eng.tensor_scalar(srow[0:1, 17:20], srow[0:1, 14:17], 127.0,
                                      None, ALU.mult)
                    return eng.tensor_scalar(srow[0:1, 20:23], srow[0:1, 11:14],
                                             INV127, None, ALU.mult)
                S.op("vector", em, incs=[("ve", 1)], waits=[("ve", rm)])
                rm = S.mark("ve")
                def em(nc_, eng):
                    return eng.tensor_mul(srow[0:1, 24:25], srow[0:1, 20:21],
                                          srow[0:1, 21:22])
                S.op("vector", em, incs=[("ve", 1)], waits=[("ve", rm)])
                vm = S.mark("ve")
                ma = S.dma("scalar", sc_dram[CRX:CRX + 1], srow[0:1, 17:18], "hs",
                           [("ve", vm)])
                ma = S.dma("scalar", sc_dram[CS1:CS1 + 1], srow[0:1, 24:25], "hs",
                           [("hs", ma)])
                ma = S.dma("scalar", sc_dram[CRW1:CRW1 + 1], srow[0:1, 18:19], "hs",
                           [("hs", ma)])
                ma = S.dma("scalar", sc_dram[CRW2:CRW2 + 1], srow[0:1, 19:20], "hs",
                           [("hs", ma)])
                me = S.dma("scalar", scal[:, 0:8],
                           sc_dram[0:8].unsqueeze(0).partition_broadcast(128),
                           "hs", [("hs", ma)])
                if debug:
                    S.dma("sync", dbgmap["sc16"][0:1, 0:16], scal[0:1, 0:16], "sdbg",
                          [("hs", me)])
                    S.dma("sync", dbgmap["xmaxd"][:, :], xmax[:, :], "sdbg",
                          [("hs", me)])
                    S.dma("sync", dbgmap["relayd"][:, :], relay[:, :], "sdbg",
                          [("hs", me)])
                    S.dma("sync", dbgmap["srowd"][0:1, :], srow[0:1, 0:32], "sdbg",
                          [("hs", me)])
                bc1m = S.mark("hs")
                def em(nc_, eng):
                    return eng.tensor_scalar(b1s[:, :], b1c[:, :],
                                             col(CS1), None, ALU.mult)
                S.op("vector", em, incs=[("ve", 1)], waits=[("hs", bc1m)])
                B1S_ve = S.mark("ve")

                wq_free = [None, None]
                for dc in range(8):
                    wp = dc % 2
                    sw = "sw%d" % wp
                    wt = [] if wq_free[wp] is None else [("ve", wq_free[wp])]
                    lm = S.dma("sync", wqsv[:, wp, :],
                               w1_d[128 * dc:128 * (dc + 1), :], sw, wt)
                    def em(nc_, eng, wp=wp):
                        return eng.tensor_scalar(wqsv[:, wp, :],
                                                 wqsv[:, wp, :],
                                                 col(CRW1), C_RND, ALU.mult,
                                                 ALU.add)
                    S.op("vector", em, incs=[("ve", 1)],
                         waits=[(sw, lm), ("hs", bc1m)])
                    qsp = S.mark("ve")
                    def em(nc_, eng, wp=wp, dc=dc):
                        return eng.tensor_scalar(w1qv[:, dc, :],
                                                 wqsv[:, wp, :], -C_RND,
                                                 None, ALU.add)
                    S.op("vector", em, incs=[("ve", 1)], waits=[("ve", qsp)])
                    wq_free[wp] = S.mark("ve")
                W1Q_ve = S.mark("ve")

                pt_free = [None, None]
                x1q_free = [None, None]
                xld_free = [None, None]
                xqb_free = [None, None]
                ph_free = [None] * 4
                hs_free = [None] * 4
                phn = [0]
                hsn = [0]
                for tg in range(8):
                    pg = tg % 2
                    wt_stage = [] if x1q_free[pg] is None else \
                        [("te", x1q_free[pg])]
                    for t4 in range(4):
                        tt = 4 * tg + t4
                        xp = tt % 2
                        sxr = "sxr%d" % xp
                        wtl = [] if xld_free[xp] is None else \
                            [("ve", xld_free[xp])]
                        lm = S.dma("sync", xldv[:, xp, :],
                                   x1_dram[128 * tt:128 * (tt + 1), :],
                                   sxr, wtl)
                        wq_ = [(sxr, lm), ("hs", bc1m)]
                        if xqb_free[xp] is not None:
                            wq_.append(("te", xqb_free[xp]))
                        def em(nc_, eng, xp=xp):
                            return eng.tensor_scalar(
                                xldv[:, xp, :], xldv[:, xp, :],
                                col(CRX), C_RND, ALU.mult, ALU.add)
                        S.op("vector", em, incs=[("ve", 1)], waits=wq_)
                        qsp = S.mark("ve")
                        def em(nc_, eng, xp=xp):
                            return eng.tensor_scalar(xqbv[:, xp, :],
                                                     xldv[:, xp, :], -C_RND,
                                                     None, ALU.add)
                        S.op("vector", em, incs=[("ve", 1)],
                             waits=[("ve", qsp)])
                        xld_free[xp] = S.mark("ve")
                        qm = S.mark("ve")
                        pp = tt % 2
                        wt2 = [("ve", qm)] + wt_stage
                        if pt_free[pp] is not None:
                            wt2.append(("sc", pt_free[pp]))
                        def em(nc_, eng, xp=xp, pp=pp):
                            r = None
                            for j in range(8):
                                r = eng.transpose(
                                    PT[pp][:, 128 * j:128 * (j + 1)],
                                    xqbv[:, xp, 128 * j:128 * (j + 1)],
                                    ident[:, :])
                            return r
                        S.op("tensor", em, incs=[("te", 1)], waits=wt2)
                        xqb_free[xp] = S.mark("te")
                        def em(nc_, eng, pg=pg, t4=t4, pp=pp):
                            return eng.activation(
                                x1qv[:, pg, 0:8, 128 * t4:128 * (t4 + 1)],
                                PTv[pp], AF.Copy)
                        S.op("scalar", em, incs=[("sc", 1)],
                             waits=[("te", S.mark("te"))])
                        pt_free[pp] = S.mark("sc")
                    stage_sc = S.mark("sc")

                    for hc in range(32):
                        sl = phn[0] % 4
                        phn[0] += 1
                        wt = [("sc", stage_sc), ("ve", W1Q_ve)]
                        if ph_free[sl] is not None:
                            wt.append(("sc", ph_free[sl]))
                        def em(nc_, eng, hc=hc, pg=pg, sl=sl):
                            r = None
                            for dc in range(8):
                                r = eng.matmul(
                                    PH[sl][:, :],
                                    w1qv[:, dc, 128 * hc:128 * (hc + 1)],
                                    x1qv[:, pg, dc, :],
                                    start=(dc == 0), stop=(dc == 7))
                            return r
                        S.op("tensor", em, incs=[("te", 1)], waits=wt)
                        tm = S.mark("te")
                        hsl = hsn[0] % 4
                        hsn[0] += 1
                        sh = "sh%d" % hsl
                        wg = [("te", tm), ("ve", B1S_ve)]
                        if hs_free[hsl] is not None:
                            wg.append((sh, hs_free[hsl][0]))
                            wg.append(("ve", hs_free[hsl][1]))
                        def em(nc_, eng, hc=hc, sl=sl, hsl=hsl):
                            return eng.activation(
                                hstv[:, hsl, :], PH[sl][:, :], AF.Gelu,
                                bias=b1s[:, hc:hc + 1], scale=col(CS1))
                        S.op("scalar", em, incs=[("sc", 1)], waits=wg)
                        ph_free[sl] = S.mark("sc")
                        gm = S.mark("sc")
                        def em(nc_, eng, hsl=hsl):
                            return eng.reduce_max(rmx[:, :], hstv[:, hsl, :],
                                                  axis=AX.X,
                                                  apply_absolute_value=True)
                        S.op("vector", em, incs=[("ve", 1)],
                             waits=[("sc", gm)])
                        rm = S.mark("ve")
                        def em(nc_, eng):
                            return eng.tensor_max(hmax[:, :], hmax[:, :],
                                                  rmx[:, :])
                        S.op("vector", em, incs=[("ve", 1)],
                             waits=[("ve", rm)])
                        hm = S.mark("ve")
                        sp = S.dma("sync", h_dram[hc, :, 512 * tg:512 * (tg + 1)],
                                   hstv[:, hsl, :], sh, [("sc", gm)])
                        hs_free[hsl] = (sp, hm)
                    x1q_free[pg] = S.mark("te")
                S.play(blk, entry=bar)

    bar = S.marks()
    with ExitStack() as fes:
        fsb = lambda nm, shp, dt: fes.enter_context(
            nc.sbuf_tensor(nm, shp, dt))
        w2qt = fsb("w2qt", [128, 32 * 1024], BF16)
        w2st = fsb("w2st", [128, 2 * 1024], F32)
        hrl = fsb("hrl", [128, 3 * 32 * 512], BF16)
        hsc = fsb("hscr", [128, 4 * 512], F32)
        x1ld = fsb("x1ldD", [128, 2 * 1024], F32)
        ot = fsb("otD", [128, 2 * 1024], F32)
        B2S = fsb("B2S", [128, 1024], F32)
        w2qv = w2qt.ap().rearrange("p (c n) -> p c n", c=32)
        w2sv = w2st.ap().rearrange("p (r n) -> p r n", r=2)
        hrlv = hrl.ap().rearrange("p (r c n) -> p r c n", r=3, c=32)
        hscv = hsc.ap().rearrange("p (c n) -> p c n", c=4)
        x1lv = x1ld.ap().rearrange("p (r n) -> p r n", r=2)
        otv = ot.ap().rearrange("p (r n) -> p r n", r=2)

        with ExitStack() as pes:
            PO2 = [pes.enter_context(
                nc.psum_tensor(f"PO2{i}", [128, 1024], F32))
                for i in range(4)]

            with nc.Block() as blk:
                m1 = S.dma("scalar", cc2i[0:1, :], hmax[:, 0], "hs")
                def em(nc_, eng):
                    return eng.collective_compute(
                        "AllReduce", ALU.max,
                        replica_groups=[list(range(NCORE))],
                        ins=[cc2i], outs=[cc2o])
                S.op("gpsimd", em, incs=[("cc", 1)], waits=[("hs", m1)])
                m2 = S.dma("scalar", relay[0:1, :], cc2o[0:1, :], "hs",
                           [("cc", S.mark("cc"))])
                def em(nc_, eng):
                    return eng.reduce_max(srow[0:1, 8:9], relay[0:1, :],
                                          axis=AX.X)
                S.op("vector", em, incs=[("ve", 1)], waits=[("hs", m2)])
                rm = S.mark("ve")
                def em(nc_, eng):
                    return eng.tensor_scalar(srow[0:1, 9:10], srow[0:1, 8:9],
                                             1e-6, None, ALU.max)
                S.op("vector", em, incs=[("ve", 1)], waits=[("ve", rm)])
                rm = S.mark("ve")
                def em(nc_, eng):
                    return eng.reciprocal(srow[0:1, 10:11], srow[0:1, 9:10])
                S.op("vector", em, incs=[("ve", 1)], waits=[("ve", rm)])
                rm = S.mark("ve")
                def em(nc_, eng):
                    eng.tensor_scalar(srow[0:1, 11:12], srow[0:1, 10:11],
                                      127.0, None, ALU.mult)
                    return eng.tensor_scalar(srow[0:1, 12:13], srow[0:1, 9:10],
                                             INV127, None, ALU.mult)
                S.op("vector", em, incs=[("ve", 1)], waits=[("ve", rm)])
                rm = S.mark("ve")
                def em(nc_, eng):
                    return eng.tensor_mul(srow[0:1, 13:14], srow[0:1, 12:13],
                                          srow[0:1, 22:23])
                S.op("vector", em, incs=[("ve", 1)], waits=[("ve", rm)])
                vm = S.mark("ve")
                ma = S.dma("scalar", sc_dram[CRH:CRH + 1], srow[0:1, 11:12],
                           "hs", [("ve", vm)])
                ma = S.dma("scalar", sc_dram[CS2:CS2 + 1], srow[0:1, 13:14],
                           "hs", [("hs", ma)])
                bc = S.dma("scalar", scal[:, 0:8],
                           sc_dram[0:8].unsqueeze(0).partition_broadcast(128),
                           "hs", [("hs", ma)])

                bm = S.dma("sync", B2S[:, :],
                           b2_d[0:1, :].partition_broadcast(128), "sc0")
                def em(nc_, eng):
                    return eng.tensor_scalar(B2S[:, :], B2S[:, :], col(CS2),
                                             None, ALU.mult)
                S.op("vector", em, incs=[("ve", 1)],
                     waits=[("sc0", bm), ("hs", bc)])
                B2S_ve = S.mark("ve")

                W2STREAMS = ["sw0", "sw1", "ssw0", "ssw1"]
                w2q_mark = [None] * 32
                w2_free = [None] * 2
                def emit_w2(lo, hi):
                    for hc in range(lo, hi):
                        wp = hc % 2
                        sw = W2STREAMS[wp]
                        wt = [] if w2_free[wp] is None else \
                            [("ve", w2_free[wp])]
                        lm = S.dma("sync", w2sv[:, wp, :],
                                   w2_d[128 * hc:128 * (hc + 1), :], sw, wt)
                        def em(nc_, eng, wp=wp):
                            return eng.tensor_scalar(
                                w2sv[:, wp, :], w2sv[:, wp, :],
                                col(CRW2), C_RND, ALU.mult, ALU.add)
                        S.op("vector", em, incs=[("ve", 1)],
                             waits=[(sw, lm)])
                        qsp = S.mark("ve")
                        def em(nc_, eng, wp=wp, hc=hc):
                            return eng.tensor_scalar(
                                w2qv[:, hc, :], w2sv[:, wp, :], -C_RND,
                                None, ALU.add)
                        S.op("vector", em, incs=[("ve", 1)],
                             waits=[("ve", qsp)])
                        w2_free[wp] = S.mark("ve")
                        w2q_mark[hc] = S.mark("ve")

                mm_done = [None] * 8
                qd_done = [None] * 8
                po_free = [None] * 4
                xl_free = [None, None]

                def emit_reload_quant(bb):
                    hp = bb % 3
                    shq = "shq%d" % (bb % 2)
                    wl = [("te", mm_done[bb - 3])] if bb >= 3 else []
                    lmg = []
                    for g in range(4):
                        lm = S.dma("sync", hrlv[:, hp, 8 * g:8 * (g + 1), :],
                                   h_dram[8 * g:8 * (g + 1), :,
                                          512 * bb:512 * (bb + 1)].rearrange(
                                       "h p n -> p h n"),
                                   shq, wl)
                        lmg.append(lm)
                    for g in range(8):
                        wq_ = [(shq, lmg[3]), ("hs", bc)]
                        def em(nc_, eng, hp=hp, g=g):
                            return eng.tensor_scalar(
                                hscv[:, :, :],
                                hrlv[:, hp, 4 * g:4 * (g + 1), :],
                                col(CRH), C_RND, ALU.mult, ALU.add)
                        S.op("vector", em, incs=[("ve", 1)], waits=wq_)
                        v1 = S.mark("ve")
                        def em(nc_, eng, hp=hp, g=g):
                            return eng.tensor_scalar(
                                hrlv[:, hp, 4 * g:4 * (g + 1), :],
                                hscv[:, :, :], -C_RND, None, ALU.add)
                        S.op("vector", em, incs=[("ve", 1)],
                             waits=[("ve", v1)])
                    qd_done[bb] = S.mark("ve")

                def emit_tiles(bb):
                    hp = bb % 3
                    for j in range(4):
                        tt = 4 * bb + j
                        pop = tt % 4
                        for gs_ in range(4):
                            wt = [("ve", w2q_mark[8 * gs_ + 7])]
                            if gs_ == 0:
                                wt.append(("ve", qd_done[bb]))
                                if po_free[pop] is not None:
                                    wt.append(("ve", po_free[pop]))
                            def em(nc_, eng, hp=hp, j=j, pop=pop, gs_=gs_):
                                r = None
                                for hc in range(8 * gs_, 8 * gs_ + 8):
                                    for hh in range(2):
                                        r = eng.matmul(
                                            PO2[pop][:,
                                                     512 * hh:512 * (hh + 1)],
                                            hrlv[:, hp, hc,
                                                 128 * j:128 * (j + 1)],
                                            w2qv[:, hc,
                                                 512 * hh:512 * (hh + 1)],
                                            start=(hc == 0),
                                            stop=(hc == 31))
                                return r
                            S.op("tensor", em, incs=[("te", 1)], waits=wt)
                        mm = S.mark("te")
                        if j == 3:
                            mm_done[bb] = mm

                        ep = tt % 2
                        sxr = "sxr%d" % ep
                        sy = "sy%d" % ep
                        wx = [] if xl_free[ep] is None else xl_free[ep]
                        lx = S.dma("sync", x1lv[:, ep, :],
                                   x1_dram[128 * tt:128 * (tt + 1), :],
                                   sxr, wx)
                        wstt = [("te", mm), ("ve", B2S_ve)]
                        if xl_free[ep]:
                            wstt += [w for w in xl_free[ep] if w[0] == sy]
                        def em(nc_, eng, ep=ep, pop=pop):
                            return eng.scalar_tensor_tensor(
                                otv[:, ep, :], PO2[pop][:, :], col(CS2),
                                B2S[:, :], ALU.mult, ALU.add)
                        S.op("vector", em, incs=[("ve", 1)], waits=wstt)
                        sttm = S.mark("ve")
                        def em(nc_, eng, ep=ep):
                            return eng.tensor_add(otv[:, ep, :],
                                                  otv[:, ep, :],
                                                  x1lv[:, ep, :])
                        S.op("vector", em, incs=[("ve", 1)],
                             waits=[(sxr, lx), ("ve", sttm)])
                        po_free[pop] = S.mark("ve")
                        def em(nc_, eng, ep=ep):
                            eng.tensor_mul(x1lv[:, ep, :], otv[:, ep, :],
                                           otv[:, ep, :])
                            return eng.reduce_sum(lns[:, :], otv[:, ep, :],
                                                  axis=AX.X)
                        S.op("vector", em, incs=[("ve", 1)],
                             waits=[("ve", po_free[pop])])
                        sqv = S.mark("ve")
                        def em(nc_, eng, ep=ep):
                            return eng.reduce_sum(lnv[:, :], x1lv[:, ep, :],
                                                  axis=AX.X)
                        S.op("vector", em, incs=[("ve", 1)],
                             waits=[("ve", sqv)])
                        summ = S.mark("ve")
                        def em(nc_, eng):
                            eng.tensor_mul(lnq[:, :], lns[:, :], lns[:, :])
                            eng.tensor_scalar(lnm[:, :], lns[:, :], 1.0 / D,
                                              None, ALU.mult)
                            return eng.tensor_scalar(
                                lnq[:, :], lnq[:, :], 1.0 / (D * D), EPS,
                                ALU.mult, ALU.subtract)
                        S.op("vector", em, incs=[("ve", 1)],
                             waits=[("ve", summ)])
                        st1a = S.mark("ve")
                        def em(nc_, eng):
                            return eng.scalar_tensor_tensor(
                                lnv[:, :], lnv[:, :], 1.0 / D, lnq[:, :],
                                ALU.mult, ALU.subtract)
                        S.op("vector", em, incs=[("ve", 1)],
                             waits=[("ve", st1a)])
                        def em(nc_, eng):
                            return eng.activation(lnr[:, :], lnv[:, :],
                                                  AF.Sqrt)
                        S.op("scalar", em, incs=[("sc", 1)],
                             waits=[("ve", S.mark("ve"))])
                        def em(nc_, eng):
                            return eng.reciprocal(lnr[:, :], lnr[:, :])
                        S.op("vector", em, incs=[("ve", 1)],
                             waits=[("sc", S.mark("sc"))])
                        vrc = S.mark("ve")
                        def em(nc_, eng, ep=ep):
                            return eng.tensor_scalar(
                                otv[:, ep, :], otv[:, ep, :], lnm[:, :],
                                lnr[:, :], ALU.subtract, ALU.mult)
                        S.op("vector", em, incs=[("ve", 1)],
                             waits=[("ve", vrc)])
                        vem = S.mark("ve")
                        sym = S.dma("sync", y_d[128 * tt:128 * (tt + 1), :],
                                    otv[:, ep, :], sy, [("ve", vem)])
                        xl_free[ep] = [("ve", vem), (sy, sym)]

                emit_w2(0, 8)
                emit_reload_quant(0)
                emit_w2(8, 16)
                emit_reload_quant(1)
                emit_w2(16, 32)
                emit_reload_quant(2)
                for bb in range(8):
                    emit_tiles(bb)
                    if bb + 3 < 8:
                        emit_reload_quant(bb + 3)
                S.play(blk, entry=bar)

    fin = S.marks()
    with nc.Block() as blk:
        def em(nc_, eng):
            return eng.engine_nop()
        S.op("vector", em, incs=[("ve", 1)])
        S.play(blk, entry=fin)

    es.close()
    return nc, dbgmap


def kernel(**inputs):
    per_core = _host_prep(inputs)
    nc, _ = build_program()
    res = run_bass_kernel_spmd(nc, per_core, core_ids=list(range(NCORE)))
    return np.concatenate(
        [np.asarray(r["y"]).reshape(2, N, D) for r in res.results], axis=0)


# version 18
# speedup vs baseline: 1.0347x; 1.0340x over previous
from contextlib import ExitStack

import numpy as np
import ml_dtypes

import concourse.bass as bass
import concourse.mybir as mybir
from concourse.bass_utils import run_bass_kernel_spmd

F32 = mybir.dt.float32
BF16 = mybir.dt.bfloat16
AF = mybir.ActivationFunctionType
ALU = mybir.AluOpType
AX = mybir.AxisListType

NCORE = 8
N = 2048
D = 1024
NH = 16
DK = 64
HF = 4096
TPC = 2 * N
C_RND = 12582912.0
EPS = 1e-5
INV127 = 1.0 / 127.0

ENGN = ("sync", "tensor", "scalar", "vector", "gpsimd")

SEMNAMES = [
    "te", "sc", "ve", "ge", "cc",
    "sc0",
    "gx0", "gx1",
    "gw",
    "gs",
    "ssw0", "ssw1",
    "sw0", "sw1",
    "sxr0", "sxr1",
    "ssp0", "ssp1",
    "sdn",
    "sbs",
    "sdl", "sdi",
    "srb", "srb2",
    "sh0", "sh1", "sh2", "sh3",
    "shq0", "shq1",
    "sy0", "sy1",
    "sdbg",
]


class Sched:

    def __init__(self, nc, sems):
        self.nc = nc
        self.sems = sems
        self.cnt = {k: 0 for k in sems}
        self.steps = []

    def op(self, engine, emit, incs=(), waits=()):
        for s, v in waits:
            assert v <= self.cnt[s], f"wait {s}>={v} but only {self.cnt[s]}"
        self.steps.append((engine, emit, tuple(incs), tuple(waits)))
        for s, k in incs:
            self.cnt[s] += k

    def dma(self, engine, out, in_, stream, waits=()):
        def em(nc_, eng, out=out, in_=in_):
            return eng.dma_start(out=out, in_=in_)
        self.op(engine, em, incs=[(stream, 16)], waits=waits)
        return self.cnt[stream]

    def mark(self, sem):
        return self.cnt[sem]

    def marks(self):
        return dict(self.cnt)

    def play(self, block, entry=None):
        nc, steps, sems = self.nc, self.steps, self.sems
        for eng_name in ENGN:
            def body(eng, _en=eng_name):
                waited = {}
                if entry:
                    for s, v in entry.items():
                        if v > 0:
                            eng.wait_ge(sems[s], v)
                            waited[s] = v
                for engine, emit, incs, waits in steps:
                    if engine != _en:
                        continue
                    for s, v in waits:
                        if waited.get(s, -1) < v:
                            eng.wait_ge(sems[s], v)
                            waited[s] = v
                    instr = emit(nc, eng)
                    for s, k in incs:
                        instr.then_inc(sems[s], k)
            getattr(block, eng_name)(body)
        self.steps = []


def _rope_perm():
    idx = np.zeros(D, dtype=np.int64)
    for h in range(NH):
        b = h * DK
        idx[b:b + 32] = b + np.arange(0, DK, 2)
        idx[b + 32:b + 64] = b + np.arange(1, DK, 2)
    return idx


def _rope_tables():
    powers = np.arange(0, DK, 2, dtype=np.float64) / DK
    freqs = 1.0 / (10000.0 ** powers)
    t = np.arange(N, dtype=np.float64)
    ang = np.outer(freqs, t)
    cos = np.cos(ang)
    sin = np.sin(ang)
    c2 = np.concatenate([cos, cos, cos, cos], 0).astype(np.float32)
    s2 = np.concatenate([-sin, sin, -sin, sin], 0).astype(np.float32)
    return (c2.astype(ml_dtypes.bfloat16), s2.astype(ml_dtypes.bfloat16))


def _host_prep(inputs):
    perm = _rope_perm()
    wq = np.asarray(inputs["L0_attn_wq"])
    wk = np.asarray(inputs["L0_attn_wk"])
    wv = np.asarray(inputs["L0_attn_wv"])
    wo = np.asarray(inputs["L0_attn_wo"])
    w1 = np.asarray(inputs["L0_ffn_w1"])
    w2 = np.asarray(inputs["L0_ffn_w2"])
    b1 = np.asarray(inputs["L0_ffn_b1"])
    b2 = np.asarray(inputs["L0_ffn_b2"])
    c2, s2 = _rope_tables()
    common = {
        "wq_t": np.ascontiguousarray(wq[perm].T),
        "wk_t": np.ascontiguousarray(wk[perm].T),
        "wv_t": np.ascontiguousarray(wv.T),
        "wo_t": np.ascontiguousarray(wo.T),
        "w1_t": np.ascontiguousarray(w1.T),
        "w2_t": np.ascontiguousarray(w2.T),
        "b1_col": np.ascontiguousarray(b1.reshape(32, 128).T),
        "b2_row": np.ascontiguousarray(b2.reshape(1, D)),
        "tab_c2": c2,
        "tab_s2": s2,
    }
    x = np.asarray(inputs["L0_attn_x"])
    return [dict(common, x=np.ascontiguousarray(
        x[2 * c:2 * c + 2].reshape(TPC, D))) for c in range(NCORE)]


def build_program(debug=False, race=True):
    nc = bass.Bass("TRN2", target_bir_lowering=False,
                   detect_race_conditions=race)

    x_d = nc.dram_tensor("x", [TPC, D], F32, kind="ExternalInput").ap()
    wq_d = nc.dram_tensor("wq_t", [D, D], F32, kind="ExternalInput").ap()
    wk_d = nc.dram_tensor("wk_t", [D, D], F32, kind="ExternalInput").ap()
    wv_d = nc.dram_tensor("wv_t", [D, D], F32, kind="ExternalInput").ap()
    wo_d = nc.dram_tensor("wo_t", [D, D], F32, kind="ExternalInput").ap()
    w1_d = nc.dram_tensor("w1_t", [D, HF], F32, kind="ExternalInput").ap()
    w2_d = nc.dram_tensor("w2_t", [HF, D], F32, kind="ExternalInput").ap()
    b1_d = nc.dram_tensor("b1_col", [128, 32], F32, kind="ExternalInput").ap()
    b2_d = nc.dram_tensor("b2_row", [1, D], F32, kind="ExternalInput").ap()
    c2_d = nc.dram_tensor("tab_c2", [128, N], BF16, kind="ExternalInput").ap()
    s2_d = nc.dram_tensor("tab_s2", [128, N], BF16, kind="ExternalInput").ap()
    y_d = nc.dram_tensor("y", [TPC, D], F32, kind="ExternalOutput").ap()

    scratch_kind = "ExternalOutput" if debug else "Internal"
    x1_dram = nc.dram_tensor("x1_scr", [TPC, D], F32, kind=scratch_kind).ap()
    h_dram = nc.dram_tensor("h_scr", [32, 128, 4096], BF16,
                            kind=scratch_kind).ap()
    den_dram = nc.dram_tensor("den_scr", [2, 8, 4096], F32).ap()
    deni_dram = nc.dram_tensor("deni_scr", [2, 8, 4096], BF16).ap()
    sc_dram = nc.dram_tensor("sc_scr", [64], F32).ap()
    c0_dram = nc.dram_tensor("c0_scr", [64], F32).ap()
    relay_dram = nc.dram_tensor("relay_scr", [1, 128], F32).ap()
    cc1i = nc.dram_tensor("cc1i", [1, 128], F32).ap()
    cc1o = nc.dram_tensor("cc1o", [1, 128], F32).ap()
    cc2i = nc.dram_tensor("cc2i", [1, 128], F32).ap()
    cc2o = nc.dram_tensor("cc2o", [1, 128], F32).ap()

    dbgmap = {}
    if debug:
        for nm, shp, dt in [("qt", [2, 128, 8 * N], BF16),
                            ("kt", [2, 128, 8 * N], BF16),
                            ("v2", [2, 128, 16 * 16 * 65], BF16),
                            ("avt", [2, 128, 8 * N], BF16),
                            ("sc16", [1, 16], F32),
                            ("xmaxd", [128, 1], F32),
                            ("relayd", [1, 128], F32),
                            ("srowd", [1, 32], F32),
                            ("nmxd", [1, 32], F32),
                            ("qt0", [128, 2048], BF16),
                            ("res0", [2, 128, 1024], F32),
                            ("lnd", [2, 128, 8], F32),
                            ("sqd", [2, 128, 1024], F32),
                            ("lnv0", [2, 128, 1], F32)]:
            dbgmap[nm] = nc.dram_tensor("dbg_" + nm, shp, dt,
                                        kind="ExternalOutput").ap()

    es = ExitStack()
    sb = lambda nm, shp, dt: es.enter_context(nc.sbuf_tensor(nm, shp, dt))
    sem = lambda nm: es.enter_context(nc.semaphore(nm))

    C2 = sb("C2", [128, N], BF16)
    S2t = sb("S2t", [128, N], BF16)
    ident = sb("ident", [128, 128], BF16)
    trimask = sb("trimask", [128, 128], BF16)
    maskA = sb("maskA", [128, 128], BF16)
    onescol = sb("onescol", [128, 1], BF16)
    ones2 = sb("ones2", [128, 2], BF16)
    c0t = sb("c0t", [1, 64], F32)
    b1c = sb("b1c", [128, 32], F32)
    b1s = sb("b1s", [128, 32], F32)
    scal = sb("scal", [128, 16], F32)
    srow = sb("srow", [128, 32], F32)
    xmax = sb("xmax", [128, 1], F32)
    hmax = sb("hmax", [128, 1], F32)
    wmax = sb("wmax", [128, 1], F32)
    w1max = sb("w1max", [128, 1], F32)
    w2max = sb("w2max", [128, 1], F32)
    relay = sb("relay", [1, 128], F32)
    lnm = sb("lnm", [128, 1], F32)
    lnv = sb("lnv", [128, 1], F32)
    lnr = sb("lnr", [128, 1], F32)
    lns = sb("lns", [128, 1], F32)
    lnq = sb("lnq", [128, 1], F32)
    rmx = sb("rmx", [128, 1], F32)

    CNC0, CNC1, CRX, CS1, CRH, CS2, CRW1, CRW2 = range(8)

    sems = {k: sem(k) for k in SEMNAMES}
    S = Sched(nc, sems)

    def col(i):
        return scal[:, i:i + 1]

    with nc.Block() as blk:
        S.dma("sync", C2[:, :], c2_d[:, :], "sc0")
        S.dma("sync", S2t[:, :], s2_d[:, :], "sc0")
        S.dma("sync", b1c[:, :], b1_d[:, :], "sc0")

        def em(nc_, eng):
            eng.memset(ident[:, :], 1.0)
            eng.memset(maskA[:, :], 0.0)
            return eng.memset(trimask[:, :], 1.0)
        S.op("vector", em, incs=[("ve", 1)])
        msk_ve = S.mark("ve")

        def em(nc_, eng):
            eng.affine_select(out=ident[:, :], in_=ident[:, :],
                              compare_op=ALU.is_equal, fill=0.0, base=0,
                              pattern=[[-1, 128]], channel_multiplier=1)
            eng.affine_select(out=trimask[:, :], in_=trimask[:, :],
                              compare_op=ALU.is_ge, fill=0.0, base=0,
                              pattern=[[1, 128]], channel_multiplier=-1)
            return eng.affine_select(out=maskA[:, :], in_=maskA[:, :],
                                     compare_op=ALU.is_ge, fill=-1e30,
                                     base=0, pattern=[[-1, 128]],
                                     channel_multiplier=1)
        S.op("gpsimd", em, incs=[("ge", 1)], waits=[("ve", msk_ve)])

        def em(nc_, eng):
            eng.memset(onescol[:, :], 1.0)
            eng.memset(ones2[:, :], 0.0)
            eng.memset(ones2[0:64, 0:1], 1.0)
            eng.memset(ones2[64:128, 1:2], 1.0)
            eng.memset(xmax[:, :], 0.0)
            eng.memset(hmax[:, :], 0.0)
            eng.memset(w1max[:, :], 0.0)
            eng.memset(w2max[:, :], 0.0)
            eng.memset(srow[:, :], 0.0)
            return eng.memset(scal[:, :], 0.0)
        S.op("vector", em, incs=[("ve", 1)])
        S.dma("gpsimd", sc_dram[0:32], srow[0:1, 0:32], "gs",
              waits=[("ve", S.mark("ve"))])
        S.play(blk)

    with ExitStack() as aes:
        asb = lambda nm, shp, dt: aes.enter_context(nc.sbuf_tensor(nm, shp, dt))
        QT = asb("QT", [128, 8 * N], BF16)
        KT = asb("KT", [128, 8 * N], BF16)
        V2 = asb("V2", [128, 16 * 16 * 65], BF16)
        slotA = asb("slotA", [128, 8 * N], BF16)
        Wld = asb("Wld", [128, 8 * 1024], BF16)
        xstage = asb("xstage", [128, 2 * 1024], BF16)
        rsc1 = asb("rsc1", [128, N], BF16)
        rsc2 = asb("rsc2", [128, N], BF16)
        ATst = asb("ATst", [128, 2 * 1024], BF16)
        Bstage = asb("Bstage", [64, N], BF16)
        rb = asb("rb", [128, 2 * N], BF16)
        deni_b = asb("deni_b", [128, 32], BF16)
        nmx = asb("nmx", [2, 32], F32)
        xf32 = asb("xf32", [128, 2 * 1024], F32)
        x1t = asb("x1t", [128, 2 * 1024], F32)
        ATq = asb("ATq", [128, 16 * 128], BF16)

        QTv = QT.ap().rearrange("p (c n) -> p c n", c=8)
        KTv = KT.ap().rearrange("p (c n) -> p c n", c=8)
        V2v = V2.ap().rearrange("p (k s e) -> p k s e", k=16, s=16)
        slotAv = slotA.ap().rearrange("p (c n) -> p c n", c=8)
        Wldv = Wld.ap().rearrange("p (c n) -> p c n", c=8)
        xstv = xstage.ap().rearrange("p (r n) -> p r n", r=2)
        ATv = ATst.ap().rearrange("p (r n) -> p r n", r=2)
        xf32v = xf32.ap().rearrange("p (r n) -> p r n", r=2)
        x1tv = x1t.ap().rearrange("p (r n) -> p r n", r=2)
        wstag = [rsc1.ap().bitcast(F32), rsc2.ap().bitcast(F32)]

        vbar = S.marks()
        with nc.Block() as blk:
            def em(nc_, eng):
                return eng.memset(V2v[:, :, :, 64], 1.0)
            S.op("vector", em, incs=[("ve", 1)])
            S.play(blk, entry=vbar)

        for s_idx in range(2):
            tok0 = s_idx * N
            bar = S.marks()

            with ExitStack() as pes:
                pps = lambda nm, shp, dt: pes.enter_context(
                    nc.psum_tensor(nm + f"_{s_idx}", shp, dt))
                PT = [pps(f"PT{i}", [128, 1024], BF16) for i in range(2)]
                PS = [pps(f"PS{i}", [128, 512], F32) for i in range(2)]
                PN = pps("PN", [2, 2048], F32)
                PTv = [p.ap().rearrange("p (c n) -> p c n", c=8) for p in PT]
                PSh = [p.ap().rearrange("p (h e) -> p h e", h=8) for p in PS]

                with nc.Block() as blk:
                    pt_free = [None, None]
                    xst_free = [None, None]
                    for tt in range(16):
                        xp = tt % 2
                        gx = "gx%d" % xp
                        w = [] if xst_free[xp] is None else [("te", xst_free[xp])]
                        ldm = S.dma("gpsimd", xstv[:, xp, :],
                                    x_d[tok0 + 128 * tt: tok0 + 128 * (tt + 1), :],
                                    gx, w)
                        pp = tt % 2
                        wt = [(gx, ldm)]
                        if pt_free[pp] is not None:
                            wt.append(("sc", pt_free[pp]))
                        def em(nc_, eng, xp=xp, pp=pp):
                            r = None
                            for j in range(8):
                                r = eng.transpose(
                                    PT[pp][:, 128 * j:128 * (j + 1)],
                                    xstv[:, xp, 128 * j:128 * (j + 1)],
                                    ident[:, :])
                            return r
                        S.op("tensor", em, incs=[("te", 1)], waits=wt)
                        xst_free[xp] = S.mark("te")
                        def em(nc_, eng, tt=tt, pp=pp):
                            return eng.activation(
                                slotAv[:, 0:8, 128 * tt:128 * (tt + 1)],
                                PTv[pp], AF.Copy)
                        S.op("scalar", em, incs=[("sc", 1)],
                             waits=[("te", S.mark("te"))])
                        pt_free[pp] = S.mark("sc")
                    XT_sc = S.mark("sc")

                    ps_free = [None] * 2
                    psn = [0]
                    rsc_guard = {"t1": [None, None], "t2": [None, None],
                                 "nm": None}
                    swn = [0]

                    QFs = [rsc1.ap().bitcast(F32), x1tv[:, 0, :]]
                    QsFs = [rsc2.ap().bitcast(F32), x1tv[:, 1, :]]

                    def proj_fm(wd, store_v, qk_idx, ldw_waits):
                        ldw = S.dma("gpsimd", Wldv[:, :, :],
                                    wd.rearrange("(c p) n -> p c n", p=128),
                                    "gw", ldw_waits)
                        add_marks = []
                        for oc in range(8):
                            for h2 in range(2):
                                sl2 = (2 * oc + h2) % 2
                                QF = QFs[sl2]
                                QsF = QsFs[sl2]
                                for q4s in range(2):
                                    q4 = 2 * h2 + q4s
                                    sl = psn[0] % 2
                                    psn[0] += 1
                                    wt = [("gw", ldw), ("sc", XT_sc)]
                                    if ps_free[sl] is not None:
                                        wt.append(("sc", ps_free[sl]))
                                    def em(nc_, eng, oc=oc, q4=q4, sl=sl):
                                        r = None
                                        for dc in range(8):
                                            r = eng.matmul(
                                                PS[sl][:, :],
                                                Wldv[:, dc,
                                                     128 * oc:128 * (oc + 1)],
                                                slotAv[:, dc,
                                                       512 * q4:512 * (q4 + 1)],
                                                start=(dc == 0),
                                                stop=(dc == 7))
                                        return r
                                    S.op("tensor", em, incs=[("te", 1)],
                                         waits=wt)
                                    tm = S.mark("te")
                                    evw = [("te", tm)]
                                    if rsc_guard["t1"][sl2] is not None:
                                        evw.append(
                                            ("ve", rsc_guard["t1"][sl2]))
                                    def em(nc_, eng, q4s=q4s, sl=sl, QF=QF):
                                        return eng.activation(
                                            QF[:, 512 * q4s:512 * (q4s + 1)],
                                            PS[sl][:, :], AF.Copy)
                                    S.op("scalar", em, incs=[("sc", 1)],
                                         waits=evw)
                                    ps_free[sl] = S.mark("sc")
                                evm = S.mark("sc")
                                ssw = "ssw%d" % sl2
                                swn[0] += 1
                                sw_wt = [("sc", evm)]
                                if rsc_guard["t2"][sl2] is not None:
                                    sw_wt.append(("ve", rsc_guard["t2"][sl2]))
                                swm = None
                                for (od, os_) in ((0, 32), (32, 0), (64, 96),
                                                  (96, 64)):
                                    swm = S.dma("sync", QsF[od:od + 32, :],
                                                QF[os_:os_ + 32, :],
                                                ssw, sw_wt)
                                    sw_wt = []
                                tok0h = 1024 * h2
                                def em(nc_, eng, tok0h=tok0h, QF=QF):
                                    return eng.tensor_mul(
                                        QF[:, :], QF[:, :],
                                        C2[:, tok0h:tok0h + 1024])
                                S.op("vector", em, incs=[("ve", 1)],
                                     waits=[(ssw, swm)])
                                def em(nc_, eng, tok0h=tok0h, QsF=QsF):
                                    return eng.tensor_mul(
                                        QsF[:, :], QsF[:, :],
                                        S2t[:, tok0h:tok0h + 1024])
                                S.op("vector", em, incs=[("ve", 1)])
                                rsc_guard["t2"][sl2] = S.mark("ve")
                                def em(nc_, eng, oc=oc, h2=h2,
                                       store_v=store_v, QF=QF, QsF=QsF):
                                    return eng.tensor_add(
                                        store_v[:, oc,
                                                1024 * h2:1024 * (h2 + 1)],
                                        QF[:, :], QsF[:, :])
                                S.op("vector", em, incs=[("ve", 1)],
                                     waits=[("ve", rsc_guard["t2"][sl2])])
                                rsc_guard["t1"][sl2] = S.mark("ve")
                            add_marks.append(S.mark("ve"))
                        proj_te = S.mark("te")
                        for oc in range(8):
                            sq_wt = [("ve", add_marks[oc])]
                            if rsc_guard["nm"] is not None:
                                sq_wt.append(("te", rsc_guard["nm"]))
                            def em(nc_, eng, oc=oc, store_v=store_v):
                                return eng.tensor_mul(ATst[:, :],
                                                      store_v[:, oc, :],
                                                      store_v[:, oc, :])
                            S.op("vector", em, incs=[("ve", 1)], waits=sq_wt)
                            sqm = ("ve", S.mark("ve"))
                            def em(nc_, eng):
                                r = None
                                for q4 in range(4):
                                    r = eng.matmul(
                                        PN[:, 512 * q4:512 * (q4 + 1)],
                                        ones2[:, :],
                                        ATst[:, 512 * q4:512 * (q4 + 1)],
                                        start=True, stop=True)
                                return r
                            S.op("tensor", em, incs=[("te", 1)],
                                 waits=[sqm])
                            nm_ = S.mark("te")
                            rsc_guard["nm"] = nm_
                            cc_ = 8 * qk_idx + oc
                            def em(nc_, eng, cc_=cc_):
                                return eng.reduce_max(nmx[0:2, cc_:cc_ + 1],
                                                      PN[0:2, :], axis=AX.X)
                            S.op("vector", em, incs=[("ve", 1)],
                                 waits=[("te", nm_)])
                        return proj_te

                    qte = proj_fm(wq_d, QTv, 0, [])
                    kte = proj_fm(wk_d, KTv, 1, [("te", qte)])
                    ROPE_ve = S.mark("ve")

                    ldw = S.dma("gpsimd", Wldv[:, :, :],
                                wv_d.rearrange("(c p) n -> p c n", p=128),
                                "gw", [("te", kte)])
                    for tt in range(16):
                        for hh in range(2):
                            sl = psn[0] % 2
                            psn[0] += 1
                            wt = [("gw", ldw), ("sc", XT_sc)]
                            if ps_free[sl] is not None:
                                wt.append(("sc", ps_free[sl]))
                            def em(nc_, eng, tt=tt, hh=hh, sl=sl):
                                r = None
                                for dc in range(8):
                                    r = eng.matmul(
                                        PS[sl][:, :],
                                        slotAv[:, dc, 128 * tt:128 * (tt + 1)],
                                        Wldv[:, dc, 512 * hh:512 * (hh + 1)],
                                        start=(dc == 0), stop=(dc == 7))
                                return r
                            S.op("tensor", em, incs=[("te", 1)], waits=wt)
                            tm = S.mark("te")
                            def em(nc_, eng, tt=tt, hh=hh, sl=sl):
                                return eng.activation(
                                    V2v[:, tt, 8 * hh:8 * (hh + 1), 0:64],
                                    PSh[sl], AF.Copy)
                            S.op("scalar", em, incs=[("sc", 1)],
                                 waits=[("te", tm)])
                            ps_free[sl] = S.mark("sc")

                    r1 = S.dma("gpsimd", c0_dram[0:32], nmx[0:2, 0:16],
                               "gs", [("ve", ROPE_ve)])
                    r2 = S.dma("gpsimd", c0t[0:1, 0:32],
                               c0_dram[0:32].unsqueeze(0), "gs",
                               [("gs", r1)])
                    def em(nc_, eng):
                        eng.tensor_mul(c0t[0:1, 32:40], c0t[0:1, 0:8],
                                       c0t[0:1, 8:16])
                        return eng.tensor_mul(c0t[0:1, 40:48],
                                              c0t[0:1, 16:24],
                                              c0t[0:1, 24:32])
                    S.op("vector", em, incs=[("ve", 1)], waits=[("gs", r2)])
                    rm = S.mark("ve")
                    def em(nc_, eng):
                        eng.tensor_scalar(c0t[0:1, 52:60], c0t[0:1, 32:40],
                                          1.0, None, ALU.mult)
                        return eng.tensor_scalar(c0t[0:1, 52:60],
                                                 c0t[0:1, 40:48], 1.0, None,
                                                 ALU.max)
                    S.op("vector", em, incs=[("ve", 1)], waits=[("ve", rm)])
                    rm = S.mark("ve")
                    def em(nc_, eng):
                        return eng.reduce_max(c0t[0:1, 48:49],
                                              c0t[0:1, 32:48], axis=AX.X)
                    S.op("vector", em, incs=[("ve", 1)], waits=[("ve", rm)])
                    def em(nc_, eng):
                        return eng.activation(c0t[0:1, 49:50],
                                              c0t[0:1, 48:49], AF.Sqrt)
                    S.op("scalar", em, incs=[("sc", 1)],
                         waits=[("ve", S.mark("ve"))])
                    def em(nc_, eng):
                        return eng.tensor_scalar(c0t[0:1, 50:51],
                                                 c0t[0:1, 49:50],
                                                 -0.125, None, ALU.mult)
                    S.op("vector", em, incs=[("ve", 1)],
                         waits=[("sc", S.mark("sc"))])
                    wm = S.dma("gpsimd", sc_dram[s_idx:s_idx + 1],
                               c0t[0:1, 50:51], "gs",
                               [("ve", S.mark("ve"))])
                    S.dma("gpsimd", col(s_idx),
                          sc_dram[s_idx:s_idx + 1].unsqueeze(0)
                          .partition_broadcast(128),
                          "gs", [("gs", wm)])

                    sp_ps = None
                    sp_pt = None
                    for hp in range(8):
                        wt = [("ve", ROPE_ve)]
                        if sp_ps is not None:
                            wt.append(("sc", sp_ps))
                        else:
                            for f in ps_free:
                                if f is not None:
                                    wt.append(("sc", f))
                        def em(nc_, eng, hp=hp):
                            eng.matmul(PS[0][:, 0:128],
                                       QTv[0:64, hp, 0:128],
                                       KTv[0:64, hp, 0:128],
                                       start=True, stop=True,
                                       tile_position=(0, 0))
                            return eng.matmul(PS[1][:, 0:128],
                                             QTv[64:128, hp, 0:128],
                                             KTv[64:128, hp, 0:128],
                                             start=True, stop=True,
                                             tile_position=(64, 0))
                        S.op("tensor", em, incs=[("te", 1)], waits=wt)
                        spm_te = S.mark("te")
                        def em(nc_, eng):
                            eng.tensor_add(PS[0][:, 0:128], PS[0][:, 0:128],
                                           maskA[:, :])
                            return eng.tensor_add(PS[1][:, 0:128],
                                                  PS[1][:, 0:128],
                                                  maskA[:, :])
                        S.op("vector", em, incs=[("ve", 1)],
                             waits=[("te", spm_te)])
                        mka = S.mark("ve")
                        def em(nc_, eng):
                            eng.reduce_max(lns[:, :], PS[0][:, 0:128],
                                           axis=AX.X, negate=True)
                            return eng.reduce_max(lnv[:, :], PS[1][:, 0:128],
                                                  axis=AX.X, negate=True)
                        S.op("vector", em, incs=[("ve", 1)],
                             waits=[("ve", mka)])
                        rdm = S.mark("ve")
                        def em(nc_, eng):
                            eng.tensor_scalar(lnm[:, :], lns[:, :], 0.125,
                                              None, ALU.mult)
                            return eng.tensor_scalar(lnr[:, :], lnv[:, :],
                                                     0.125, None, ALU.mult)
                        S.op("vector", em, incs=[("ve", 1)],
                             waits=[("ve", rdm)])
                        bsm = S.mark("ve")
                        def em(nc_, eng):
                            eng.activation(rsc1[:, 0:128], PS[0][:, 0:128],
                                           AF.Exp, bias=lnm[:, :],
                                           scale=0.125)
                            return eng.activation(rsc1[:, 128:256],
                                                  PS[1][:, 0:128],
                                                  AF.Exp, bias=lnr[:, :],
                                                  scale=0.125)
                        S.op("scalar", em, incs=[("sc", 1)],
                             waits=[("ve", bsm)])
                        sp_ps = S.mark("sc")
                        wtt = [("sc", sp_ps)]
                        if sp_pt is not None:
                            wtt.append(("sc", sp_pt))
                        else:
                            for f in pt_free:
                                if f is not None:
                                    wtt.append(("sc", f))
                        def em(nc_, eng):
                            eng.transpose(PT[0][:, 0:128], rsc1[:, 0:128],
                                          ident[:, :])
                            return eng.transpose(PT[1][:, 0:128],
                                                 rsc1[:, 128:256],
                                                 ident[:, :])
                        S.op("tensor", em, incs=[("te", 1)], waits=wtt)
                        trm = S.mark("te")
                        def em(nc_, eng, hp=hp):
                            eng.activation(
                                ATq[:, 256 * hp:256 * hp + 128],
                                PT[0][:, 0:128], AF.Copy)
                            return eng.activation(
                                ATq[:, 256 * hp + 128:256 * hp + 256],
                                PT[1][:, 0:128], AF.Copy)
                        S.op("scalar", em, incs=[("sc", 1)],
                             waits=[("te", trm)])
                        sp_pt = S.mark("sc")
                    S.play(blk, entry=bar)
            C0_gs = S.mark("gs")

            if debug:
                dbar = S.marks()
                with nc.Block() as blk:
                    S.dma("sync", dbgmap["qt"][s_idx], QT[:, :], "sdbg")
                    S.dma("sync", dbgmap["kt"][s_idx], KT[:, :], "sdbg")
                    S.dma("sync", dbgmap["v2"][s_idx], V2[:, :], "sdbg")
                    if s_idx == 0:
                        S.dma("sync", dbgmap["nmxd"][0:1, :],
                              nmx[0:2, 0:16], "sdbg")
                    S.play(blk, entry=dbar)

            bar = S.marks()
            with ExitStack() as pes:
                pps = lambda nm, shp, dt: pes.enter_context(
                    nc.psum_tensor(nm + f"_{s_idx}", shp, dt))
                PSC = [pps(f"PSC{i}", [128, 1024], F32) for i in range(2)]
                PAV = [pps(f"PAV{i}", [65, 512], F32) for i in range(2)]
                PO = pps("PO", [128, 1024], F32)
                PAVB = [[PAV[0].ap(), PAV[1].ap()],
                        [PO.ap()[0:65, 0:512], PO.ap()[0:65, 512:1024]]]
                PSCh = [p.ap().rearrange("p (h e) -> p h e", h=2) for p in PSC]

                with nc.Block() as blk:
                    ldwo = S.dma("gpsimd", Wldv[:, :, :],
                                 wo_d.rearrange("(c p) n -> p c n", p=128),
                                 "gw")

                    wd_stream = w1_d if s_idx == 0 else w2_d
                    wtgt = w1max if s_idx == 0 else w2max
                    wflat = wd_stream.rearrange("a b -> (a b)")
                    gfree = [None, None]
                    for t in range(32):
                        sl = t % 2
                        sw = "sw%d" % sl
                        wt = [] if gfree[sl] is None else [("ve", gfree[sl])]
                        lm = S.dma("sync", wstag[sl][:, :],
                                   wflat[131072 * t:131072 * (t + 1)]
                                   .rearrange("(p n) -> p n", p=128), sw, wt)
                        def em(nc_, eng, sl=sl):
                            return eng.reduce_max(wmax[:, 0:1],
                                                  wstag[sl][:, :],
                                                  axis=AX.X,
                                                  apply_absolute_value=True)
                        S.op("vector", em, incs=[("ve", 1)],
                             waits=[(sw, lm)])
                        rm = S.mark("ve")
                        def em(nc_, eng, wtgt=wtgt):
                            return eng.tensor_max(wtgt[:, :], wtgt[:, :],
                                                  wmax[:, 0:1])
                        S.op("vector", em, incs=[("ve", 1)],
                             waits=[("ve", rm)])
                        gfree[sl] = S.mark("ve")

                    psc_free = [None, None]
                    at_free = [None, None]
                    av_free2 = [None, None]
                    den_freeA = None
                    den_freeB = None
                    bst_free = None
                    den_recm = {}
                    den_std = {}
                    nrm_marks = {}
                    rbv = rb.ap().rearrange("p (r n) -> p r n", r=2)
                    atn = [0]
                    pend_den = [None]
                    pend_norm = []
                    norm_delay = [0]

                    def emit_den(hp, bstm):
                        wld = [("sdn", S.mark("sdn"))]
                        if hp >= 1:
                            wld.append(("ve", den_recm[hp - 1]))
                        ldd = S.dma("sync",
                                    xf32v[:, 0, 32 * hp:32 * (hp + 1)],
                                    den_dram[s_idx, hp].rearrange(
                                        "(p n) -> p n", p=128),
                                    "sdl", wld)
                        def em(nc_, eng, hp=hp):
                            return eng.reciprocal(
                                xf32v[:, 1, 32 * hp:32 * (hp + 1)],
                                xf32v[:, 0, 32 * hp:32 * (hp + 1)])
                        S.op("vector", em, incs=[("ve", 1)],
                             waits=[("sdl", ldd)])
                        rcp = S.mark("ve")
                        wcv = [("ve", rcp)]
                        if hp >= 1:
                            wcv.append(("sdi", den_std[hp - 1]))
                        def em(nc_, eng, hp=hp):
                            return eng.tensor_scalar(
                                deni_b[:, :],
                                xf32v[:, 1, 32 * hp:32 * (hp + 1)],
                                1.0, None, ALU.mult)
                        S.op("vector", em, incs=[("ve", 1)], waits=wcv)
                        den_recm[hp] = S.mark("ve")
                        std = S.dma("sync",
                                    deni_dram[s_idx, hp].rearrange(
                                        "(p n) -> p n", p=128),
                                    deni_b[:, :],
                                    "sdi", [("ve", den_recm[hp])])
                        den_std[hp] = std
                        sl = hp % 2
                        srbs = "srb" if sl == 0 else "srb2"
                        wtb = [("sdi", std), ("sbs", bstm)]
                        if hp >= 2:
                            wtb.append(("ve", nrm_marks[hp - 2]))
                        S.dma("sync", rbv[0:64, sl, :],
                              deni_dram[s_idx, hp, 0:2048].unsqueeze(0)
                              .partition_broadcast(64),
                              srbs, wtb)
                        bB = S.dma("sync", rbv[64:128, sl, :],
                                   deni_dram[s_idx, hp, 2048:4096]
                                   .unsqueeze(0).partition_broadcast(64),
                                   srbs)
                        for pc in range(4):
                            pend_norm.append((hp, sl, pc, srbs, bB))
                        norm_delay[0] = 6

                    def emit_norm_piece():
                        hp, sl, pc, srbs, bB = pend_norm.pop(0)
                        def em(nc_, eng, hp=hp, sl=sl, pc=pc):
                            return eng.tensor_mul(
                                slotAv[:, hp, 512 * pc:512 * (pc + 1)],
                                slotAv[:, hp, 512 * pc:512 * (pc + 1)],
                                rbv[:, sl, 512 * pc:512 * (pc + 1)])
                        S.op("vector", em, incs=[("ve", 1)],
                             waits=[(srbs, bB)])
                        if pc == 3:
                            nrm_marks[hp] = S.mark("ve")

                    for hp in range(8):
                        for qc in range(4):
                            first = True
                            pend_av = None
                            if qc == 0:
                                wavs = []
                                if av_free2[0] is not None:
                                    wavs.append(("ve", av_free2[0]))
                                def em(nc_, eng, hp=hp):
                                    r = None
                                    for hh in (0, 1):
                                        r = eng.matmul(
                                            PAVB[0][hh][:, 0:128],
                                            V2v[:, 0, 2 * hp + hh, :],
                                            ATq[:, 256 * hp + 128 * hh:
                                                256 * hp + 128 * hh + 128],
                                            start=True, stop=False)
                                    return r
                                S.op("tensor", em, incs=[("te", 1)],
                                     waits=wavs)
                                first = False
                            for kt in range(4 * qc + 4):
                                a = max(512 * qc, 128 * kt, 128)
                                w = 512 * (qc + 1) - a
                                pf = atn[0] % 2
                                ap_ = atn[0] % 2
                                atn[0] += 1
                                wt = [("gs", C0_gs)]
                                if psc_free[pf] is not None:
                                    wt.append(("sc", psc_free[pf]))
                                def em(nc_, eng, hp=hp, kt=kt, a=a, w=w,
                                       pf=pf):
                                    eng.matmul(
                                        PSC[pf][:, 0:w],
                                        KTv[0:64, hp, 128 * kt:128 * (kt + 1)],
                                        QTv[0:64, hp, a:a + w],
                                        start=True, stop=True,
                                        tile_position=(0, 0))
                                    return eng.matmul(
                                        PSC[pf][:, 512:512 + w],
                                        KTv[64:128, hp,
                                            128 * kt:128 * (kt + 1)],
                                        QTv[64:128, hp, a:a + w],
                                        start=True, stop=True,
                                        tile_position=(64, 0))
                                S.op("tensor", em, incs=[("te", 1)], waits=wt)
                                qkm = S.mark("te")
                                wte = [("te", qkm)]
                                if at_free[ap_] is not None:
                                    wte.append(("te", at_free[ap_]))
                                def em(nc_, eng, w=w, pf=pf, ap_=ap_,
                                       s_idx=s_idx):
                                    return eng.activation(
                                        ATv[:, ap_, :].rearrange(
                                            "p (h e) -> p h e", h=2)[:, :, 0:w],
                                        PSCh[pf][:, :, 0:w],
                                        AF.Exp, bias=col(s_idx), scale=0.125)
                                S.op("scalar", em, incs=[("sc", 1)], waits=wte)
                                psc_free[pf] = S.mark("sc")
                                expm = S.mark("sc")
                                mkm = None
                                if a == 128 * kt:
                                    def em(nc_, eng, ap_=ap_):
                                        eng.tensor_mul(ATv[:, ap_, 0:128],
                                                       ATv[:, ap_, 0:128],
                                                       trimask[:, :])
                                        return eng.tensor_mul(
                                            ATv[:, ap_, 512:640],
                                            ATv[:, ap_, 512:640],
                                            trimask[:, :])
                                    S.op("vector", em, incs=[("ve", 1)],
                                         waits=[("sc", expm)])
                                    mkm = S.mark("ve")
                                wav = [("sc", expm)]
                                if mkm is not None:
                                    wav.append(("ve", mkm))
                                if first and av_free2[qc % 2] is not None:
                                    wav.append(("ve", av_free2[qc % 2]))
                                def em(nc_, eng, hp=hp, kt=kt, a=a, w=w,
                                       qc=qc, ap_=ap_, first=first):
                                    off = a - 512 * qc
                                    last = (kt == 4 * qc + 3)
                                    r = None
                                    for hh in (0, 1):
                                        r = eng.matmul(
                                            PAVB[qc % 2][hh][:, off:off + w],
                                            V2v[:, kt, 2 * hp + hh, :],
                                            ATv[:, ap_,
                                                512 * hh:512 * hh + w],
                                            start=first, stop=last)
                                    return r
                                if pend_av is not None:
                                    pw, pap = pend_av
                                    S.op("tensor", pw, incs=[("te", 1)],
                                         waits=pap)
                                    at_free[1 - ap_] = S.mark("te")
                                pend_av = (em, wav)
                                first = False
                                norm_delay[0] -= 1
                                if pend_norm and norm_delay[0] <= 0:
                                    emit_norm_piece()
                            pw, pap = pend_av
                            S.op("tensor", pw, incs=[("te", 1)], waits=pap)
                            at_free[(atn[0] - 1) % 2] = S.mark("te")
                            avm = S.mark("te")
                            wtA = [("te", avm)]
                            if den_freeA is not None:
                                wtA.append(("sdn", den_freeA))
                            def em(nc_, eng, hp=hp, qc=qc):
                                eng.tensor_copy(xf32.ap()[64:65, 256:768],
                                                PAVB[qc % 2][0][64:65, :])
                                return eng.tensor_copy(
                                    slotAv[0:64, hp, 512 * qc:512 * (qc + 1)],
                                    PAVB[qc % 2][0][0:64, :])
                            S.op("vector", em, incs=[("ve", 1)], waits=wtA)
                            dmA = S.dma("sync",
                                        den_dram[s_idx, hp,
                                                 512 * qc:512 * (qc + 1)],
                                        xf32.ap()[64:65, 256:768], "sdn",
                                        [("ve", S.mark("ve"))])
                            den_freeA = dmA
                            wtB = []
                            if den_freeB is not None:
                                wtB.append(("sdn", den_freeB))
                            if bst_free is not None and qc == 0:
                                wtB.append(("sbs", bst_free))
                            def em(nc_, eng, qc=qc):
                                eng.tensor_copy(xf32.ap()[64:65, 1280:1792],
                                                PAVB[qc % 2][1][64:65, :])
                                return eng.tensor_copy(
                                    Bstage[0:64, 512 * qc:512 * (qc + 1)],
                                    PAVB[qc % 2][1][0:64, :])
                            S.op("vector", em, incs=[("ve", 1)], waits=wtB)
                            dmB = S.dma("sync",
                                        den_dram[s_idx, hp,
                                                 2048 + 512 * qc:
                                                 2048 + 512 * (qc + 1)],
                                        xf32.ap()[64:65, 1280:1792], "sdn",
                                        [("ve", S.mark("ve"))])
                            den_freeB = dmB
                            av_free2[qc % 2] = S.mark("ve")
                            if qc == 0 and pend_den[0] is not None:
                                emit_den(*pend_den[0])
                                pend_den[0] = None
                        bst_free = S.dma("sync", slotAv[64:128, hp, :],
                                         Bstage[0:64, :], "sbs",
                                         [("ve", S.mark("ve"))])
                        pend_den[0] = (hp, bst_free)
                    emit_den(*pend_den[0])
                    pend_den[0] = None
                    while pend_norm:
                        emit_norm_piece()
                    NRM_ve = S.mark("ve")
                    recm = den_recm[7]
                    std = den_std[7]

                    if debug:
                        S.dma("sync", dbgmap["avt"][s_idx], slotA[:, :],
                              "sdbg", [("ve", NRM_ve)])

                    po_free = None
                    sdn_fin = S.mark("sdn")
                    x_free = [[("ve", recm), ("sdn", sdn_fin)],
                              [("ve", recm), ("sdi", std), ("sdn", sdn_fin)]]
                    for tt in range(16):
                        xp = tt % 2
                        sxr = "sxr%d" % xp
                        ssp = "ssp%d" % xp
                        lx = S.dma("sync", xf32v[:, xp, :],
                                   x_d[tok0 + 128 * tt:
                                       tok0 + 128 * (tt + 1), :],
                                   sxr, x_free[xp])
                        wt = [("ve", NRM_ve), ("gw", ldwo)]
                        if po_free is not None:
                            wt.append(("ve", po_free))
                        def em(nc_, eng, tt=tt):
                            r = None
                            for hh in range(2):
                                for dc in range(8):
                                    r = eng.matmul(
                                        PO[:, 512 * hh:512 * (hh + 1)],
                                        slotAv[:, dc, 128 * tt:128 * (tt + 1)],
                                        Wldv[:, dc, 512 * hh:512 * (hh + 1)],
                                        start=(dc == 0), stop=(dc == 7))
                            return r
                        S.op("tensor", em, incs=[("te", 1)], waits=wt)
                        opm = S.mark("te")
                        def em(nc_, eng, xp=xp):
                            return eng.tensor_add(x1tv[:, xp, :], PO[:, :],
                                                  xf32v[:, xp, :])
                        S.op("vector", em, incs=[("ve", 1)],
                             waits=[("te", opm), (sxr, lx)])
                        po_free = S.mark("ve")
                        dbg_gate = [("ve", po_free)]
                        if debug and tt == 0:
                            dm_ = S.dma("sync", dbgmap["res0"][s_idx],
                                        x1tv[:, xp, :], "sdbg",
                                        [("ve", po_free)])
                            dbg_gate = [("sdbg", dm_)]
                        def em(nc_, eng, xp=xp):
                            return eng.activation(xf32v[:, xp, :],
                                                  x1tv[:, xp, :], AF.Square)
                        S.op("scalar", em, incs=[("sc", 1)], waits=dbg_gate)
                        sqsc = S.mark("sc")
                        def em(nc_, eng, xp=xp):
                            return eng.reduce_sum(lns[:, :], x1tv[:, xp, :],
                                                  axis=AX.X)
                        S.op("vector", em, incs=[("ve", 1)], waits=dbg_gate)
                        sqv = S.mark("ve")
                        def em(nc_, eng, xp=xp):
                            return eng.reduce_sum(lnv[:, :], xf32v[:, xp, :],
                                                  axis=AX.X)
                        S.op("vector", em, incs=[("ve", 1)],
                             waits=[("ve", sqv), ("sc", sqsc)])
                        summ = S.mark("ve")
                        sqm = summ
                        if debug and tt == 0:
                            dm2_ = S.dma("sync", dbgmap["sqd"][s_idx],
                                         xf32v[:, xp, :], "sdbg",
                                         [("ve", summ)])
                            def em(nc_, eng):
                                return eng.tensor_scalar(srow[:, 31:32],
                                                         lnv[:, :], 1.0,
                                                         None, ALU.mult)
                            S.op("vector", em, incs=[("ve", 1)],
                                 waits=[("ve", summ)])
                            dm3_ = S.dma("sync", dbgmap["lnv0"][s_idx],
                                         srow[:, 31:32], "sdbg",
                                         [("ve", S.mark("ve"))])
                            summ = S.mark("ve")
                        def em(nc_, eng):
                            eng.tensor_mul(lnq[:, :], lns[:, :], lns[:, :])
                            eng.tensor_scalar(lnm[:, :], lns[:, :], 1.0 / D,
                                              None, ALU.mult)
                            return eng.tensor_scalar(
                                lnq[:, :], lnq[:, :], 1.0 / (D * D), EPS,
                                ALU.mult, ALU.subtract)
                        S.op("vector", em, incs=[("ve", 1)],
                             waits=[("ve", summ)])
                        st1a = S.mark("ve")
                        def em(nc_, eng):
                            return eng.scalar_tensor_tensor(
                                lnv[:, :], lnv[:, :], 1.0 / D, lnq[:, :],
                                ALU.mult, ALU.subtract)
                        S.op("vector", em, incs=[("ve", 1)],
                             waits=[("ve", st1a)])
                        st1m_ = S.mark("ve")
                        sq_gate = [("ve", st1m_)]
                        if debug and tt == 0:
                            def em(nc_, eng, s_idx=s_idx):
                                eng.tensor_scalar(srow[:, 28:29], lns[:, :],
                                                  1.0, None, ALU.mult)
                                eng.tensor_scalar(srow[:, 29:30], lnm[:, :],
                                                  1.0, None, ALU.mult)
                                return eng.tensor_scalar(srow[:, 30:31],
                                                         lnv[:, :], 1.0,
                                                         None, ALU.mult)
                            S.op("vector", em, incs=[("ve", 1)],
                                 waits=[("ve", st1m_)])
                            dm_ = S.dma("sync", dbgmap["lnd"][s_idx][:, 0:3],
                                        srow[:, 28:31], "sdbg",
                                        [("ve", S.mark("ve"))])
                            sq_gate = [("sdbg", dm_)]
                        def em(nc_, eng):
                            return eng.activation(lnr[:, :], lnv[:, :],
                                                  AF.Sqrt)
                        S.op("scalar", em, incs=[("sc", 1)],
                             waits=sq_gate)
                        def em(nc_, eng):
                            return eng.reciprocal(lnr[:, :], lnr[:, :])
                        S.op("vector", em, incs=[("ve", 1)],
                             waits=[("sc", S.mark("sc"))])
                        vrc = S.mark("ve")
                        def em(nc_, eng, xp=xp):
                            return eng.tensor_scalar(
                                x1tv[:, xp, :], x1tv[:, xp, :], lnm[:, :],
                                lnr[:, :], ALU.subtract, ALU.mult)
                        S.op("vector", em, incs=[("ve", 1)],
                             waits=[("ve", vrc)])
                        def em(nc_, eng, xp=xp):
                            return eng.reduce_max(rmx[:, :], x1tv[:, xp, :],
                                                  axis=AX.X,
                                                  apply_absolute_value=True)
                        S.op("vector", em, incs=[("ve", 1)])
                        rm = S.mark("ve")
                        def em(nc_, eng):
                            return eng.tensor_max(xmax[:, :], xmax[:, :],
                                                  rmx[:, :])
                        S.op("vector", em, incs=[("ve", 1)],
                             waits=[("ve", rm)])
                        vem = S.mark("ve")
                        spm = S.dma("sync",
                                    x1_dram[tok0 + 128 * tt:
                                            tok0 + 128 * (tt + 1), :],
                                    x1tv[:, xp, :], ssp, [("ve", vem)])
                        x_free[xp] = [("ve", vem), (ssp, spm)]
                    S.play(blk, entry=bar)

    bar = S.marks()
    with ExitStack() as fes:
        fsb = lambda nm, shp, dt: fes.enter_context(
            nc.sbuf_tensor(nm, shp, dt))
        w1qt = fsb("w1qt", [128, 8 * HF], BF16)
        wqst = fsb("wqst", [128, 2 * HF], F32)
        x1qs = fsb("x1qs", [128, 2 * 8 * 512], BF16)
        xld = fsb("xldC", [128, 2 * 1024], F32)
        xqb = fsb("xqbC", [128, 2 * 1024], BF16)
        hst = fsb("hstC", [128, 4 * 512], BF16)
        w1qv = w1qt.ap().rearrange("p (c n) -> p c n", c=8)
        wqsv = wqst.ap().rearrange("p (r n) -> p r n", r=2)
        x1qv = x1qs.ap().rearrange("p (g c n) -> p g c n", g=2, c=8)
        xldv = xld.ap().rearrange("p (r n) -> p r n", r=2)
        xqbv = xqb.ap().rearrange("p (r n) -> p r n", r=2)
        hstv = hst.ap().rearrange("p (r n) -> p r n", r=4)

        with ExitStack() as pes:
            pps = lambda nm, shp, dt: pes.enter_context(
                nc.psum_tensor(nm, shp, dt))
            PT = [pps(f"PTC{i}", [128, 1024], BF16) for i in range(2)]
            PH = [pps(f"PH{i}", [128, 512], F32) for i in range(4)]
            PTv = [p.ap().rearrange("p (c n) -> p c n", c=8) for p in PT]

            with nc.Block() as blk:
                m1 = S.dma("gpsimd", cc1i[0:1, :], xmax[:, 0], "gs")
                def em(nc_, eng):
                    return eng.collective_compute(
                        "AllReduce", ALU.max, replica_groups=[list(range(NCORE))],
                        ins=[cc1i], outs=[cc1o])
                S.op("gpsimd", em, incs=[("cc", 1)], waits=[("gs", m1)])
                m3 = S.dma("gpsimd", relay_dram[0, :], w1max[:, 0], "gs")
                m4 = S.dma("gpsimd", relay[0:1, :], relay_dram[0:1, :], "gs",
                           [("gs", m3)])
                def em(nc_, eng):
                    return eng.reduce_max(srow[0:1, 9:10], relay[0:1, :], axis=AX.X)
                S.op("vector", em, incs=[("ve", 1)], waits=[("gs", m4)])
                vw1 = S.mark("ve")
                m5 = S.dma("gpsimd", relay_dram[0, :], w2max[:, 0], "gs",
                           [("ve", vw1)])
                m6 = S.dma("gpsimd", relay[0:1, :], relay_dram[0:1, :], "gs",
                           [("gs", m5)])
                def em(nc_, eng):
                    return eng.reduce_max(srow[0:1, 10:11], relay[0:1, :], axis=AX.X)
                S.op("vector", em, incs=[("ve", 1)], waits=[("gs", m6)])
                vw2 = S.mark("ve")
                m2 = S.dma("gpsimd", relay[0:1, :], cc1o[0:1, :], "gs",
                           [("cc", S.mark("cc")), ("ve", vw2)])
                def em(nc_, eng):
                    return eng.reduce_max(srow[0:1, 8:9], relay[0:1, :], axis=AX.X)
                S.op("vector", em, incs=[("ve", 1)], waits=[("gs", m2)])
                rm = S.mark("ve")
                def em(nc_, eng):
                    eng.tensor_scalar(srow[0:1, 11:12], srow[0:1, 8:9], 1e-6, None,
                                      ALU.max)
                    eng.tensor_scalar(srow[0:1, 12:13], srow[0:1, 9:10], 1e-6, None,
                                      ALU.max)
                    return eng.tensor_scalar(srow[0:1, 13:14], srow[0:1, 10:11],
                                             1e-6, None, ALU.max)
                S.op("vector", em, incs=[("ve", 1)], waits=[("ve", rm)])
                rm = S.mark("ve")
                def em(nc_, eng):
                    eng.reciprocal(srow[0:1, 14:15], srow[0:1, 11:12])
                    eng.reciprocal(srow[0:1, 15:16], srow[0:1, 12:13])
                    return eng.reciprocal(srow[0:1, 16:17], srow[0:1, 13:14])
                S.op("vector", em, incs=[("ve", 1)], waits=[("ve", rm)])
                rm = S.mark("ve")
                def em(nc_, eng):
                    eng.tensor_scalar(srow[0:1, 17:20], srow[0:1, 14:17], 127.0,
                                      None, ALU.mult)
                    return eng.tensor_scalar(srow[0:1, 20:23], srow[0:1, 11:14],
                                             INV127, None, ALU.mult)
                S.op("vector", em, incs=[("ve", 1)], waits=[("ve", rm)])
                rm = S.mark("ve")
                def em(nc_, eng):
                    return eng.tensor_mul(srow[0:1, 24:25], srow[0:1, 20:21],
                                          srow[0:1, 21:22])
                S.op("vector", em, incs=[("ve", 1)], waits=[("ve", rm)])
                vm = S.mark("ve")
                ma = S.dma("gpsimd", sc_dram[CRX:CRX + 1], srow[0:1, 17:18], "gs",
                           [("ve", vm)])
                ma = S.dma("gpsimd", sc_dram[CS1:CS1 + 1], srow[0:1, 24:25], "gs",
                           [("gs", ma)])
                ma = S.dma("gpsimd", sc_dram[CRW1:CRW1 + 1], srow[0:1, 18:19], "gs",
                           [("gs", ma)])
                ma = S.dma("gpsimd", sc_dram[CRW2:CRW2 + 1], srow[0:1, 19:20], "gs",
                           [("gs", ma)])
                me = S.dma("gpsimd", scal[:, 0:8],
                           sc_dram[0:8].unsqueeze(0).partition_broadcast(128),
                           "gs", [("gs", ma)])
                if debug:
                    S.dma("sync", dbgmap["sc16"][0:1, 0:16], scal[0:1, 0:16], "sdbg",
                          [("gs", me)])
                    S.dma("sync", dbgmap["xmaxd"][:, :], xmax[:, :], "sdbg",
                          [("gs", me)])
                    S.dma("sync", dbgmap["relayd"][:, :], relay[:, :], "sdbg",
                          [("gs", me)])
                    S.dma("sync", dbgmap["srowd"][0:1, :], srow[0:1, 0:32], "sdbg",
                          [("gs", me)])
                bc1m = S.mark("gs")
                def em(nc_, eng):
                    return eng.tensor_scalar(b1s[:, :], b1c[:, :],
                                             col(CS1), None, ALU.mult)
                S.op("vector", em, incs=[("ve", 1)], waits=[("gs", bc1m)])
                B1S_ve = S.mark("ve")

                wq_free = [None, None]
                for dc in range(8):
                    wp = dc % 2
                    sw = "sw%d" % wp
                    wt = [] if wq_free[wp] is None else [("ve", wq_free[wp])]
                    lm = S.dma("sync", wqsv[:, wp, :],
                               w1_d[128 * dc:128 * (dc + 1), :], sw, wt)
                    def em(nc_, eng, wp=wp):
                        return eng.tensor_scalar(wqsv[:, wp, :],
                                                 wqsv[:, wp, :],
                                                 col(CRW1), C_RND, ALU.mult,
                                                 ALU.add)
                    S.op("vector", em, incs=[("ve", 1)],
                         waits=[(sw, lm), ("gs", bc1m)])
                    qsp = S.mark("ve")
                    def em(nc_, eng, wp=wp, dc=dc):
                        return eng.tensor_scalar(w1qv[:, dc, :],
                                                 wqsv[:, wp, :], -C_RND,
                                                 None, ALU.add)
                    S.op("vector", em, incs=[("ve", 1)], waits=[("ve", qsp)])
                    wq_free[wp] = S.mark("ve")
                W1Q_ve = S.mark("ve")

                pt_free = [None, None]
                x1q_free = [None, None]
                xld_free = [None, None]
                xqb_free = [None, None]
                ph_free = [None] * 4
                hs_free = [None] * 4
                phn = [0]
                hsn = [0]
                for tg in range(8):
                    pg = tg % 2
                    wt_stage = [] if x1q_free[pg] is None else \
                        [("te", x1q_free[pg])]
                    for t4 in range(4):
                        tt = 4 * tg + t4
                        xp = tt % 2
                        sxr = "sxr%d" % xp
                        wtl = [] if xld_free[xp] is None else \
                            [("ve", xld_free[xp])]
                        lm = S.dma("sync", xldv[:, xp, :],
                                   x1_dram[128 * tt:128 * (tt + 1), :],
                                   sxr, wtl)
                        wq_ = [(sxr, lm), ("gs", bc1m)]
                        if xqb_free[xp] is not None:
                            wq_.append(("te", xqb_free[xp]))
                        def em(nc_, eng, xp=xp):
                            return eng.tensor_scalar(
                                xldv[:, xp, :], xldv[:, xp, :],
                                col(CRX), C_RND, ALU.mult, ALU.add)
                        S.op("vector", em, incs=[("ve", 1)], waits=wq_)
                        qsp = S.mark("ve")
                        def em(nc_, eng, xp=xp):
                            return eng.tensor_scalar(xqbv[:, xp, :],
                                                     xldv[:, xp, :], -C_RND,
                                                     None, ALU.add)
                        S.op("vector", em, incs=[("ve", 1)],
                             waits=[("ve", qsp)])
                        xld_free[xp] = S.mark("ve")
                        qm = S.mark("ve")
                        pp = tt % 2
                        wt2 = [("ve", qm)] + wt_stage
                        if pt_free[pp] is not None:
                            wt2.append(("sc", pt_free[pp]))
                        def em(nc_, eng, xp=xp, pp=pp):
                            r = None
                            for j in range(8):
                                r = eng.transpose(
                                    PT[pp][:, 128 * j:128 * (j + 1)],
                                    xqbv[:, xp, 128 * j:128 * (j + 1)],
                                    ident[:, :])
                            return r
                        S.op("tensor", em, incs=[("te", 1)], waits=wt2)
                        xqb_free[xp] = S.mark("te")
                        def em(nc_, eng, pg=pg, t4=t4, pp=pp):
                            return eng.activation(
                                x1qv[:, pg, 0:8, 128 * t4:128 * (t4 + 1)],
                                PTv[pp], AF.Copy)
                        S.op("scalar", em, incs=[("sc", 1)],
                             waits=[("te", S.mark("te"))])
                        pt_free[pp] = S.mark("sc")
                    stage_sc = S.mark("sc")

                    for hc in range(32):
                        sl = phn[0] % 4
                        phn[0] += 1
                        wt = [("sc", stage_sc), ("ve", W1Q_ve)]
                        if ph_free[sl] is not None:
                            wt.append(("sc", ph_free[sl]))
                        def em(nc_, eng, hc=hc, pg=pg, sl=sl):
                            r = None
                            for dc in range(8):
                                r = eng.matmul(
                                    PH[sl][:, :],
                                    w1qv[:, dc, 128 * hc:128 * (hc + 1)],
                                    x1qv[:, pg, dc, :],
                                    start=(dc == 0), stop=(dc == 7))
                            return r
                        S.op("tensor", em, incs=[("te", 1)], waits=wt)
                        tm = S.mark("te")
                        hsl = hsn[0] % 4
                        hsn[0] += 1
                        sh = "sh%d" % hsl
                        wg = [("te", tm), ("ve", B1S_ve)]
                        if hs_free[hsl] is not None:
                            wg.append((sh, hs_free[hsl][0]))
                            wg.append(("ve", hs_free[hsl][1]))
                        def em(nc_, eng, hc=hc, sl=sl, hsl=hsl):
                            return eng.activation(
                                hstv[:, hsl, :], PH[sl][:, :], AF.Gelu,
                                bias=b1s[:, hc:hc + 1], scale=col(CS1))
                        S.op("scalar", em, incs=[("sc", 1)], waits=wg)
                        ph_free[sl] = S.mark("sc")
                        gm = S.mark("sc")
                        def em(nc_, eng, hsl=hsl):
                            return eng.reduce_max(rmx[:, :], hstv[:, hsl, :],
                                                  axis=AX.X,
                                                  apply_absolute_value=True)
                        S.op("vector", em, incs=[("ve", 1)],
                             waits=[("sc", gm)])
                        rm = S.mark("ve")
                        def em(nc_, eng):
                            return eng.tensor_max(hmax[:, :], hmax[:, :],
                                                  rmx[:, :])
                        S.op("vector", em, incs=[("ve", 1)],
                             waits=[("ve", rm)])
                        hm = S.mark("ve")
                        sp = S.dma("sync", h_dram[hc, :, 512 * tg:512 * (tg + 1)],
                                   hstv[:, hsl, :], sh, [("sc", gm)])
                        hs_free[hsl] = (sp, hm)
                    x1q_free[pg] = S.mark("te")
                S.play(blk, entry=bar)

    bar = S.marks()
    with ExitStack() as fes:
        fsb = lambda nm, shp, dt: fes.enter_context(
            nc.sbuf_tensor(nm, shp, dt))
        w2qt = fsb("w2qt", [128, 32 * 1024], BF16)
        w2st = fsb("w2st", [128, 2 * 1024], F32)
        hrl = fsb("hrl", [128, 3 * 32 * 512], BF16)
        hsc = fsb("hscr", [128, 4 * 512], F32)
        x1ld = fsb("x1ldD", [128, 2 * 1024], F32)
        ot = fsb("otD", [128, 2 * 1024], F32)
        B2S = fsb("B2S", [128, 1024], F32)
        w2qv = w2qt.ap().rearrange("p (c n) -> p c n", c=32)
        w2sv = w2st.ap().rearrange("p (r n) -> p r n", r=2)
        hrlv = hrl.ap().rearrange("p (r c n) -> p r c n", r=3, c=32)
        hscv = hsc.ap().rearrange("p (c n) -> p c n", c=4)
        x1lv = x1ld.ap().rearrange("p (r n) -> p r n", r=2)
        otv = ot.ap().rearrange("p (r n) -> p r n", r=2)

        with ExitStack() as pes:
            PO2 = [pes.enter_context(
                nc.psum_tensor(f"PO2{i}", [128, 1024], F32))
                for i in range(4)]

            with nc.Block() as blk:
                m1 = S.dma("gpsimd", cc2i[0:1, :], hmax[:, 0], "gs")
                def em(nc_, eng):
                    return eng.collective_compute(
                        "AllReduce", ALU.max,
                        replica_groups=[list(range(NCORE))],
                        ins=[cc2i], outs=[cc2o])
                S.op("gpsimd", em, incs=[("cc", 1)], waits=[("gs", m1)])
                m2 = S.dma("gpsimd", relay[0:1, :], cc2o[0:1, :], "gs",
                           [("cc", S.mark("cc"))])
                def em(nc_, eng):
                    return eng.reduce_max(srow[0:1, 8:9], relay[0:1, :],
                                          axis=AX.X)
                S.op("vector", em, incs=[("ve", 1)], waits=[("gs", m2)])
                rm = S.mark("ve")
                def em(nc_, eng):
                    return eng.tensor_scalar(srow[0:1, 9:10], srow[0:1, 8:9],
                                             1e-6, None, ALU.max)
                S.op("vector", em, incs=[("ve", 1)], waits=[("ve", rm)])
                rm = S.mark("ve")
                def em(nc_, eng):
                    return eng.reciprocal(srow[0:1, 10:11], srow[0:1, 9:10])
                S.op("vector", em, incs=[("ve", 1)], waits=[("ve", rm)])
                rm = S.mark("ve")
                def em(nc_, eng):
                    eng.tensor_scalar(srow[0:1, 11:12], srow[0:1, 10:11],
                                      127.0, None, ALU.mult)
                    return eng.tensor_scalar(srow[0:1, 12:13], srow[0:1, 9:10],
                                             INV127, None, ALU.mult)
                S.op("vector", em, incs=[("ve", 1)], waits=[("ve", rm)])
                rm = S.mark("ve")
                def em(nc_, eng):
                    return eng.tensor_mul(srow[0:1, 13:14], srow[0:1, 12:13],
                                          srow[0:1, 22:23])
                S.op("vector", em, incs=[("ve", 1)], waits=[("ve", rm)])
                vm = S.mark("ve")
                ma = S.dma("gpsimd", sc_dram[CRH:CRH + 1], srow[0:1, 11:12],
                           "gs", [("ve", vm)])
                ma = S.dma("gpsimd", sc_dram[CS2:CS2 + 1], srow[0:1, 13:14],
                           "gs", [("gs", ma)])
                bc = S.dma("gpsimd", scal[:, 0:8],
                           sc_dram[0:8].unsqueeze(0).partition_broadcast(128),
                           "gs", [("gs", ma)])

                bm = S.dma("sync", B2S[:, :],
                           b2_d[0:1, :].partition_broadcast(128), "sc0")
                def em(nc_, eng):
                    return eng.tensor_scalar(B2S[:, :], B2S[:, :], col(CS2),
                                             None, ALU.mult)
                S.op("vector", em, incs=[("ve", 1)],
                     waits=[("sc0", bm), ("gs", bc)])
                B2S_ve = S.mark("ve")

                W2STREAMS = ["sw0", "sw1", "ssw0", "ssw1"]
                w2q_mark = [None] * 32
                w2_free = [None] * 2
                def emit_w2(lo, hi):
                    for hc in range(lo, hi):
                        wp = hc % 2
                        sw = W2STREAMS[wp]
                        wt = [] if w2_free[wp] is None else \
                            [("ve", w2_free[wp])]
                        lm = S.dma("sync", w2sv[:, wp, :],
                                   w2_d[128 * hc:128 * (hc + 1), :], sw, wt)
                        def em(nc_, eng, wp=wp):
                            return eng.tensor_scalar(
                                w2sv[:, wp, :], w2sv[:, wp, :],
                                col(CRW2), C_RND, ALU.mult, ALU.add)
                        S.op("vector", em, incs=[("ve", 1)],
                             waits=[(sw, lm)])
                        qsp = S.mark("ve")
                        def em(nc_, eng, wp=wp, hc=hc):
                            return eng.tensor_scalar(
                                w2qv[:, hc, :], w2sv[:, wp, :], -C_RND,
                                None, ALU.add)
                        S.op("vector", em, incs=[("ve", 1)],
                             waits=[("ve", qsp)])
                        w2_free[wp] = S.mark("ve")
                        w2q_mark[hc] = S.mark("ve")

                mm_done = [None] * 8
                qd_done = [None] * 8
                po_free = [None] * 4
                xl_free = [None, None]

                def emit_reload_quant(bb):
                    hp = bb % 3
                    shq = "shq%d" % (bb % 2)
                    wl = [("te", mm_done[bb - 3])] if bb >= 3 else []
                    lmg = []
                    for g in range(4):
                        lm = S.dma("sync", hrlv[:, hp, 8 * g:8 * (g + 1), :],
                                   h_dram[8 * g:8 * (g + 1), :,
                                          512 * bb:512 * (bb + 1)].rearrange(
                                       "h p n -> p h n"),
                                   shq, wl)
                        lmg.append(lm)
                    for g in range(8):
                        wq_ = [(shq, lmg[3]), ("gs", bc)]
                        def em(nc_, eng, hp=hp, g=g):
                            return eng.tensor_scalar(
                                hscv[:, :, :],
                                hrlv[:, hp, 4 * g:4 * (g + 1), :],
                                col(CRH), C_RND, ALU.mult, ALU.add)
                        S.op("vector", em, incs=[("ve", 1)], waits=wq_)
                        v1 = S.mark("ve")
                        def em(nc_, eng, hp=hp, g=g):
                            return eng.tensor_scalar(
                                hrlv[:, hp, 4 * g:4 * (g + 1), :],
                                hscv[:, :, :], -C_RND, None, ALU.add)
                        S.op("vector", em, incs=[("ve", 1)],
                             waits=[("ve", v1)])
                    qd_done[bb] = S.mark("ve")

                def emit_tiles(bb):
                    hp = bb % 3
                    for j in range(4):
                        tt = 4 * bb + j
                        pop = tt % 4
                        for gs_ in range(4):
                            wt = [("ve", w2q_mark[8 * gs_ + 7])]
                            if gs_ == 0:
                                wt.append(("ve", qd_done[bb]))
                                if po_free[pop] is not None:
                                    wt.append(("ve", po_free[pop]))
                            def em(nc_, eng, hp=hp, j=j, pop=pop, gs_=gs_):
                                r = None
                                for hc in range(8 * gs_, 8 * gs_ + 8):
                                    for hh in range(2):
                                        r = eng.matmul(
                                            PO2[pop][:,
                                                     512 * hh:512 * (hh + 1)],
                                            hrlv[:, hp, hc,
                                                 128 * j:128 * (j + 1)],
                                            w2qv[:, hc,
                                                 512 * hh:512 * (hh + 1)],
                                            start=(hc == 0),
                                            stop=(hc == 31))
                                return r
                            S.op("tensor", em, incs=[("te", 1)], waits=wt)
                        mm = S.mark("te")
                        if j == 3:
                            mm_done[bb] = mm

                        ep = tt % 2
                        sxr = "sxr%d" % ep
                        sy = "sy%d" % ep
                        wx = [] if xl_free[ep] is None else xl_free[ep]
                        lx = S.dma("sync", x1lv[:, ep, :],
                                   x1_dram[128 * tt:128 * (tt + 1), :],
                                   sxr, wx)
                        wstt = [("te", mm), ("ve", B2S_ve)]
                        if xl_free[ep]:
                            wstt += [w for w in xl_free[ep] if w[0] == sy]
                        def em(nc_, eng, ep=ep, pop=pop):
                            return eng.scalar_tensor_tensor(
                                otv[:, ep, :], PO2[pop][:, :], col(CS2),
                                B2S[:, :], ALU.mult, ALU.add)
                        S.op("vector", em, incs=[("ve", 1)], waits=wstt)
                        sttm = S.mark("ve")
                        def em(nc_, eng, ep=ep):
                            return eng.tensor_add(otv[:, ep, :],
                                                  otv[:, ep, :],
                                                  x1lv[:, ep, :])
                        S.op("vector", em, incs=[("ve", 1)],
                             waits=[(sxr, lx), ("ve", sttm)])
                        po_free[pop] = S.mark("ve")
                        def em(nc_, eng, ep=ep):
                            eng.tensor_mul(x1lv[:, ep, :], otv[:, ep, :],
                                           otv[:, ep, :])
                            return eng.reduce_sum(lns[:, :], otv[:, ep, :],
                                                  axis=AX.X)
                        S.op("vector", em, incs=[("ve", 1)],
                             waits=[("ve", po_free[pop])])
                        sqv = S.mark("ve")
                        def em(nc_, eng, ep=ep):
                            return eng.reduce_sum(lnv[:, :], x1lv[:, ep, :],
                                                  axis=AX.X)
                        S.op("vector", em, incs=[("ve", 1)],
                             waits=[("ve", sqv)])
                        summ = S.mark("ve")
                        def em(nc_, eng):
                            eng.tensor_mul(lnq[:, :], lns[:, :], lns[:, :])
                            eng.tensor_scalar(lnm[:, :], lns[:, :], 1.0 / D,
                                              None, ALU.mult)
                            return eng.tensor_scalar(
                                lnq[:, :], lnq[:, :], 1.0 / (D * D), EPS,
                                ALU.mult, ALU.subtract)
                        S.op("vector", em, incs=[("ve", 1)],
                             waits=[("ve", summ)])
                        st1a = S.mark("ve")
                        def em(nc_, eng):
                            return eng.scalar_tensor_tensor(
                                lnv[:, :], lnv[:, :], 1.0 / D, lnq[:, :],
                                ALU.mult, ALU.subtract)
                        S.op("vector", em, incs=[("ve", 1)],
                             waits=[("ve", st1a)])
                        def em(nc_, eng):
                            return eng.activation(lnr[:, :], lnv[:, :],
                                                  AF.Sqrt)
                        S.op("scalar", em, incs=[("sc", 1)],
                             waits=[("ve", S.mark("ve"))])
                        def em(nc_, eng):
                            return eng.reciprocal(lnr[:, :], lnr[:, :])
                        S.op("vector", em, incs=[("ve", 1)],
                             waits=[("sc", S.mark("sc"))])
                        vrc = S.mark("ve")
                        def em(nc_, eng, ep=ep):
                            return eng.tensor_scalar(
                                otv[:, ep, :], otv[:, ep, :], lnm[:, :],
                                lnr[:, :], ALU.subtract, ALU.mult)
                        S.op("vector", em, incs=[("ve", 1)],
                             waits=[("ve", vrc)])
                        vem = S.mark("ve")
                        sym = S.dma("sync", y_d[128 * tt:128 * (tt + 1), :],
                                    otv[:, ep, :], sy, [("ve", vem)])
                        xl_free[ep] = [("ve", vem), (sy, sym)]

                emit_w2(0, 8)
                emit_reload_quant(0)
                emit_w2(8, 16)
                emit_reload_quant(1)
                emit_w2(16, 32)
                emit_reload_quant(2)
                for bb in range(8):
                    emit_tiles(bb)
                    if bb + 3 < 8:
                        emit_reload_quant(bb + 3)
                S.play(blk, entry=bar)

    fin = S.marks()
    with nc.Block() as blk:
        def em(nc_, eng):
            return eng.engine_nop()
        S.op("vector", em, incs=[("ve", 1)])
        S.play(blk, entry=fin)

    es.close()
    return nc, dbgmap


def kernel(**inputs):
    per_core = _host_prep(inputs)
    nc, _ = build_program()
    res = run_bass_kernel_spmd(nc, per_core, core_ids=list(range(NCORE)))
    return np.concatenate(
        [np.asarray(r["y"]).reshape(2, N, D) for r in res.results], axis=0)
